# version 03
# baseline (speedup 1.0000x reference)
import numpy as np
import ml_dtypes
import concourse.bass as bass
import concourse.mybir as mybir

F32 = mybir.dt.float32
BF16 = mybir.dt.bfloat16
AF = mybir.ActivationFunctionType
ALU = mybir.AluOpType
AX = mybir.AxisListType

P = 128
TOK = 4096
D = 1024
NT = TOK // P
NMC = D // P
CAP = 5120.0
INV_D = 1.0 / D
EPS = 1e-5


def prep_host(inputs):
    g = {}
    W1 = np.asarray(inputs["L0_ffn_W1"])
    W2 = np.asarray(inputs["L0_ffn_W2"])
    g["W1R"] = np.ascontiguousarray(W1.reshape(8, 8, 128, 8, 512).transpose(0, 3, 2, 1, 4))
    g["W2R"] = np.ascontiguousarray(W2.reshape(8, 32, 128, 4, 256).transpose(0, 3, 2, 1, 4))
    g["WrR"] = np.ascontiguousarray(
        np.asarray(inputs["L0_ffn_Wr"]).reshape(8, 128, 8).transpose(1, 0, 2))
    g["b1R"] = np.ascontiguousarray(
        np.asarray(inputs["L0_ffn_b1"]).reshape(8, 32, 128).transpose(2, 0, 1))
    g["b2"] = np.asarray(inputs["L0_ffn_b2"])
    g["b2R"] = np.ascontiguousarray(
        np.asarray(inputs["L0_ffn_b2"]).reshape(8, 8, 128).transpose(2, 0, 1))
    g["ident_bf"] = np.eye(P, dtype=ml_dtypes.bfloat16)
    g["u128"] = np.triu(np.ones((P, P), dtype=np.float32))
    g["u32x"] = np.triu(np.ones((32, 32), dtype=np.float32), 1)
    g["iota8"] = np.broadcast_to(np.arange(8, dtype=np.float32), (P, 8)).copy()
    g["onesc"] = np.ones((P, 1), dtype=np.float32)
    g["onesr_bf"] = np.ones((1, P), dtype=ml_dtypes.bfloat16)
    g["onesr_f"] = np.ones((1, 128), dtype=np.float32)
    g["ones32c"] = np.ones((32, 1), dtype=np.float32)
    g["epsc"] = np.full((P, 1), EPS, dtype=np.float32)
    W_h = np.asarray(inputs["L0_attn_W_h"])
    g["WhR"] = np.ascontiguousarray(W_h.reshape(8, 128, 4096).transpose(1, 0, 2))
    W_o = np.asarray(inputs["L0_attn_W_o"])
    g["WoR"] = np.ascontiguousarray(W_o.reshape(16, 128, 1024).transpose(1, 0, 2))
    W_qk = np.asarray(inputs["L0_attn_W_qk"])
    g["WqkR"] = np.ascontiguousarray(W_qk.reshape(8, 128, 128).transpose(1, 0, 2))
    rows = np.zeros((1, 8192), dtype=np.float32)
    rows[0, 0:4096] = np.asarray(inputs["L0_attn_b_h"])
    rows[0, 4096:5120] = np.asarray(inputs["L0_attn_b_o"])
    g["rows_s2"] = rows
    g["osgR"] = np.ascontiguousarray(np.asarray(inputs["L0_attn_os_g"]).T)
    g["osbR"] = np.ascontiguousarray(np.asarray(inputs["L0_attn_os_b"]).T)
    g["bqkc"] = np.asarray(inputs["L0_attn_b_qk"]).reshape(128, 1).copy()
    g["lngR"] = np.ascontiguousarray(np.asarray(inputs["L0_attn_ln_g"]).reshape(8, 128).T)
    g["lnbR"] = np.ascontiguousarray(np.asarray(inputs["L0_attn_ln_b"]).reshape(8, 128).T)
    import math
    G = 256
    pos = np.arange(G)
    nn = pos[:, None] - pos[None, :]
    nb = 16
    ret = (nn < 0).astype(np.int32) * nb
    na = np.abs(nn)
    mx = 8
    is_small = na < mx
    val_large = mx + (np.log(np.maximum(na, 1).astype(np.float32) / mx)
                      / math.log(128 / mx) * (nb - mx)).astype(np.int32)
    val_large = np.minimum(val_large, nb - 1)
    bucket = ret + np.where(is_small, na, val_large)
    masks = np.zeros((32, 2, 128, 256), dtype=ml_dtypes.bfloat16)
    for u in range(32):
        mu = (bucket == u)
        mT = mu.T
        masks[u, 0] = mT[0:128, :]
        masks[u, 1] = mT[128:256, :]
    g["masksT"] = masks
    q = lambda n: np.asarray(inputs["L1_ffn_" + n])
    g["encR"] = np.ascontiguousarray(q("enc_W").reshape(8, 128, 2048).transpose(1, 0, 2))
    g["decR"] = np.ascontiguousarray(q("dec_W").reshape(16, 128, 1024).transpose(1, 0, 2))
    g["s2cR"] = np.ascontiguousarray(q("s2c_W").reshape(16, 128, 64).transpose(1, 0, 2))
    g["c2sR"] = q("c2s_W").reshape(64, 16, 128)
    g["rW1R"] = q("rW1")
    g["rW2R"] = q("rW2")
    g["We1R"] = np.ascontiguousarray(q("We1").reshape(8, 64, 8, 128).transpose(1, 0, 2, 3))
    g["We2R"] = np.ascontiguousarray(q("We2").reshape(8, 8, 128, 64).transpose(2, 0, 1, 3))
    g["encbR"] = np.ascontiguousarray(q("enc_b").reshape(16, 128).T)
    g["c2sbR"] = np.ascontiguousarray(q("c2s_b").reshape(16, 128).T)
    g["s2cbc"] = q("s2c_b").reshape(64, 1).copy()
    g["rb1c"] = q("rb1").reshape(64, 1).copy()
    g["rb2c"] = q("rb2").reshape(8, 1).copy()
    g["be1R"] = np.ascontiguousarray(q("be1").reshape(8, 8, 128).transpose(2, 0, 1))
    g["be2R"] = np.ascontiguousarray(q("be2").T)
    rows4 = np.zeros((1, 8192), dtype=np.float32)
    rows4[0, 0:1024] = q("dec_b")
    g["rows_s4"] = rows4
    rowf4 = np.zeros((1, 2048), dtype=np.float32)
    rowf4[0, 0:1024] = q("ln_g")
    rowf4[0, 1024:2048] = q("ln_b")
    g["rowf_s4"] = rowf4
    Wq = np.asarray(inputs["L1_attn_qkv_kernel"])
    g["QR"] = np.ascontiguousarray(
        Wq.reshape(8, 128, 24, 128).transpose(1, 2, 0, 3).reshape(128, 192, 128))
    g["qkvbR"] = np.ascontiguousarray(
        np.asarray(inputs["L1_attn_qkv_bias"]).reshape(24, 128).T)
    tri = np.zeros((128, 128), dtype=ml_dtypes.bfloat16)
    p2 = np.arange(128)[:, None]
    p1 = np.arange(128)[None, :]
    g["tincl"] = (p1 >= p2).astype(ml_dtypes.bfloat16)
    g["texcl"] = (p1 > p2).astype(ml_dtypes.bfloat16)
    return g


class BK:
    def __init__(self, nc, es):
        self.nc = nc
        self.es = es
        self.sem = {}
        self.cnt = {}
        for name in ["LD", "WD", "ST", "TV", "VV", "SV", "CC1", "CC2"]:
            self.sem[name] = es.enter_context(nc.semaphore(name))
            self.cnt[name] = 0
        self.guard = {}

    def inc(self, instr, name, n=1):
        instr.then_inc(self.sem[name], n)
        self.cnt[name] += n
        return self.cnt[name]

    def w(self, eng, name, val=None):
        if val is None:
            val = self.cnt[name]
        if val > 0:
            eng.wait_ge(self.sem[name], val)

    def fence(self):
        return dict(self.cnt)

    def wait_fence(self, eng, f, names):
        for n in names:
            if f[n] > 0:
                eng.wait_ge(self.sem[n], f[n])

    def gwait(self, eng, key):
        if key in self.guard:
            nm, v = self.guard[key]
            if v > 0:
                eng.wait_ge(self.sem[nm], v)

    def gset(self, key, name, val=None):
        self.guard[key] = (name, self.cnt[name] if val is None else val)



def alloc_ffn0_sbuf(nc, es, sb):
    def S(name, shape, dt):
        sb[name] = es.enter_context(nc.sbuf_tensor("sb_" + name, shape, dt))
    S("b1", [P, 8 * 32], F32)
    S("b2", [1, 8 * 1024], BF16)
    S("xt0", [P, D], F32)
    S("xt1", [P, D], F32)
    S("zbf0", [P, D], BF16)
    S("zbf1", [P, D], BF16)
    S("stats", [P, 2 * NT], F32)
    S("rsm", [P, 16], F32)
    S("probs", [P, 8 * NT], F32)
    S("tkt", [P, 8], F32)
    S("tkb", [P, 8 * NT], F32)
    S("tkc", [P, 8 * NT], F32)
    S("tks", [P, 8 * NT], F32)
    S("tkm", [P, 256], F32)
    S("oh1", [P, 8 * NT], F32)
    S("oh2", [P, 8 * NT], F32)
    S("wh", [P, 2 * NT], F32)
    S("wcoef", [P, 8 * NT], F32)
    S("cntrow", [1, 512], F32)
    S("cntmat", [32, 16], F32)
    S("ccrow", [1, 16], F32)
    S("bgat", [8, 16], F32)
    S("base", [1, 16], F32)
    S("offb", [32, 16], F32)
    S("offrow", [1, 512], F32)
    S("mask8", [8, 1], F32)
    S("xts", [P, 8 * D], F32)
    S("lnT", [P, NMC * 1024], BF16)
    S("hsb", [P, 32 * 1024], BF16)
    S("w1b0", [P, 8 * 512], BF16)
    S("w1b1", [P, 8 * 512], BF16)
    S("w2b0", [P, 32 * 256], BF16)
    S("w2b1", [P, 32 * 256], BF16)


def emit_ffn0(bk, block, io):
    c = io["c"]
    sb = io["sb"]
    cached = io.get("cached", False)
    W1C, W2C = io["W1C"], io["W2C"]
    wr_sb = sb["tks"][:, 128:160].bitcast(BF16)
    pbf = io["pbf"]
    pf = io["pf"]
    cc_name = io["cc_name"]
    xin, xout = io["xin"], io["xout"]

    f0 = bk.fence()

    @block.gpsimd
    def _(g):
        bk.wait_fence(g, f0, ["TV", "VV", "SV", "ST"])
        bk.inc(g.dma_start(out=wr_sb.rearrange("p (c e) -> p c e", c=NMC),
                           in_=io["WrR"]), "WD", 16)
        bk.inc(g.dma_start(out=sb["b1"][:].rearrange("p (e g) -> p e g", e=8),
                           in_=io["b1R"]), "WD", 16)
        bk.inc(g.dma_start(out=sb["tkm"][:, 192:256].rearrange("p (e h) -> p e h", e=8),
                           in_=io["b2R"]), "WD", 16)
        bk.inc(g.dma_start(out=sb["mask8"][:], in_=io["mask8_in"]), "WD", 16)
    wd_pre = bk.cnt["WD"]

    vv_done = [0] * NT
    for i in range(NT):
        par = i % 2
        xt = sb["xt0"] if par == 0 else sb["xt1"]
        zbf = sb["zbf0"] if par == 0 else sb["zbf1"]
        zT = sb["lnT"][:, 0:D] if par == 0 else sb["lnT"][:, D:2 * D]

        @block.sync
        def _(sync):
            if i == 0:
                bk.wait_fence(sync, f0, ["TV", "VV", "SV", "ST"])
            bk.gwait(sync, ("xt", par))
            bk.inc(sync.dma_start(out=xt[:], in_=xin[i * P:(i + 1) * P, :]), "LD", 16)
        ld_x = bk.cnt["LD"]

        @block.scalar
        def _(s):
            bk.w(s, "LD", ld_x)
            bk.inc(s.activation(sb["xts"][:, 0:D], xt[:], AF.Square,
                                accum_out=sb["rsm"][:, par:par + 1]), "SV")
        sv_sq = bk.cnt["SV"]

        @block.vector
        def _(v):
            bk.w(v, "LD", ld_x)
            nc = bk.nc
            v.reduce_sum(sb["rsm"][:, 2 + par:3 + par], xt[:], axis=AX.X)
            bk.w(v, "SV", sv_sq)
            v.tensor_scalar(sb["rsm"][:, 4 + par:5 + par], sb["rsm"][:, par:par + 1],
                            INV_D, None, ALU.mult)
            v.tensor_scalar(sb["rsm"][:, 6 + par:7 + par],
                            sb["rsm"][:, 2 + par:3 + par],
                            sb["rsm"][:, 2 + par:3 + par], INV_D * INV_D,
                            ALU.mult, ALU.mult)
            v.tensor_scalar(sb["stats"][:, 2 * i:2 * i + 1],
                            sb["rsm"][:, 2 + par:3 + par], INV_D, None, ALU.mult)
            bk.inc(v.tensor_tensor(sb["rsm"][:, 4 + par:5 + par],
                                   sb["rsm"][:, 4 + par:5 + par],
                                   sb["rsm"][:, 6 + par:7 + par], ALU.subtract), "VV")
        vv_var = bk.cnt["VV"]

        @block.scalar
        def _(s):
            bk.w(s, "VV", vv_var)
            s.activation(sb["rsm"][:, 8 + par:9 + par], sb["rsm"][:, 4 + par:5 + par],
                         AF.Ln, bias=float(EPS))
            bk.nc.scalar.nop()
            bk.inc(s.activation(sb["stats"][:, 2 * i + 1:2 * i + 2],
                                sb["rsm"][:, 8 + par:9 + par], AF.Exp, scale=-0.5), "SV")
        sv_rstd = bk.cnt["SV"]

        @block.vector
        def _(v):
            bk.w(v, "SV", sv_rstd)
            bk.gwait(v, ("zbf", par))
            bk.inc(v.tensor_scalar(zbf[:], xt[:], sb["stats"][:, 2 * i:2 * i + 1],
                                   sb["stats"][:, 2 * i + 1:2 * i + 2],
                                   ALU.subtract, ALU.mult), "VV")
        vv_z = bk.cnt["VV"]
        bk.gset(("xt", par), "VV")

        @block.tensor
        def _(t):
            bk.w(t, "VV", vv_z)
            bk.gwait(t, ("pbf", par))
            for mc in range(NMC):
                ins = t.transpose(pbf[par][:, mc * P:(mc + 1) * P],
                                  zbf[:, mc * P:(mc + 1) * P], c["ident"])
            bk.inc(ins, "TV")
        tv_tr = bk.cnt["TV"]

        @block.vector
        def _(v):
            bk.w(v, "TV", tv_tr)
            bk.gwait(v, ("zT", par))
            bk.inc(v.tensor_copy(zT, pbf[par][:]), "VV")
        vv_zT = bk.cnt["VV"]
        bk.gset(("pbf", par), "VV")
        bk.gset(("zbf", par), "VV")

        @block.tensor
        def _(t):
            bk.w(t, "VV", vv_zT)
            if i == 0:
                bk.w(t, "WD", wd_pre)
            bk.gwait(t, ("pfl", par))
            for mc in range(NMC):
                ins = t.matmul(pf[par][:, 0:8], zT[:, mc * P:(mc + 1) * P],
                               wr_sb[:, mc * 8:(mc + 1) * 8],
                               start=(mc == 0), stop=(mc == NMC - 1))
            bk.inc(ins, "TV")
        tv_logit = bk.cnt["TV"]
        bk.gset(("zT", par), "TV")

        @block.vector
        def _(v):
            bk.w(v, "TV", tv_logit)
            bk.inc(v.tensor_reduce(sb["rsm"][:, 10 + par:11 + par], pf[par][:, 0:8],
                                   axis=AX.X, op=ALU.max, negate=True), "VV")
        vv_nm = bk.cnt["VV"]

        @block.scalar
        def _(s):
            bk.w(s, "VV", vv_nm)
            bk.inc(s.activation(sb["probs"][:, 8 * i:8 * i + 8], pf[par][:, 0:8], AF.Exp,
                                bias=sb["rsm"][:, 10 + par:11 + par]), "SV")
        sv_exp = bk.cnt["SV"]
        bk.gset(("pfl", par), "SV")

        @block.vector
        def _(v):
            bk.w(v, "SV", sv_exp)
            bk.inc(bk.nc.vector.nop(), "VV")
        vv_done[i] = bk.cnt["VV"]

    fsw0 = bk.fence()

    @block.vector
    def _(v):
        bk.wait_fence(v, fsw0, ["SV"])
        pr = sb["probs"]
        m1 = sb["tkm"][:, 0:32]
        i1 = sb["tkm"][:, 32:64]
        m2 = sb["tkm"][:, 64:96]
        i2 = sb["tkm"][:, 96:128]
        s12 = sb["tkm"][:, 128:160]
        rh = sb["tkm"][:, 160:192]
        eq = sb["tkb"]
        e9 = sb["tkc"]
        for i in range(NT):
            v.reduce_max(m1[:, i:i + 1], pr[:, 8 * i:8 * i + 8], axis=AX.X)
        for i in range(NT):
            v.tensor_scalar(eq[:, 8 * i:8 * i + 8], pr[:, 8 * i:8 * i + 8],
                            m1[:, i:i + 1], None, ALU.is_equal)
        for i in range(NT):
            v.scalar_tensor_tensor(e9[:, 8 * i:8 * i + 8], eq[:, 8 * i:8 * i + 8],
                                   999.0, c["iota8"], ALU.mult, ALU.subtract)
        for i in range(NT):
            v.reduce_max(i1[:, i:i + 1], e9[:, 8 * i:8 * i + 8], axis=AX.X)
        for i in range(NT):
            v.tensor_scalar(i1[:, i:i + 1], i1[:, i:i + 1], -1.0, 999.0, ALU.mult, ALU.add)
        for i in range(NT):
            v.tensor_scalar(sb["oh1"][:, 8 * i:8 * i + 8], c["iota8"], i1[:, i:i + 1],
                            None, ALU.is_equal)
        for i in range(NT):
            v.scalar_tensor_tensor(eq[:, 8 * i:8 * i + 8], sb["oh1"][:, 8 * i:8 * i + 8],
                                   -1e30, pr[:, 8 * i:8 * i + 8], ALU.mult, ALU.add)
        for i in range(NT):
            v.reduce_max(m2[:, i:i + 1], eq[:, 8 * i:8 * i + 8], axis=AX.X)
        for i in range(NT):
            v.tensor_scalar(e9[:, 8 * i:8 * i + 8], eq[:, 8 * i:8 * i + 8],
                            m2[:, i:i + 1], None, ALU.is_equal)
        for i in range(NT):
            v.scalar_tensor_tensor(e9[:, 8 * i:8 * i + 8], e9[:, 8 * i:8 * i + 8],
                                   999.0, c["iota8"], ALU.mult, ALU.subtract)
        for i in range(NT):
            v.reduce_max(i2[:, i:i + 1], e9[:, 8 * i:8 * i + 8], axis=AX.X)
        for i in range(NT):
            v.tensor_scalar(i2[:, i:i + 1], i2[:, i:i + 1], -1.0, 999.0, ALU.mult, ALU.add)
        for i in range(NT):
            v.tensor_scalar(sb["oh2"][:, 8 * i:8 * i + 8], c["iota8"], i2[:, i:i + 1],
                            None, ALU.is_equal)
        for i in range(NT):
            v.tensor_tensor(s12[:, i:i + 1], m1[:, i:i + 1], m2[:, i:i + 1], ALU.add)
        for i in range(NT):
            v.reciprocal(rh[:, i:i + 1], s12[:, i:i + 1])
        for i in range(NT):
            v.tensor_scalar(rh[:, i:i + 1], rh[:, i:i + 1], 0.5, None, ALU.mult)
        for i in range(NT):
            v.tensor_scalar(sb["wh"][:, 2 * i:2 * i + 1], m1[:, i:i + 1], rh[:, i:i + 1],
                            None, ALU.mult)
        ins = None
        for i in range(NT):
            ins = v.tensor_scalar(sb["wh"][:, 2 * i + 1:2 * i + 2], m2[:, i:i + 1],
                                  rh[:, i:i + 1], None, ALU.mult)
        bk.inc(ins, "VV")

    fsw = bk.fence()

    @block.tensor
    def _(t):
        bk.wait_fence(t, fsw, ["VV", "SV"])
        t.matmul(pf[2][0:1, 0:256], c["onesc"], sb["oh1"][:], start=True, stop=True)
        bk.inc(t.matmul(pf[3][0:1, 0:256], c["onesc"], sb["oh2"][:], start=True,
                        stop=True), "TV")
    tv_cnt = bk.cnt["TV"]

    @block.vector
    def _(v):
        bk.w(v, "TV", tv_cnt)
        v.tensor_copy(sb["cntrow"][0:1, 0:256], pf[2][0:1, 0:256])
        bk.inc(v.tensor_copy(sb["cntrow"][0:1, 256:512], pf[3][0:1, 0:256]), "VV")
    vv_cnt = bk.cnt["VV"]

    rt_flat = io["rt"].rearrange("a b -> (a b)")

    @block.sync
    def _(sync):
        bk.w(sync, "VV", vv_cnt)
        bk.inc(sync.dma_start(out=rt_flat[0:512][None], in_=sb["cntrow"][:]), "LD", 16)
        bk.w(sync, "LD")
        bk.inc(sync.dma_start(out=sb["cntmat"][:, 0:8],
                              in_=rt_flat[0:256].rearrange("(i e) -> i e", e=8)), "LD", 16)
        bk.inc(sync.dma_start(out=sb["cntmat"][:, 8:16],
                              in_=rt_flat[256:512].rearrange("(i e) -> i e", e=8)), "LD", 16)
    ld_cm = bk.cnt["LD"]

    @block.tensor
    def _(t):
        bk.w(t, "LD", ld_cm)
        bk.inc(t.matmul(pf[4][0:1, 0:16], c["ones32c"], sb["cntmat"][:], start=True,
                        stop=True), "TV")
    tv_tot = bk.cnt["TV"]

    @block.vector
    def _(v):
        bk.w(v, "TV", tv_tot)
        bk.inc(v.tensor_copy(sb["ccrow"][:], pf[4][0:1, 0:16]), "VV")
    vv_tot = bk.cnt["VV"]

    @block.gpsimd
    def _(g):
        bk.w(g, "VV", vv_tot)
        bk.inc(g.dma_start(out=io["cc_in"], in_=sb["ccrow"][:]), "WD", 16)
        bk.w(g, "WD")
        g.collective_compute("AllGather", ALU.bypass,
                             replica_groups=[list(range(8))],
                             ins=[io["cc_in"].opt()],
                             outs=[io["cc_out"].opt()]).then_inc(bk.sem[cc_name], 1)
        g.wait_ge(bk.sem[cc_name], 1)
        bk.inc(g.dma_start(out=sb["bgat"][:], in_=io["cc_out"]), "WD", 16)
    wd_cc = bk.cnt["WD"]

    @block.tensor
    def _(t):
        bk.w(t, "WD", wd_cc)
        bk.inc(t.matmul(pf[4][0:1, 16:32], sb["mask8"][:], sb["bgat"][:], start=True,
                        stop=True), "TV")
    tv_base = bk.cnt["TV"]

    @block.vector
    def _(v):
        bk.w(v, "TV", tv_base)
        bk.inc(v.tensor_copy(sb["base"][:], pf[4][0:1, 16:32]), "VV")
    vv_base = bk.cnt["VV"]

    @block.tensor
    def _(t):
        bk.w(t, "VV", vv_base)
        for k in range(2):
            t.matmul(pf[4][0:32, 32 + 8 * k:40 + 8 * k], c["u32x"],
                     sb["cntmat"][:, 8 * k:8 * k + 8], start=True, stop=False)
            ins = t.matmul(pf[4][0:32, 32 + 8 * k:40 + 8 * k], c["onesr_f"][0:1, 0:32],
                           sb["base"][0:1, 8 * k:8 * k + 8], start=False, stop=True)
        bk.inc(ins, "TV")
    tv_offb = bk.cnt["TV"]

    @block.vector
    def _(v):
        bk.w(v, "TV", tv_offb)
        bk.inc(v.tensor_copy(sb["offb"][:], pf[4][0:32, 32:48]), "VV")
    vv_offb = bk.cnt["VV"]

    @block.sync
    def _(sync):
        bk.w(sync, "VV", vv_offb)
        bk.inc(sync.dma_start(out=rt_flat[512:1024].rearrange("(i k e) -> i k e", i=32, k=2),
                              in_=sb["offb"][:].rearrange("i (k e) -> i k e", k=2)), "LD", 16)
        bk.w(sync, "LD")
        bk.inc(sync.dma_start(
            out=sb["offrow"][:].rearrange("q (k i e) -> q k i e", k=2, i=32),
            in_=rt_flat[512:1024].rearrange("(i k e) -> k i e", i=32, k=2)[None]),
            "LD", 16)
    ld_off = bk.cnt["LD"]

    for i in range(NT):
        par = i % 2

        @block.tensor
        def _(t):
            if i == 0:
                bk.w(t, "LD", ld_off)
            bk.gwait(t, ("pff", par))
            for k in range(2):
                oh = sb["oh1"] if k == 0 else sb["oh2"]
                t.matmul(pf[par][:, 16 + 8 * k:24 + 8 * k], c["u128"],
                         oh[:, 8 * i:8 * i + 8], start=True, stop=False)
                ins = t.matmul(pf[par][:, 16 + 8 * k:24 + 8 * k], c["onesr_f"][0:1, :],
                               sb["offrow"][0:1, 256 * k + 8 * i:256 * k + 8 * i + 8],
                               start=False, stop=True)
            bk.inc(ins, "TV")
        tv_rank = bk.cnt["TV"]

        @block.vector
        def _(v):
            bk.w(v, "TV", tv_rank)
            v.tensor_copy(sb["tkb"][:, 8 * i:8 * i + 8], pf[par][:, 16:24])
            bk.inc(v.tensor_copy(sb["tkc"][:, 8 * i:8 * i + 8], pf[par][:, 24:32]), "VV")
        bk.gset(("pff", par), "VV")

    @block.vector
    def _(v):
        eq = sb["tks"]
        ins = None
        for k in range(2):
            rk = sb["tkb"] if k == 0 else sb["tkc"]
            oh = sb["oh1"] if k == 0 else sb["oh2"]
            for i in range(NT):
                v.tensor_single_scalar(eq[:, 8 * i:8 * i + 8], rk[:, 8 * i:8 * i + 8],
                                       CAP, ALU.is_le)
            for i in range(NT):
                v.tensor_tensor(eq[:, 8 * i:8 * i + 8], eq[:, 8 * i:8 * i + 8],
                                oh[:, 8 * i:8 * i + 8], ALU.mult)
            for i in range(NT):
                if k == 0:
                    ins = v.tensor_scalar(sb["wcoef"][:, 8 * i:8 * i + 8],
                                          eq[:, 8 * i:8 * i + 8],
                                          sb["wh"][:, 2 * i:2 * i + 1], None, ALU.mult)
                else:
                    ins = v.scalar_tensor_tensor(sb["wcoef"][:, 8 * i:8 * i + 8],
                                                 eq[:, 8 * i:8 * i + 8],
                                                 sb["wh"][:, 2 * i + 1:2 * i + 2],
                                                 sb["wcoef"][:, 8 * i:8 * i + 8],
                                                 ALU.mult, ALU.add)
        bk.inc(ins, "VV")

    bk.guard[("ph", 0)] = bk.guard.get(("pff", 0), ("VV", 0))
    bk.guard[("ph", 1)] = bk.guard.get(("pff", 1), ("VV", 0))
    bk.guard[("py", 0)] = ("VV", bk.cnt["VV"])
    bk.guard[("py", 1)] = ("VV", bk.cnt["VV"])
    bk.guard[("pbf", 0)] = bk.guard.get(("pbf", 0), ("VV", 0))
    bk.guard[("pbf", 1)] = bk.guard.get(("pbf", 1), ("VV", 0))

    frt = bk.fence()
    NSUB = 4
    for ts in range(NSUB):
        @block.sync
        def _(sync):
            if ts == 0:
                bk.wait_fence(sync, frt, ["VV", "TV", "SV"])
            bk.gwait(sync, "xts")
            bk.inc(sync.dma_start(
                out=sb["xts"][:].rearrange("p (tt d) -> p tt d", tt=8),
                in_=xin[ts * 1024:(ts + 1) * 1024, :].rearrange("(tt p) d -> p tt d", p=P)),
                "LD", 16)
        ld_xts = bk.cnt["LD"]

        for tt in range(8):
            i = ts * 8 + tt
            par = tt % 2
            zbf = sb["zbf0"] if par == 0 else sb["zbf1"]

            @block.vector
            def _(v):
                bk.w(v, "LD", ld_xts)
                bk.gwait(v, ("zbf", par))
                bk.inc(v.tensor_scalar(zbf[:], sb["xts"][:, tt * D:(tt + 1) * D],
                                       sb["stats"][:, 2 * i:2 * i + 1],
                                       sb["stats"][:, 2 * i + 1:2 * i + 2],
                                       ALU.subtract, ALU.mult), "VV")
            vv_lnz = bk.cnt["VV"]

            @block.tensor
            def _(t):
                bk.w(t, "VV", vv_lnz)
                bk.gwait(t, ("pbf", par))
                for mc in range(NMC):
                    ins = t.transpose(pbf[par][:, mc * P:(mc + 1) * P],
                                      zbf[:, mc * P:(mc + 1) * P], c["ident"])
                bk.inc(ins, "TV")
            tv_tr2 = bk.cnt["TV"]
            bk.gset(("zbf", par), "TV")

            @block.vector
            def _(v):
                bk.w(v, "TV", tv_tr2)
                for mc in range(NMC):
                    ins = v.tensor_copy(
                        sb["lnT"][:, mc * 1024 + tt * P: mc * 1024 + (tt + 1) * P],
                        pbf[par][:, mc * P:(mc + 1) * P])
                bk.inc(ins, "VV")
            bk.gset(("pbf", par), "VV")
        vv_lnT = bk.cnt["VV"]

        if ts == 0:
            @block.gpsimd
            def _(g):
                for j in range(2):
                    buf = sb["w1b0"] if j % 2 == 0 else sb["w1b1"]
                    if cached:
                        bk.gwait(g, ("w1c", 0, j))
                        bk.inc(g.dma_start(out=buf[:], in_=W1C[0, j]), "WD", 16)
                    else:
                        bk.inc(g.dma_start(out=buf[:].rearrange("p (m f) -> p m f", m=8),
                                           in_=io["W1R"][0, j]), "WD", 16)
                    bk.guard[("wd_w1", j % 2)] = bk.cnt["WD"]
                for j in range(2):
                    buf = sb["w2b0"] if j % 2 == 0 else sb["w2b1"]
                    if cached:
                        bk.gwait(g, ("w2c", 0, j))
                        bk.inc(g.dma_start(out=buf[:], in_=W2C[0, j]), "WD", 16)
                    else:
                        bk.inc(g.dma_start(out=buf[:].rearrange("p (g d) -> p g d", g=32),
                                           in_=io["W2R"][0, j]), "WD", 16)
                    bk.guard[("wd_w2", j % 2)] = bk.cnt["WD"]

        for e in range(8):
            pi = ts * 8 + e

            for fs in range(8):
                buf = sb["w1b0"] if fs % 2 == 0 else sb["w1b1"]
                for fc in range(4):
                    gch = fs * 4 + fc
                    for th in range(2):
                        bidx = (gch * 2 + th) % 2
                        pk = pf[bidx]

                        @block.tensor
                        def _(t):
                            if fs == 0 and fc == 0 and th == 0 and e == 0:
                                bk.w(t, "VV", vv_lnT)
                            bk.w(t, "WD", bk.guard[("wd_w1", fs % 2)])
                            bk.gwait(t, ("ph", bidx))
                            for mc in range(NMC):
                                ins = t.matmul(
                                    pk[:],
                                    buf[:, mc * 512 + fc * P: mc * 512 + (fc + 1) * P],
                                    sb["lnT"][:, mc * 1024 + th * 512: mc * 1024 + (th + 1) * 512],
                                    start=(mc == 0), stop=(mc == NMC - 1))
                            bk.inc(ins, "TV")
                        tv_h = bk.cnt["TV"]

                        @block.scalar
                        def _(s):
                            bk.w(s, "TV", tv_h)
                            bk.inc(s.activation(
                                sb["hsb"][:, gch * 1024 + th * 512: gch * 1024 + (th + 1) * 512],
                                pk[:], AF.Gelu,
                                bias=sb["b1"][:, e * 32 + gch:e * 32 + gch + 1]), "SV")
                        bk.gset(("ph", bidx), "SV")
                bk.guard[("w1", fs % 2)] = ("TV", bk.cnt["TV"])
                if (not cached) and ts == 0:
                    @block.sync
                    def _(sync):
                        buf = sb["w1b0"] if fs % 2 == 0 else sb["w1b1"]
                        bk.gwait(sync, ("w1", fs % 2))
                        bk.inc(sync.dma_start(out=W1C[e, fs], in_=buf[:]), "ST", 16)
                        bk.guard[("w1c", e, fs)] = ("ST", bk.cnt["ST"])
                jn = pi * 8 + fs + 2
                if jn < 32 * 8:
                    pin, nfs = divmod(jn, 8)
                    ne = pin % 8

                    @block.gpsimd
                    def _(g):
                        buf2 = sb["w1b0"] if nfs % 2 == 0 else sb["w1b1"]
                        bk.gwait(g, ("w1", nfs % 2))
                        if (not cached) and pi < 8:
                            bk.gwait(g, ("w1c", e, fs))
                        if cached or pin >= 8:
                            bk.gwait(g, ("w1c", ne, nfs))
                            bk.inc(g.dma_start(out=buf2[:], in_=W1C[ne, nfs]), "WD", 16)
                        else:
                            bk.inc(g.dma_start(out=buf2[:].rearrange("p (m f) -> p m f", m=8),
                                               in_=io["W1R"][ne, nfs]), "WD", 16)
                        bk.guard[("wd_w1", nfs % 2)] = bk.cnt["WD"]
            sv_h_all = bk.cnt["SV"]

            b2c = sb["tkm"][:, 192:256]
            for ds in range(4):
                buf = sb["w2b0"] if ds % 2 == 0 else sb["w2b1"]
                for sub in range(4):
                    th, dc = divmod(sub, 2)
                    pk = pf[2 + sub % 2]

                    @block.tensor
                    def _(t):
                        if ds == 0 and sub == 0:
                            bk.w(t, "SV", sv_h_all)
                        bk.w(t, "WD", bk.guard[("wd_w2", ds % 2)])
                        bk.gwait(t, ("py", sub % 2))
                        ins = None
                        for gg in range(32):
                            ins = t.matmul(
                                pk[:],
                                buf[:, gg * 256 + dc * 128:gg * 256 + (dc + 1) * 128],
                                sb["hsb"][:, gg * 1024 + th * 512:gg * 1024 + (th + 1) * 512],
                                start=(gg == 0), stop=(gg == 31))
                        bk.inc(ins, "TV")
                    tv_y = bk.cnt["TV"]

                    @block.scalar
                    def _(s):
                        bk.w(s, "TV", tv_y)
                        bk.gwait(s, ("ybf", sub % 2))
                        bk.inc(s.activation(
                            sb["zbf0"][:, (sub % 2) * 512:(sub % 2) * 512 + 512], pk[:],
                            AF.Identity,
                            bias=b2c[:, e * 8 + ds * 2 + dc:e * 8 + ds * 2 + dc + 1]), "SV")
                    sv_ybf = bk.cnt["SV"]
                    bk.gset(("py", sub % 2), "SV")

                    @block.tensor
                    def _(t):
                        bk.w(t, "SV", sv_ybf)
                        bk.gwait(t, ("pbf", sub % 2))
                        ins = None
                        for tq in range(4):
                            ins = t.transpose(
                                pbf[sub % 2][:, tq * 128:(tq + 1) * 128],
                                sb["zbf0"][:, (sub % 2) * 512 + tq * 128:
                                           (sub % 2) * 512 + (tq + 1) * 128],
                                c["ident"])
                        bk.inc(ins, "TV")
                    tv_ytr = bk.cnt["TV"]
                    bk.gset(("ybf", sub % 2), "TV")

                    @block.vector
                    def _(v):
                        bk.w(v, "TV", tv_ytr)
                        ins = None
                        for tq in range(4):
                            tt = th * 4 + tq
                            i = ts * 8 + tt
                            dst = sb["xts"][:, tt * D + ds * 256 + dc * 128:
                                            tt * D + ds * 256 + (dc + 1) * 128]
                            ins = v.scalar_tensor_tensor(
                                dst, pbf[sub % 2][:, tq * 128:(tq + 1) * 128],
                                sb["wcoef"][:, 8 * i + e:8 * i + e + 1],
                                dst, ALU.mult, ALU.add)
                        bk.inc(ins, "VV")
                    bk.gset(("pbf", sub % 2), "VV")
                bk.guard[("w2", ds % 2)] = ("TV", bk.cnt["TV"])
                if (not cached) and ts == 0:
                    @block.sync
                    def _(sync):
                        buf2 = sb["w2b0"] if ds % 2 == 0 else sb["w2b1"]
                        bk.gwait(sync, ("w2", ds % 2))
                        bk.inc(sync.dma_start(out=W2C[e, ds], in_=buf2[:]), "ST", 16)
                        bk.guard[("w2c", e, ds)] = ("ST", bk.cnt["ST"])
                j2n = pi * 4 + ds + 2
                if j2n < 32 * 4:
                    pin, nds = divmod(j2n, 4)
                    ne = pin % 8

                    @block.gpsimd
                    def _(g):
                        buf2 = sb["w2b0"] if nds % 2 == 0 else sb["w2b1"]
                        bk.gwait(g, ("w2", nds % 2))
                        if (not cached) and pi < 8:
                            bk.gwait(g, ("w2c", e, ds))
                        if cached or pin >= 8:
                            bk.gwait(g, ("w2c", ne, nds))
                            bk.inc(g.dma_start(out=buf2[:], in_=W2C[ne, nds]), "WD", 16)
                        else:
                            bk.inc(g.dma_start(out=buf2[:].rearrange("p (g d) -> p g d", g=32),
                                               in_=io["W2R"][ne, nds]), "WD", 16)
                        bk.guard[("wd_w2", nds % 2)] = bk.cnt["WD"]

        @block.sync
        def _(sync):
            bk.w(sync, "VV")
            bk.inc(sync.dma_start(
                out=xout[ts * 1024:(ts + 1) * 1024, :].rearrange("(tt p) d -> p tt d", p=P),
                in_=sb["xts"][:].rearrange("p (tt d) -> p tt d", tt=8)), "ST", 16)
        bk.gset("xts", "ST")



def alloc_attn0_sbuf(nc, es, sb):
    def S(name, shape, dt):
        sb[name] = es.enter_context(nc.sbuf_tensor("sb_" + name, shape, dt))
    S("relrow", [1, 32], F32)
    S("rowf", [1, 1024], F32)


def emit_attn0(bk, block, io):
    c = io["c"]
    sb = io["sb"]
    pbf = io["pbf"]
    pf = io["pf"]
    xin, xout = io["xin"], io["xout"]
    GINV = 1.0 / 256.0
    SCALE = float(128.0 ** 0.5)

    A = sb["hsb"]
    znT = [sb["lnT"][:, 0:D], sb["lnT"][:, D:2 * D]]
    hid = sb["lnT"][:, 2048:6144]
    outg = sb["lnT"][:, 6144:8192]
    outT = sb["lnT"][:, 0:2048]
    vg = sb["w1b0"]
    gg = sb["w1b1"]
    wo = [sb["w2b0"], sb["w2b1"]]
    qkT = sb["xts"][:, 0:4096]
    z0f = sb["xts"][:, 4096:5120]
    sqj = sb["xts"][:, 6144:7168]
    res = sb["xts"][:, 6144:7168]
    wqk = sb["xts"][:, 5120:5632].bitcast(BF16)
    lkv = sb["xts"][:, 7168:8192].bitcast(BF16)
    tkm = sb["tkm"]
    m1s = [tkm[:, 12:13], tkm[:, 13:14]]
    r1s = [tkm[:, 14:15], tkm[:, 15:16]]
    relsc = tkm[:, 128:160]
    ossc = tkm[:, 160:176]
    lng = tkm[:, 176:184]
    lnb = tkm[:, 184:192]
    bqk = tkm[:, 192:193]
    biasT = [sb["tkb"][:, 0:256], sb["tkc"][:, 0:256]]
    simf = [sb["tks"][:, 0:256], sb["probs"][:, 0:256]]
    lkbuf = sb["tks"][:, 0:128].bitcast(BF16)
    atb = [sb["zbf0"][:, 0:256], sb["zbf0"][:, 256:512]]
    qq = sb["zbf1"][:, 0:256]
    qk2 = sb["zbf1"][:, 256:512]
    lq = sb["zbf1"][:, 512:768]

    f0 = bk.fence()

    @block.gpsimd
    def _(g):
        bk.wait_fence(g, f0, ["TV", "VV", "SV", "ST"])
        bk.inc(g.dma_start(out=A[:, 0:16384].rearrange("p (u j i) -> p u j i", u=32, j=2),
                           in_=io["masksT"].rearrange("u j p i -> p u j i")), "WD", 16)
        bk.inc(g.dma_start(out=wqk.rearrange("p (m k) -> p m k", m=8),
                           in_=io["WqkR"]), "WD", 16)
        bk.inc(g.dma_start(out=sb["b2"][:], in_=io["rows_s2"]), "WD", 16)
        bk.inc(g.dma_start(out=tkm[:, 160:164], in_=io["osgR"]), "WD", 16)
        bk.inc(g.dma_start(out=tkm[:, 164:168], in_=io["osbR"]), "WD", 16)
        bk.inc(g.dma_start(out=lng, in_=io["lngR"]), "WD", 16)
        bk.inc(g.dma_start(out=lnb, in_=io["lnbR"]), "WD", 16)
        bk.inc(g.dma_start(out=bqk, in_=io["bqkc"]), "WD", 16)
        bk.inc(g.dma_start(out=sb["relrow"][:],
                           in_=io["rel"].rearrange("u q -> q u")), "WD", 16)
        bk.inc(g.dma_start(out=wo[0][:].rearrange("p (e d) -> p e d", e=8),
                           in_=io["WoR"][:, 0:8]), "WD", 16)
        bk.inc(g.dma_start(out=wo[1][:].rearrange("p (e d) -> p e d", e=8),
                           in_=io["WoR"][:, 8:16]), "WD", 16)
    wd_pre = bk.cnt["WD"]

    @block.tensor
    def _(t):
        bk.w(t, "WD", wd_pre)
        bk.inc(t.matmul(pf[0][:, 0:32], c["onesr_f"], sb["relrow"][:],
                        start=True, stop=True), "TV")
    tv_rel = bk.cnt["TV"]

    @block.vector
    def _(v):
        bk.w(v, "TV", tv_rel)
        v.tensor_scalar(relsc, pf[0][:, 0:32], SCALE, None, ALU.mult)
        v.tensor_scalar(ossc[:, 0:1], tkm[:, 160:161], GINV, None, ALU.mult)
        v.tensor_scalar(ossc[:, 1:2], tkm[:, 164:165], GINV, None, ALU.mult)
        v.tensor_copy(ossc[:, 2:3], tkm[:, 162:163])
        v.tensor_copy(ossc[:, 3:4], tkm[:, 166:167])
        v.tensor_copy(ossc[:, 4:5], tkm[:, 161:162])
        v.tensor_copy(ossc[:, 5:6], tkm[:, 165:166])
        v.tensor_copy(ossc[:, 6:7], tkm[:, 163:164])
        v.tensor_copy(ossc[:, 7:8], tkm[:, 167:168])
        v.memset(biasT[0], 0.0)
        bk.inc(v.memset(biasT[1], 0.0), "VV")
        ins = None
        for u in range(32):
            for jh in range(2):
                ins = v.scalar_tensor_tensor(
                    biasT[jh], A[:, (u * 2 + jh) * 256:(u * 2 + jh + 1) * 256],
                    relsc[:, u:u + 1], biasT[jh], ALU.mult, ALU.add)
        bk.inc(ins, "VV")
    vv_bias = bk.cnt["VV"]

    @block.gpsimd
    def _(g):
        bk.w(g, "VV", vv_bias)
        bk.inc(g.dma_start(out=A[:].rearrange("p (m f) -> p m f", m=8),
                           in_=io["WhR"]), "WD", 16)
    wd_wh = bk.cnt["WD"]

    vsp, gsp = io["vsp"], io["gsp"]
    for i in range(NT):
        par = i % 2
        xt = sb["xt0"] if par == 0 else sb["xt1"]
        zbf = sb["zbf0"] if par == 0 else sb["zbf1"]

        @block.sync
        def _(sync):
            if i == 0:
                bk.wait_fence(sync, f0, ["TV", "VV", "SV", "ST"])
            bk.gwait(sync, ("xt", par))
            bk.inc(sync.dma_start(out=xt[:], in_=xin[i * P:(i + 1) * P, :]), "LD", 16)
        ld_x = bk.cnt["LD"]

        @block.scalar
        def _(s):
            bk.w(s, "LD", ld_x)
            bk.inc(s.activation(sqj, xt[:], AF.Square,
                                accum_out=sb["rsm"][:, par:par + 1]), "SV")
        sv_sq = bk.cnt["SV"]

        @block.vector
        def _(v):
            bk.w(v, "LD", ld_x)
            v.reduce_sum(sb["rsm"][:, 2 + par:3 + par], xt[:], axis=AX.X)
            bk.w(v, "SV", sv_sq)
            v.tensor_scalar(sb["rsm"][:, 4 + par:5 + par], sb["rsm"][:, par:par + 1],
                            INV_D, None, ALU.mult)
            v.tensor_scalar(sb["rsm"][:, 6 + par:7 + par],
                            sb["rsm"][:, 2 + par:3 + par],
                            sb["rsm"][:, 2 + par:3 + par], INV_D * INV_D,
                            ALU.mult, ALU.mult)
            v.tensor_scalar(sb["stats"][:, 2 * i:2 * i + 1],
                            sb["rsm"][:, 2 + par:3 + par], INV_D, None, ALU.mult)
            bk.inc(v.tensor_tensor(sb["rsm"][:, 4 + par:5 + par],
                                   sb["rsm"][:, 4 + par:5 + par],
                                   sb["rsm"][:, 6 + par:7 + par], ALU.subtract), "VV")
        vv_var = bk.cnt["VV"]

        @block.scalar
        def _(s):
            bk.w(s, "VV", vv_var)
            s.activation(sb["rsm"][:, 8 + par:9 + par], sb["rsm"][:, 4 + par:5 + par],
                         AF.Ln, bias=float(EPS))
            bk.nc.scalar.nop()
            bk.inc(s.activation(sb["stats"][:, 2 * i + 1:2 * i + 2],
                                sb["rsm"][:, 8 + par:9 + par], AF.Exp, scale=-0.5), "SV")
        sv_r0 = bk.cnt["SV"]

        @block.vector
        def _(v):
            bk.w(v, "SV", sv_r0)
            bk.gwait(v, "z0f")
            bk.inc(v.tensor_scalar(z0f, xt[:], sb["stats"][:, 2 * i:2 * i + 1],
                                   sb["stats"][:, 2 * i + 1:2 * i + 2],
                                   ALU.subtract, ALU.mult), "VV")
        vv_z0 = bk.cnt["VV"]
        bk.gset(("xt", par), "VV")

        @block.scalar
        def _(s):
            bk.w(s, "VV", vv_z0)
            bk.inc(s.activation(sqj, z0f, AF.Square,
                                accum_out=sb["rsm"][:, par:par + 1]), "SV")
        sv_sq1 = bk.cnt["SV"]

        @block.vector
        def _(v):
            v.reduce_sum(sb["rsm"][:, 2 + par:3 + par], z0f, axis=AX.X)
            bk.w(v, "SV", sv_sq1)
            v.tensor_scalar(sb["rsm"][:, 4 + par:5 + par], sb["rsm"][:, par:par + 1],
                            INV_D, None, ALU.mult)
            v.tensor_scalar(sb["rsm"][:, 6 + par:7 + par],
                            sb["rsm"][:, 2 + par:3 + par],
                            sb["rsm"][:, 2 + par:3 + par], INV_D * INV_D,
                            ALU.mult, ALU.mult)
            v.tensor_scalar(m1s[par], sb["rsm"][:, 2 + par:3 + par], INV_D, None, ALU.mult)
            bk.inc(v.tensor_tensor(sb["rsm"][:, 4 + par:5 + par],
                                   sb["rsm"][:, 4 + par:5 + par],
                                   sb["rsm"][:, 6 + par:7 + par], ALU.subtract), "VV")
        vv_var1 = bk.cnt["VV"]

        @block.scalar
        def _(s):
            bk.w(s, "VV", vv_var1)
            s.activation(sb["rsm"][:, 8 + par:9 + par], sb["rsm"][:, 4 + par:5 + par],
                         AF.Ln, bias=float(EPS))
            bk.nc.scalar.nop()
            bk.inc(s.activation(r1s[par], sb["rsm"][:, 8 + par:9 + par],
                                AF.Exp, scale=-0.5), "SV")
        sv_r1 = bk.cnt["SV"]

        @block.vector
        def _(v):
            bk.w(v, "SV", sv_r1)
            bk.gwait(v, ("zbf", par))
            bk.inc(v.tensor_scalar(zbf[:], z0f, m1s[par], r1s[par],
                                   ALU.subtract, ALU.mult), "VV")
        vv_zn = bk.cnt["VV"]
        bk.gset("z0f", "VV")

        @block.tensor
        def _(t):
            bk.w(t, "VV", vv_zn)
            bk.gwait(t, ("pbf", par))
            for mc in range(NMC):
                ins = t.transpose(pbf[par][:, mc * P:(mc + 1) * P],
                                  zbf[:, mc * P:(mc + 1) * P], c["ident"])
            bk.inc(ins, "TV")
        tv_tr = bk.cnt["TV"]
        bk.gset(("zbf", par), "TV")

        @block.vector
        def _(v):
            bk.w(v, "TV", tv_tr)
            bk.gwait(v, ("znT", par))
            for mc in range(NMC):
                ins = v.tensor_scalar(znT[par][:, mc * P:(mc + 1) * P],
                                      pbf[par][:, mc * P:(mc + 1) * P],
                                      lng[:, mc:mc + 1], lnb[:, mc:mc + 1],
                                      ALU.mult, ALU.add)
            bk.inc(ins, "VV")
        vv_znT = bk.cnt["VV"]
        bk.gset(("pbf", par), "VV")

        for fc in range(8):
            bidx = fc % 2
            pk = pf[bidx]

            @block.tensor
            def _(t):
                if i == 0 and fc == 0:
                    bk.w(t, "WD", wd_wh)
                bk.w(t, "VV", vv_znT)
                bk.gwait(t, ("ph", bidx))
                for mc in range(NMC):
                    t.matmul(pk[:], znT[par][:, mc * P:(mc + 1) * P],
                             A[:, mc * 4096 + fc * 512: mc * 4096 + (fc + 1) * 512],
                             start=(mc == 0), stop=False)
                ins = t.matmul(pk[:], c["onesr_bf"],
                               sb["b2"][0:1, fc * 512:(fc + 1) * 512],
                               start=False, stop=True)
                bk.inc(ins, "TV")
            tv_h = bk.cnt["TV"]

            @block.scalar
            def _(s):
                bk.w(s, "TV", tv_h)
                bk.gwait(s, "hid")
                bk.gwait(s, "hid_st")
                bk.inc(s.activation(hid[:, fc * 512:(fc + 1) * 512], pk[:],
                                    AF.Silu), "SV")
            bk.gset(("ph", bidx), "SV")
        sv_hid = bk.cnt["SV"]

        @block.sync
        def _(sync):
            bk.w(sync, "SV", sv_hid)
            sync.dma_start(out=vsp[i * P:(i + 1) * P, :], in_=hid[:, 0:2048]).then_inc(
                bk.sem["ST"], 16)
            bk.cnt["ST"] += 16
            bk.inc(sync.dma_start(out=gsp[i * P:(i + 1) * P, :], in_=hid[:, 2048:4096]),
                   "ST", 16)
        st_sp = bk.cnt["ST"]

        @block.tensor
        def _(t):
            bk.gwait(t, ("ph", par))
            for mc in range(NMC):
                ins = t.matmul(pf[par][:, 0:128], wqk[:, mc * P:(mc + 1) * P],
                               znT[par][:, mc * P:(mc + 1) * P],
                               start=(mc == 0), stop=(mc == NMC - 1))
            bk.inc(ins, "TV")
        tv_qk = bk.cnt["TV"]
        bk.gset(("znT", par), "TV")

        @block.scalar
        def _(s):
            bk.w(s, "TV", tv_qk)
            bk.inc(s.activation(qkT[:, i * P:(i + 1) * P], pf[par][:, 0:128],
                                AF.Silu, bias=bqk), "SV")
        sv_qkT = bk.cnt["SV"]
        bk.gset(("ph", par), "SV")

        @block.vector
        def _(v):
            bk.w(v, "SV", sv_qkT)
            bk.inc(v.tensor_scalar(lkbuf[:, 0:128], qkT[:, i * P:(i + 1) * P],
                                   ossc[:, 6:7], ossc[:, 7:8], ALU.mult, ALU.add), "VV")
        vv_lk = bk.cnt["VV"]

        @block.tensor
        def _(t):
            bk.w(t, "VV", vv_lk)
            bk.inc(t.transpose(pbf[par][:, 896:1024], lkbuf[:, 0:128],
                               c["ident"]), "TV")
        tv_lkt = bk.cnt["TV"]

        @block.vector
        def _(v):
            bk.w(v, "TV", tv_lkt)
            bk.inc(v.tensor_copy(lkbuf[:, 128:256], pbf[par][:, 896:1024]), "VV")
        vv_lkt = bk.cnt["VV"]
        bk.gset(("pbf", par), "VV")

        @block.tensor
        def _(t):
            bk.w(t, "VV", vv_lkt)
            ins = None
            for es_ in range(4):
                ins = t.matmul(pf[2 + es_][:], lkbuf[:, 128:256],
                               hid[:, es_ * 512:(es_ + 1) * 512],
                               start=(i == 0), stop=(i == NT - 1))
            bk.inc(ins, "TV")
        tv_pkv = bk.cnt["TV"]
        bk.guard["hid"] = ("TV", tv_pkv)
        bk.guard["hid_st"] = ("ST", st_sp)

    fp1 = bk.fence()

    @block.vector
    def _(v):
        bk.wait_fence(v, fp1, ["TV"])
        ins = None
        for es_ in range(4):
            ins = v.tensor_scalar(lkv[:, es_ * 512:(es_ + 1) * 512],
                                  pf[2 + es_][:], 1.0 / 4096.0, None, ALU.mult)
        bk.inc(ins, "VV")
    vv_lkv = bk.cnt["VV"]
    bk.guard[("psim", 0)] = ("VV", vv_lkv)
    bk.guard[("pout", 0)] = ("VV", vv_lkv)
    bk.guard[("pout", 1)] = ("VV", vv_lkv)
    bk.guard[("pwo", 0)] = ("SV", bk.cnt["SV"])
    bk.guard[("pwo", 1)] = ("SV", bk.cnt["SV"])

    for gidx in range(16):
        @block.sync
        def _(sync):
            if gidx == 0:
                bk.w(sync, "VV", vv_lkv)
            bk.gwait(sync, "vgld")
            sync.dma_start(out=vg[:].rearrange("p (tt e) -> p tt e", tt=2),
                           in_=vsp[gidx * 256:(gidx + 1) * 256, :].rearrange(
                               "(tt p) e -> p tt e", p=P)).then_inc(bk.sem["LD"], 16)
            bk.cnt["LD"] += 16
            bk.inc(sync.dma_start(out=gg[:].rearrange("p (tt e) -> p tt e", tt=2),
                                  in_=gsp[gidx * 256:(gidx + 1) * 256, :].rearrange(
                                      "(tt p) e -> p tt e", p=P)), "LD", 16)
        ld_vg = bk.cnt["LD"]

        @block.vector
        def _(v):
            bk.gwait(v, "qslice")
            v.tensor_scalar(qq, qkT[:, gidx * 256:(gidx + 1) * 256],
                            ossc[:, 0:1], ossc[:, 1:2], ALU.mult, ALU.add)
            v.tensor_scalar(qk2, qkT[:, gidx * 256:(gidx + 1) * 256],
                            ossc[:, 2:3], ossc[:, 3:4], ALU.mult, ALU.add)
            bk.inc(v.tensor_scalar(lq, qkT[:, gidx * 256:(gidx + 1) * 256],
                                   ossc[:, 4:5], ossc[:, 5:6], ALU.mult, ALU.add), "VV")
        vv_q = bk.cnt["VV"]

        @block.tensor
        def _(t):
            bk.w(t, "VV", vv_q)
            bk.gwait(t, ("psim", 0))
            t.matmul(pf[4][:, 0:256], qk2[:, 0:128], qq, start=True, stop=True)
            bk.inc(t.matmul(pf[5][:, 0:256], qk2[:, 128:256], qq, start=True,
                            stop=True), "TV")
        tv_sim = bk.cnt["TV"]
        bk.gset("qslice", "TV")

        @block.vector
        def _(v):
            bk.w(v, "TV", tv_sim)
            v.tensor_tensor(simf[0], pf[4][:, 0:256], biasT[0], ALU.add)
            v.tensor_tensor(simf[1], pf[5][:, 0:256], biasT[1], ALU.add)
            v.tensor_scalar(simf[0], simf[0], 0.0, None, ALU.max)
            bk.inc(v.tensor_scalar(simf[1], simf[1], 0.0, None, ALU.max), "VV")
        vv_sim = bk.cnt["VV"]
        bk.gset(("psim", 0), "VV")

        @block.scalar
        def _(s):
            bk.w(s, "VV", vv_sim)
            s.activation(atb[0], simf[0], AF.Square)
            bk.inc(s.activation(atb[1], simf[1], AF.Square), "SV")
        sv_at = bk.cnt["SV"]

        for tt in range(2):
            i2 = gidx * 2 + tt
            par2 = i2 % 2
            xt2 = sb["xt0"] if par2 == 0 else sb["xt1"]

            @block.sync
            def _(sync):
                bk.gwait(sync, ("xt", par2))
                bk.inc(sync.dma_start(out=xt2[:], in_=xin[i2 * P:(i2 + 1) * P, :]),
                       "LD", 16)
            ld_x2 = bk.cnt["LD"]

            for es_ in range(4):
                pk = pf[2 + tt]

                @block.tensor
                def _(t):
                    if tt == 0 and es_ == 0:
                        bk.w(t, "SV", sv_at)
                        bk.w(t, "LD", ld_vg)
                    bk.gwait(t, ("pout", tt))
                    t.matmul(pk[:], atb[0][:, tt * 128:(tt + 1) * 128],
                             vg[:, es_ * 512:(es_ + 1) * 512],
                             start=True, stop=False)
                    t.matmul(pk[:], atb[1][:, tt * 128:(tt + 1) * 128],
                             vg[:, 2048 + es_ * 512:2048 + (es_ + 1) * 512],
                             start=False, stop=False)
                    ins = t.matmul(pk[:], lq[:, tt * 128:(tt + 1) * 128],
                                   lkv[:, es_ * 512:(es_ + 1) * 512],
                                   start=False, stop=True)
                    bk.inc(ins, "TV")
                tv_out = bk.cnt["TV"]

                @block.vector
                def _(v):
                    bk.w(v, "TV", tv_out)
                    bk.inc(v.tensor_tensor(outg[:, es_ * 512:(es_ + 1) * 512], pk[:],
                                           gg[:, tt * 2048 + es_ * 512:
                                              tt * 2048 + (es_ + 1) * 512],
                                           ALU.mult), "VV")
                bk.gset(("pout", tt), "VV")
            vv_og = bk.cnt["VV"]
            if tt == 1:
                bk.gset("vgld", "VV")

            for half in range(2):
                @block.tensor
                def _(t):
                    if half == 0:
                        bk.w(t, "VV", vv_og)
                    bk.gwait(t, ("pbf", half))
                    for k8 in range(8):
                        ec = half * 8 + k8
                        ins = t.transpose(pbf[half][:, k8 * 128:(k8 + 1) * 128],
                                          outg[:, ec * 128:(ec + 1) * 128], c["ident"])
                    bk.inc(ins, "TV")
                tv_tp = bk.cnt["TV"]

                @block.vector
                def _(v):
                    bk.w(v, "TV", tv_tp)
                    bk.inc(v.tensor_copy(outT[:, half * 1024:(half + 1) * 1024],
                                         pbf[half][:]), "VV")
                bk.gset(("pbf", half), "VV")
            vv_oT = bk.cnt["VV"]
            bk.gset("outg_free", "VV")

            for dh in range(2):
                pk = pf[dh]

                @block.tensor
                def _(t):
                    if dh == 0:
                        bk.w(t, "VV", vv_oT)
                    bk.gwait(t, ("pwo", dh))
                    for ec in range(16):
                        t.matmul(pk[:], outT[:, ec * 128:(ec + 1) * 128],
                                 wo[ec // 8][:, (ec % 8) * 1024 + dh * 512:
                                              (ec % 8) * 1024 + (dh + 1) * 512],
                                 start=(ec == 0), stop=False)
                    ins = t.matmul(pk[:], c["onesr_bf"],
                                   sb["b2"][0:1, 4096 + dh * 512:4096 + (dh + 1) * 512],
                                   start=False, stop=True)
                    bk.inc(ins, "TV")
            tv_wo = bk.cnt["TV"]

            @block.vector
            def _(v):
                bk.w(v, "TV", tv_wo)
                bk.w(v, "LD", ld_x2)
                bk.gwait(v, "z0f")
                bk.gwait(v, "resfree")
                v.tensor_scalar(z0f, xt2[:], sb["stats"][:, 2 * i2:2 * i2 + 1],
                                sb["stats"][:, 2 * i2 + 1:2 * i2 + 2],
                                ALU.subtract, ALU.mult)
                v.tensor_tensor(res[:, 0:512], pf[0][:], z0f[:, 0:512], ALU.add)
                v.tensor_tensor(res[:, 512:1024], pf[1][:], z0f[:, 512:1024], ALU.add)
                v.tensor_tensor(res[:, 0:512], res[:, 0:512], xt2[:, 0:512], ALU.add)
                bk.inc(v.tensor_tensor(res[:, 512:1024], res[:, 512:1024],
                                       xt2[:, 512:1024], ALU.add), "VV")
            vv_res = bk.cnt["VV"]
            bk.gset(("xt", par2), "VV")
            bk.gset(("pwo", 0), "VV")
            bk.gset(("pwo", 1), "VV")
            bk.gset("z0f", "VV")

            @block.sync
            def _(sync):
                bk.w(sync, "VV", vv_res)
                bk.inc(sync.dma_start(out=xout[i2 * P:(i2 + 1) * P, :], in_=res),
                       "ST", 16)
            bk.gset("resfree", "ST")



def emit_ffn1(bk, block, io):
    c = io["c"]
    sb = io["sb"]
    pbf = io["pbf"]
    pf = io["pf"]
    xin, xout = io["xin"], io["xout"]

    A = sb["hsb"]
    znT = sb["lnT"][:, 0:4096]
    smT = [sb["lnT"][:, 0:4096], sb["w2b0"][:, 0:4096]]
    contT = sb["lnT"][:, 4096:8192]
    spT = [sb["w1b0"], sb["w1b1"]]
    We1R = sb["w2b1"]
    We2R = sb["w1b0"]
    heT = sb["w1b1"]
    sw = sb["w2b0"][:, 4096:8192]
    xg = sb["xts"][:, 0:4096]
    dec_t = sb["xts"][:, 4096:5120]
    sqj = sb["xts"][:, 5120:6144]
    ghbc = sb["xts"][:, 6144:7168]
    bhbc = sb["xts"][:, 7168:8192]
    encb = sb["tkm"][:, 192:208]
    c2sb = sb["tkm"][:, 208:224]
    s2cb = sb["tkm"][:, 224:225]
    rb1 = sb["tkm"][:, 225:226]
    rb2 = sb["tkm"][:, 226:227]
    be1 = sb["tkm"][:, 128:192]
    be2 = sb["tkm"][:, 227:235]
    prall = sb["probs"]
    f0 = bk.fence()

    @block.gpsimd
    def _(g):
        bk.wait_fence(g, f0, ["TV", "VV", "SV", "ST"])
        bk.inc(g.dma_start(out=A[:, 0:16384].rearrange("p (m f) -> p m f", m=8),
                           in_=io["encR"]), "WD", 16)
        bk.inc(g.dma_start(out=A[:, 16384:32768].rearrange("p (f d) -> p f d", f=16),
                           in_=io["decR"]), "WD", 16)
        bk.inc(g.dma_start(out=sw[:, 0:1024].rearrange("p (f k) -> p f k", f=16),
                           in_=io["s2cR"]), "WD", 16)
        bk.inc(g.dma_start(out=sw[0:64, 1024:3072].rearrange("p (f k) -> p f k", f=16),
                           in_=io["c2sR"]), "WD", 16)
        bk.inc(g.dma_start(out=sw[0:64, 3072:3136], in_=io["rW1R"]), "WD", 16)
        bk.inc(g.dma_start(out=sw[0:64, 3136:3144], in_=io["rW2R"]), "WD", 16)
        bk.inc(g.dma_start(out=encb, in_=io["encbR"]), "WD", 16)
        bk.inc(g.dma_start(out=c2sb, in_=io["c2sbR"]), "WD", 16)
        bk.inc(g.dma_start(out=s2cb[0:64, :], in_=io["s2cbc"]), "WD", 16)
        bk.inc(g.dma_start(out=rb1[0:64, :], in_=io["rb1c"]), "WD", 16)
        bk.inc(g.dma_start(out=rb2[0:8, :], in_=io["rb2c"]), "WD", 16)
        bk.inc(g.dma_start(out=be1, in_=io["be1R"].rearrange("p e h -> p (e h)")), "WD", 16)
        bk.inc(g.dma_start(out=be2[0:64, :], in_=io["be2R"]), "WD", 16)
        bk.inc(g.dma_start(out=sb["b2"][:], in_=io["rows_s4"]), "WD", 16)
        bk.inc(g.dma_start(out=sb["rowf"][:], in_=io["rowf_s4"][:, 0:1024]), "WD", 16)
    wd_pre = bk.cnt["WD"]

    @block.tensor
    def _(t):
        bk.w(t, "WD", wd_pre)
        t.matmul(pf[4][:], c["onesr_f"], sb["rowf"][0:1, 0:512], start=True, stop=True)
        bk.inc(t.matmul(pf[5][:], c["onesr_f"], sb["rowf"][0:1, 512:1024],
                        start=True, stop=True), "TV")
    tv_g1 = bk.cnt["TV"]

    @block.vector
    def _(v):
        bk.w(v, "TV", tv_g1)
        v.tensor_scalar(ghbc[:, 0:512], pf[4][:], 0.5, None, ALU.mult)
        bk.inc(v.tensor_scalar(ghbc[:, 512:1024], pf[5][:], 0.5, None, ALU.mult), "VV")
    vv_g1 = bk.cnt["VV"]

    @block.gpsimd
    def _(g):
        bk.w(g, "VV", vv_g1)
        bk.inc(g.dma_start(out=sb["rowf"][:], in_=io["rowf_s4"][:, 1024:2048]), "WD", 16)
    wd_r2 = bk.cnt["WD"]

    @block.tensor
    def _(t):
        bk.w(t, "WD", wd_r2)
        t.matmul(pf[4][:], c["onesr_f"], sb["rowf"][0:1, 0:512], start=True, stop=True)
        bk.inc(t.matmul(pf[5][:], c["onesr_f"], sb["rowf"][0:1, 512:1024],
                        start=True, stop=True), "TV")
    tv_g2 = bk.cnt["TV"]

    @block.vector
    def _(v):
        bk.w(v, "TV", tv_g2)
        v.tensor_scalar(bhbc[:, 0:512], pf[4][:], 0.5, None, ALU.mult)
        bk.inc(v.tensor_scalar(bhbc[:, 512:1024], pf[5][:], 0.5, None, ALU.mult), "VV")
    vv_g2 = bk.cnt["VV"]
    bk.guard[("p45",)] = ("VV", vv_g2)

    for gi in range(8):
        @block.sync
        def _(sync):
            if gi == 0:
                bk.wait_fence(sync, f0, ["TV", "VV", "SV", "ST"])
            bk.gwait(sync, "xg")
            bk.inc(sync.dma_start(
                out=xg.rearrange("p (tt d) -> p tt d", tt=4),
                in_=xin[gi * 512:(gi + 1) * 512, :].rearrange("(tt p) d -> p tt d", p=P)),
                "LD", 16)
        ld_xg = bk.cnt["LD"]

        for tt in range(4):
            i = gi * 4 + tt
            par = tt % 2
            zbf = sb["zbf0"] if par == 0 else sb["zbf1"]
            xtile = xg[:, tt * D:(tt + 1) * D]

            @block.scalar
            def _(s):
                bk.w(s, "LD", ld_xg)
                bk.gwait(s, "sqj")
                bk.inc(s.activation(sqj, xtile, AF.Square,
                                    accum_out=sb["rsm"][:, par:par + 1]), "SV")
            sv_sq = bk.cnt["SV"]

            @block.vector
            def _(v):
                bk.w(v, "LD", ld_xg)
                v.reduce_sum(sb["rsm"][:, 2 + par:3 + par], xtile, axis=AX.X)
                bk.w(v, "SV", sv_sq)
                v.tensor_scalar(sb["rsm"][:, 4 + par:5 + par], sb["rsm"][:, par:par + 1],
                                INV_D, None, ALU.mult)
                v.tensor_scalar(sb["rsm"][:, 6 + par:7 + par],
                                sb["rsm"][:, 2 + par:3 + par],
                                sb["rsm"][:, 2 + par:3 + par], INV_D * INV_D,
                                ALU.mult, ALU.mult)
                v.tensor_scalar(sb["stats"][:, 2 * i:2 * i + 1],
                                sb["rsm"][:, 2 + par:3 + par], INV_D, None, ALU.mult)
                bk.inc(v.tensor_tensor(sb["rsm"][:, 4 + par:5 + par],
                                       sb["rsm"][:, 4 + par:5 + par],
                                       sb["rsm"][:, 6 + par:7 + par], ALU.subtract), "VV")
            vv_var = bk.cnt["VV"]

            @block.scalar
            def _(s):
                bk.w(s, "VV", vv_var)
                s.activation(sb["rsm"][:, 8 + par:9 + par], sb["rsm"][:, 4 + par:5 + par],
                             AF.Ln, bias=float(EPS))
                bk.nc.scalar.nop()
                bk.inc(s.activation(sb["stats"][:, 2 * i + 1:2 * i + 2],
                                    sb["rsm"][:, 8 + par:9 + par], AF.Exp, scale=-0.5), "SV")
            sv_r = bk.cnt["SV"]

            @block.vector
            def _(v):
                bk.w(v, "SV", sv_r)
                bk.gwait(v, ("zbf", par))
                bk.inc(v.tensor_scalar(zbf[:], xtile, sb["stats"][:, 2 * i:2 * i + 1],
                                       sb["stats"][:, 2 * i + 1:2 * i + 2],
                                       ALU.subtract, ALU.mult), "VV")
            vv_z = bk.cnt["VV"]
            bk.guard["sqj"] = ("VV", vv_z)

            @block.tensor
            def _(t):
                bk.w(t, "VV", vv_z)
                bk.gwait(t, ("pbf", par))
                for mc in range(NMC):
                    ins = t.transpose(pbf[par][:, mc * P:(mc + 1) * P],
                                      zbf[:, mc * P:(mc + 1) * P], c["ident"])
                bk.inc(ins, "TV")
            tv_tr = bk.cnt["TV"]
            bk.gset(("zbf", par), "TV")

            @block.vector
            def _(v):
                bk.w(v, "TV", tv_tr)
                bk.gwait(v, ("znT", 0))
                for mc in range(NMC):
                    ins = v.tensor_copy(znT[:, mc * 512 + tt * P: mc * 512 + (tt + 1) * P],
                                        pbf[par][:, mc * P:(mc + 1) * P])
                bk.inc(ins, "VV")
            bk.gset(("pbf", par), "VV")
        vv_znT = bk.cnt["VV"]
        bk.guard["xg_znT"] = ("VV", vv_znT)

        for fch in range(16):
            bidx = fch % 2
            pk = pf[bidx]

            @block.tensor
            def _(t):
                if gi == 0 and fch == 0:
                    bk.w(t, "WD", wd_pre)
                bk.w(t, "VV", vv_znT)
                bk.gwait(t, ("ph", bidx))
                for mc in range(NMC):
                    ins = t.matmul(pk[:], A[:, mc * 2048 + fch * P: mc * 2048 + (fch + 1) * P],
                                   znT[:, mc * 512:(mc + 1) * 512],
                                   start=(mc == 0), stop=(mc == NMC - 1))
                bk.inc(ins, "TV")
            tv_e = bk.cnt["TV"]

            @block.scalar
            def _(s):
                bk.w(s, "TV", tv_e)
                bk.gwait(s, ("spT", fch % 2))
                bk.inc(s.activation(spT[fch // 8][:, (fch % 8) * 512:(fch % 8 + 1) * 512],
                                    pk[:], AF.Sigmoid,
                                    bias=encb[:, fch:fch + 1]), "SV")
            bk.gset(("ph", bidx), "SV")
        sv_sp = bk.cnt["SV"]

        @block.tensor
        def _(t):
            bk.w(t, "SV", sv_sp)
            bk.gwait(t, ("pc",))
            ins = None
            for fch in range(16):
                ins = t.matmul(pf[2][0:64, 0:512], sw[:, fch * 64:(fch + 1) * 64],
                               spT[fch // 8][:, (fch % 8) * 512:(fch % 8 + 1) * 512],
                               start=(fch == 0), stop=(fch == 15))
            bk.inc(ins, "TV")
        tv_c = bk.cnt["TV"]
        bk.guard[("spT", 0)] = ("TV", tv_c)
        bk.guard[("spT", 1)] = ("TV", tv_c)

        @block.scalar
        def _(s):
            bk.w(s, "TV", tv_c)
            bk.inc(s.activation(contT[0:64, gi * 512:(gi + 1) * 512], pf[2][0:64, 0:512],
                                AF.Identity, bias=s2cb[0:64, :]), "SV")
        sv_ct = bk.cnt["SV"]
        bk.gset(("pc",), "SV")

        @block.tensor
        def _(t):
            bk.w(t, "SV", sv_ct)
            bk.gwait(t, ("ph1",))
            bk.inc(t.matmul(pf[3][0:64, 0:512], sw[0:64, 3072:3136],
                            contT[0:64, gi * 512:(gi + 1) * 512],
                            start=True, stop=True), "TV")
        tv_h1 = bk.cnt["TV"]

        @block.scalar
        def _(s):
            bk.w(s, "TV", tv_h1)
            bk.inc(s.activation(sb["zbf0"][0:64, 0:512], pf[3][0:64, 0:512],
                                AF.Tanh, bias=rb1[0:64, :]), "SV")
        sv_h1 = bk.cnt["SV"]
        bk.gset(("ph1",), "SV")

        @block.tensor
        def _(t):
            bk.w(t, "SV", sv_h1)
            bk.gwait(t, ("plg",))
            bk.inc(t.matmul(pf[3][0:8, 0:512], sw[0:64, 3136:3144],
                            sb["zbf0"][0:64, 0:512], start=True, stop=True), "TV")
        tv_lg = bk.cnt["TV"]

        @block.scalar
        def _(s):
            bk.w(s, "TV", tv_lg)
            bk.inc(s.activation(sb["zbf1"][0:8, 0:512], pf[3][0:8, 0:512],
                                AF.Identity, bias=rb2[0:8, :]), "SV")
        sv_lg = bk.cnt["SV"]
        bk.gset(("plg",), "SV")

        @block.tensor
        def _(t):
            bk.w(t, "SV", sv_lg)
            bk.gwait(t, ("pbf", 0))
            for tt in range(4):
                ins = t.transpose(pbf[0][:, tt * 8:(tt + 1) * 8],
                                  sb["zbf1"][0:8, tt * 128:(tt + 1) * 128],
                                  c["ident"][0:8, 0:8])
            bk.inc(ins, "TV")
        tv_lt = bk.cnt["TV"]

        @block.vector
        def _(v):
            bk.w(v, "TV", tv_lt)
            ins = None
            for tt in range(4):
                i = gi * 4 + tt
                ins = v.tensor_copy(prall[:, 8 * i:8 * i + 8], pbf[0][:, tt * 8:(tt + 1) * 8])
            bk.inc(ins, "VV")
        bk.gset(("pbf", 0), "VV")
        bk.gset("xg", "VV")
    fA = bk.fence()

    @block.vector
    def _(v):
        bk.wait_fence(v, fA, ["VV", "SV"])
        for i in range(NT):
            v.tensor_reduce(sb["tkm"][:, i:i + 1], prall[:, 8 * i:8 * i + 8],
                            axis=AX.X, op=ALU.max, negate=True)
        bk.inc(bk.nc.vector.nop(), "VV")
    vv_nm = bk.cnt["VV"]

    @block.scalar
    def _(s):
        bk.w(s, "VV", vv_nm)
        ins = None
        for i in range(NT):
            ins = s.activation(prall[:, 8 * i:8 * i + 8], prall[:, 8 * i:8 * i + 8],
                               AF.Exp, bias=sb["tkm"][:, i:i + 1])
        bk.inc(ins, "SV")
    sv_ex = bk.cnt["SV"]

    @block.vector
    def _(v):
        bk.w(v, "SV", sv_ex)
        pr = prall
        m1 = sb["tkm"][:, 0:32]
        i1 = sb["tkm"][:, 32:64]
        m2 = sb["tkm"][:, 64:96]
        i2 = sb["tkm"][:, 96:128]
        rh = sb["tkm"][:, 235:236]
        eq = sb["tkb"]
        e9 = sb["tkc"]
        for i in range(NT):
            v.reduce_max(m1[:, i:i + 1], pr[:, 8 * i:8 * i + 8], axis=AX.X)
        for i in range(NT):
            v.tensor_scalar(eq[:, 8 * i:8 * i + 8], pr[:, 8 * i:8 * i + 8],
                            m1[:, i:i + 1], None, ALU.is_equal)
        for i in range(NT):
            v.scalar_tensor_tensor(e9[:, 8 * i:8 * i + 8], eq[:, 8 * i:8 * i + 8],
                                   999.0, c["iota8"], ALU.mult, ALU.subtract)
        for i in range(NT):
            v.reduce_max(i1[:, i:i + 1], e9[:, 8 * i:8 * i + 8], axis=AX.X)
        for i in range(NT):
            v.tensor_scalar(i1[:, i:i + 1], i1[:, i:i + 1], -1.0, 999.0, ALU.mult, ALU.add)
        for i in range(NT):
            v.tensor_scalar(sb["oh1"][:, 8 * i:8 * i + 8], c["iota8"], i1[:, i:i + 1],
                            None, ALU.is_equal)
        for i in range(NT):
            v.scalar_tensor_tensor(eq[:, 8 * i:8 * i + 8], sb["oh1"][:, 8 * i:8 * i + 8],
                                   -1e30, pr[:, 8 * i:8 * i + 8], ALU.mult, ALU.add)
        for i in range(NT):
            v.reduce_max(m2[:, i:i + 1], eq[:, 8 * i:8 * i + 8], axis=AX.X)
        for i in range(NT):
            v.tensor_scalar(e9[:, 8 * i:8 * i + 8], eq[:, 8 * i:8 * i + 8],
                            m2[:, i:i + 1], None, ALU.is_equal)
        for i in range(NT):
            v.scalar_tensor_tensor(e9[:, 8 * i:8 * i + 8], e9[:, 8 * i:8 * i + 8],
                                   999.0, c["iota8"], ALU.mult, ALU.subtract)
        for i in range(NT):
            v.reduce_max(i2[:, i:i + 1], e9[:, 8 * i:8 * i + 8], axis=AX.X)
        for i in range(NT):
            v.tensor_scalar(i2[:, i:i + 1], i2[:, i:i + 1], -1.0, 999.0, ALU.mult, ALU.add)
        for i in range(NT):
            v.tensor_scalar(sb["oh2"][:, 8 * i:8 * i + 8], c["iota8"], i2[:, i:i + 1],
                            None, ALU.is_equal)
        for i in range(NT):
            v.tensor_tensor(sb["tks"][:, i:i + 1], m1[:, i:i + 1], m2[:, i:i + 1], ALU.add)
        for i in range(NT):
            v.reciprocal(sb["tks"][:, 32 + i:33 + i], sb["tks"][:, i:i + 1])
        for i in range(NT):
            v.tensor_scalar(sb["wh"][:, 2 * i:2 * i + 1], m1[:, i:i + 1],
                            sb["tks"][:, 32 + i:33 + i], None, ALU.mult)
        for i in range(NT):
            v.tensor_scalar(sb["wh"][:, 2 * i + 1:2 * i + 2], m2[:, i:i + 1],
                            sb["tks"][:, 32 + i:33 + i], None, ALU.mult)
        for i in range(NT):
            v.tensor_scalar(sb["wcoef"][:, 8 * i:8 * i + 8], sb["oh1"][:, 8 * i:8 * i + 8],
                            sb["wh"][:, 2 * i:2 * i + 1], None, ALU.mult)
        ins = None
        for i in range(NT):
            ins = v.scalar_tensor_tensor(sb["wcoef"][:, 8 * i:8 * i + 8],
                                         sb["oh2"][:, 8 * i:8 * i + 8],
                                         sb["wh"][:, 2 * i + 1:2 * i + 2],
                                         sb["wcoef"][:, 8 * i:8 * i + 8],
                                         ALU.mult, ALU.add)
        bk.inc(ins, "VV")
    vv_gates = bk.cnt["VV"]

    @block.gpsimd
    def _(g):
        bk.w(g, "VV", vv_gates)
        bk.gwait(g, ("spT", 0))
        bk.gwait(g, ("spT", 1))
        bk.inc(g.dma_start(out=We1R[0:64, :].rearrange("p (e h f) -> p e h f", e=8, h=8),
                           in_=io["We1R"]), "WD", 16)
        bk.inc(g.dma_start(out=We2R[:].rearrange("p (e h k) -> p e h k", e=8, h=8),
                           in_=io["We2R"]), "WD", 16)
    wd_we = bk.cnt["WD"]

    for gi in range(8):
        @block.sync
        def _(sync):
            bk.gwait(sync, "xg")
            bk.inc(sync.dma_start(
                out=xg.rearrange("p (tt d) -> p tt d", tt=4),
                in_=xin[gi * 512:(gi + 1) * 512, :].rearrange("(tt p) d -> p tt d", p=P)),
                "LD", 16)
        ld_xg = bk.cnt["LD"]

        @block.vector
        def _(v):
            bk.gwait(v, "ofacc")
            bk.inc(v.memset(sb["tkb"][:, 0:256], 0.0), "VV")
        vv_of0 = bk.cnt["VV"]

        for e in range(8):
            for ch in range(8):
                bidx = ch % 2
                pk = pf[bidx]

                @block.tensor
                def _(t):
                    if gi == 0 and e == 0 and ch == 0:
                        bk.w(t, "WD", wd_we)
                        bk.w(t, "VV", vv_of0)
                    bk.gwait(t, ("ph", bidx))
                    bk.inc(t.matmul(pk[:], We1R[0:64, (e * 8 + ch) * 128:(e * 8 + ch + 1) * 128],
                                    contT[0:64, gi * 512:(gi + 1) * 512],
                                    start=True, stop=True), "TV")
                tv_he = bk.cnt["TV"]

                @block.scalar
                def _(s):
                    bk.w(s, "TV", tv_he)
                    bk.gwait(s, ("heT", ch % 2))
                    bk.inc(s.activation(heT[:, ch * 512:(ch + 1) * 512], pk[:],
                                        AF.Relu, bias=be1[:, e * 8 + ch:e * 8 + ch + 1]),
                           "SV")
                bk.gset(("ph", bidx), "SV")
            sv_he = bk.cnt["SV"]

            @block.tensor
            def _(t):
                bk.w(t, "SV", sv_he)
                bk.gwait(t, ("pc",))
                ins = None
                for ch in range(8):
                    ins = t.matmul(pf[2][0:64, 0:512],
                                   We2R[:, (e * 8 + ch) * 64:(e * 8 + ch + 1) * 64],
                                   heT[:, ch * 512:(ch + 1) * 512],
                                   start=(ch == 0), stop=(ch == 7))
                bk.inc(ins, "TV")
            tv_eo = bk.cnt["TV"]
            bk.guard[("heT", 0)] = ("TV", tv_eo)
            bk.guard[("heT", 1)] = ("TV", tv_eo)

            @block.scalar
            def _(s):
                bk.w(s, "TV", tv_eo)
                bk.gwait(s, ("eobf",))
                bk.inc(s.activation(sb["zbf0"][0:64, 0:512], pf[2][0:64, 0:512],
                                    AF.Identity, bias=be2[0:64, e:e + 1]), "SV")
            sv_eob = bk.cnt["SV"]
            bk.gset(("pc",), "SV")

            @block.tensor
            def _(t):
                bk.w(t, "SV", sv_eob)
                bk.gwait(t, ("pbf", 0))
                for tt in range(4):
                    ins = t.transpose(pbf[0][:, tt * 64:(tt + 1) * 64],
                                      sb["zbf0"][0:64, tt * 128:(tt + 1) * 128],
                                      c["ident"][0:64, 0:64])
                bk.inc(ins, "TV")
            tv_eot = bk.cnt["TV"]
            bk.gset(("eobf",), "TV")

            @block.vector
            def _(v):
                bk.w(v, "TV", tv_eot)
                ins = None
                for tt in range(4):
                    i = gi * 4 + tt
                    dst = sb["tkb"][:, tt * 64:(tt + 1) * 64]
                    ins = v.scalar_tensor_tensor(dst, pbf[0][:, tt * 64:(tt + 1) * 64],
                                                 sb["wcoef"][:, 8 * i + e:8 * i + e + 1],
                                                 dst, ALU.mult, ALU.add)
                bk.inc(ins, "VV")
            bk.gset(("pbf", 0), "VV")
        vv_of = bk.cnt["VV"]

        @block.vector
        def _(v):
            v.tensor_copy(sb["tkt"][:, 0:1], c["iota8"][:, 0:1])
            v.tensor_copy(sb["tkt"][:, 1:2], c["iota8"][:, 0:1])
            bk.inc(v.tensor_copy(sb["zbf1"][:, 0:256], sb["tkb"][:, 0:256]), "VV")
        vv_ofb = bk.cnt["VV"]
        bk.gset("ofacc", "VV")

        @block.tensor
        def _(t):
            bk.w(t, "VV", vv_ofb)
            bk.gwait(t, ("pbf", 1))
            for tt in range(4):
                ins = t.transpose(pbf[1][0:64, tt * 128:(tt + 1) * 128],
                                  sb["zbf1"][:, tt * 64:(tt + 1) * 64], c["ident"])
            bk.inc(ins, "TV")
        tv_oft = bk.cnt["TV"]

        @block.vector
        def _(v):
            bk.w(v, "TV", tv_oft)
            ins = None
            for tt in range(4):
                ins = v.tensor_copy(sb["zbf0"][0:64, tt * 128:(tt + 1) * 128],
                                    pbf[1][0:64, tt * 128:(tt + 1) * 128])
            bk.inc(ins, "VV")
        vv_ofT = bk.cnt["VV"]
        bk.gset(("pbf", 1), "VV")

        for fch in range(16):
            bidx = fch % 2
            pk = pf[bidx]

            @block.tensor
            def _(t):
                if fch == 0:
                    bk.w(t, "VV", vv_ofT)
                bk.gwait(t, ("ph", bidx))
                bk.inc(t.matmul(pk[:], sw[0:64, 1024 + fch * 128:1024 + (fch + 1) * 128],
                                sb["zbf0"][0:64, 0:512], start=True, stop=True), "TV")
            tv_sm = bk.cnt["TV"]

            @block.scalar
            def _(s):
                bk.w(s, "TV", tv_sm)
                bk.gwait(s, ("smT", fch % 2))
                bk.inc(s.activation(smT[fch // 8][:, (fch % 8) * 512:(fch % 8 + 1) * 512],
                                    pk[:], AF.Sigmoid, bias=c2sb[:, fch:fch + 1]), "SV")
            bk.gset(("ph", bidx), "SV")
        sv_smT = bk.cnt["SV"]

        for tt in range(4):
            i = gi * 4 + tt
            for dh in range(2):
                pk = pf[2 + dh]

                @block.tensor
                def _(t):
                    if tt == 0 and dh == 0:
                        bk.w(t, "SV", sv_smT)
                    bk.gwait(t, ("pd", dh))
                    for fch in range(16):
                        t.matmul(pk[:], smT[fch // 8][:, (fch % 8) * 512 + tt * 128:
                                                      (fch % 8) * 512 + (tt + 1) * 128],
                                 A[:, 16384 + fch * 1024 + dh * 512:
                                   16384 + fch * 1024 + (dh + 1) * 512],
                                 start=(fch == 0), stop=False)
                    ins = t.matmul(pk[:], c["onesr_bf"],
                                   sb["b2"][0:1, dh * 512:(dh + 1) * 512],
                                   start=False, stop=True)
                    bk.inc(ins, "TV")
                tv_d = bk.cnt["TV"]

                @block.scalar
                def _(s):
                    bk.w(s, "TV", tv_d)
                    bk.gwait(s, ("dec_t",))
                    bk.inc(s.activation(dec_t[:, dh * 512:(dh + 1) * 512], pk[:],
                                        AF.Sigmoid), "SV")
                bk.gset(("pd", dh), "SV")
            sv_dec = bk.cnt["SV"]

            par = tt % 2

            @block.scalar
            def _(s):
                bk.w(s, "SV", sv_dec)
                bk.gwait(s, "sqj")
                bk.inc(s.activation(sqj, dec_t, AF.Square,
                                    accum_out=sb["rsm"][:, par:par + 1]), "SV")
            sv_sq = bk.cnt["SV"]

            @block.vector
            def _(v):
                bk.w(v, "SV", sv_sq)
                v.reduce_sum(sb["rsm"][:, 2 + par:3 + par], dec_t, axis=AX.X)
                v.tensor_scalar(sb["rsm"][:, 4 + par:5 + par], sb["rsm"][:, par:par + 1],
                                INV_D, None, ALU.mult)
                v.tensor_scalar(sb["rsm"][:, 6 + par:7 + par],
                                sb["rsm"][:, 2 + par:3 + par],
                                sb["rsm"][:, 2 + par:3 + par], INV_D * INV_D,
                                ALU.mult, ALU.mult)
                v.tensor_scalar(sb["rsm"][:, 10 + par:11 + par],
                                sb["rsm"][:, 2 + par:3 + par], INV_D, None, ALU.mult)
                bk.inc(v.tensor_tensor(sb["rsm"][:, 4 + par:5 + par],
                                       sb["rsm"][:, 4 + par:5 + par],
                                       sb["rsm"][:, 6 + par:7 + par], ALU.subtract), "VV")
            vv_var = bk.cnt["VV"]

            @block.scalar
            def _(s):
                bk.w(s, "VV", vv_var)
                s.activation(sb["rsm"][:, 8 + par:9 + par], sb["rsm"][:, 4 + par:5 + par],
                             AF.Ln, bias=float(EPS))
                bk.nc.scalar.nop()
                bk.inc(s.activation(sb["rsm"][:, 12 + par:13 + par],
                                    sb["rsm"][:, 8 + par:9 + par], AF.Exp, scale=-0.5), "SV")
            sv_r = bk.cnt["SV"]

            @block.vector
            def _(v):
                bk.w(v, "SV", sv_r)
                bk.w(v, "LD", ld_xg)
                v.tensor_scalar(dec_t[:, 0:512], dec_t[:, 0:512],
                                sb["rsm"][:, 10 + par:11 + par],
                                sb["rsm"][:, 12 + par:13 + par], ALU.subtract, ALU.mult)
                v.tensor_scalar(dec_t[:, 512:1024], dec_t[:, 512:1024],
                                sb["rsm"][:, 10 + par:11 + par],
                                sb["rsm"][:, 12 + par:13 + par], ALU.subtract, ALU.mult)
                v.tensor_tensor(dec_t[:, 0:512], dec_t[:, 0:512], ghbc[:, 0:512], ALU.mult)
                v.tensor_tensor(dec_t[:, 512:1024], dec_t[:, 512:1024], ghbc[:, 512:1024],
                                ALU.mult)
                v.tensor_tensor(dec_t[:, 0:512], dec_t[:, 0:512], bhbc[:, 0:512], ALU.add)
                v.tensor_tensor(dec_t[:, 512:1024], dec_t[:, 512:1024], bhbc[:, 512:1024],
                                ALU.add)
                xsl = xg[:, tt * D:(tt + 1) * D]
                v.tensor_tensor(xsl[:, 0:512], xsl[:, 0:512], dec_t[:, 0:512], ALU.add)
                bk.inc(v.tensor_tensor(xsl[:, 512:1024], xsl[:, 512:1024],
                                       dec_t[:, 512:1024], ALU.add), "VV")
            vv_res = bk.cnt["VV"]
            bk.gset(("dec_t",), "VV")
            bk.guard["sqj"] = ("VV", vv_res)

        @block.sync
        def _(sync):
            bk.w(sync, "VV")
            bk.inc(sync.dma_start(
                out=xout[gi * 512:(gi + 1) * 512, :].rearrange("(tt p) d -> p tt d", p=P),
                in_=xg.rearrange("p (tt d) -> p tt d", tt=4)), "ST", 16)
        bk.gset("xg", "ST")



def emit_attn1(bk, block, io):
    c = io["c"]
    sb = io["sb"]
    pbf = io["pbf"]
    pf = io["pf"]
    xin, xout = io["xin"], io["xout"]
    QKSCALE = float(1.0 / np.sqrt(2.0 * 1024.0))
    SHIFT = 0.0

    A = sb["hsb"]
    qkvT = A[:, 0:12288]
    tincl = A[:, 12288:12416]
    texcl = A[:, 12416:12544]
    QR1 = A[:, 12544:32768]
    QR2 = sb["w2b0"][:, 0:4352]
    znT = sb["lnT"][:, 0:4096]
    qs = sb["lnT"][:, 4096:8192]
    ks = sb["w2b1"][:, 0:4096]
    Vb = sb["w1b0"][:, 0:2080]
    PT = sb["w1b1"]
    otb = sb["zbf0"][0:65, 0:128]
    xg = sb["xts"][:, 0:4096]
    res = sb["xts"][:, 4096:5120]
    sqj = sb["xts"][:, 5120:6144]
    qkvb = sb["tkm"][:, 192:216]
    recs = sb["tkm"][:, 0:64]

    def qr_slice(fch, mc):
        g = fch * 8 + mc
        if g < 158:
            return QR1[:, g * 128:(g + 1) * 128]
        return QR2[:, (g - 158) * 128:(g - 157) * 128]

    def slot(j):
        return j // 2 + 8 * (j % 2)

    def q_slice(hh, j, prange):
        s = slot(j)
        return qs[0:64, s * 256 + prange[0]: s * 256 + prange[1]]

    def k_slice(hh, j, prange):
        s = slot(j)
        return ks[0:64, s * 256 + prange[0]: s * 256 + prange[1]]

    def v_src(hh, j, prange):
        ch, off = (3 * j + 2) // 2, 64 * (j % 2)
        base = hh * 256
        return (qkvT[off:off + 64, ch * 512 + base + prange[0]: ch * 512 + base + prange[1]],
                off)

    f0 = bk.fence()

    @block.gpsimd
    def _(g):
        bk.wait_fence(g, f0, ["TV", "VV", "SV", "ST"])
        bk.inc(g.dma_start(out=QR1[:].rearrange("p (g f) -> p g f", f=128),
                           in_=io["QR"][:, 0:158]), "WD", 16)
        bk.inc(g.dma_start(out=QR2[:].rearrange("p (g f) -> p g f", f=128),
                           in_=io["QR"][:, 158:192]), "WD", 16)
        bk.inc(g.dma_start(out=tincl, in_=io["tincl"]), "WD", 16)
        bk.inc(g.dma_start(out=texcl, in_=io["texcl"]), "WD", 16)
        bk.inc(g.dma_start(out=qkvb, in_=io["qkvbR"]), "WD", 16)
    wd_pre = bk.cnt["WD"]

    for hp in range(8):
        @block.sync
        def _(sync):
            if hp == 0:
                bk.wait_fence(sync, f0, ["TV", "VV", "SV", "ST"])
            bk.gwait(sync, "xg")
            bk.inc(sync.dma_start(
                out=xg.rearrange("p (tt d) -> p tt d", tt=4),
                in_=xin[hp * 512:(hp + 1) * 512, :].rearrange("(tt p) d -> p tt d", p=P)),
                "LD", 16)
        ld_xg = bk.cnt["LD"]

        for tt in range(4):
            i = hp * 4 + tt
            par = tt % 2
            zbf = sb["zbf1"]
            xtile = xg[:, tt * D:(tt + 1) * D]

            @block.scalar
            def _(s):
                bk.w(s, "LD", ld_xg)
                bk.gwait(s, "sqj")
                bk.inc(s.activation(sqj, xtile, AF.Square,
                                    accum_out=sb["rsm"][:, par:par + 1]), "SV")
            sv_sq = bk.cnt["SV"]

            @block.vector
            def _(v):
                bk.w(v, "LD", ld_xg)
                v.reduce_sum(sb["rsm"][:, 2 + par:3 + par], xtile, axis=AX.X)
                bk.w(v, "SV", sv_sq)
                v.tensor_scalar(sb["rsm"][:, 4 + par:5 + par], sb["rsm"][:, par:par + 1],
                                INV_D, None, ALU.mult)
                v.tensor_scalar(sb["rsm"][:, 6 + par:7 + par],
                                sb["rsm"][:, 2 + par:3 + par],
                                sb["rsm"][:, 2 + par:3 + par], INV_D * INV_D,
                                ALU.mult, ALU.mult)
                v.tensor_scalar(sb["stats"][:, 2 * i:2 * i + 1],
                                sb["rsm"][:, 2 + par:3 + par], INV_D, None, ALU.mult)
                bk.inc(v.tensor_tensor(sb["rsm"][:, 4 + par:5 + par],
                                       sb["rsm"][:, 4 + par:5 + par],
                                       sb["rsm"][:, 6 + par:7 + par], ALU.subtract), "VV")
            vv_var = bk.cnt["VV"]

            @block.scalar
            def _(s):
                bk.w(s, "VV", vv_var)
                s.activation(sb["rsm"][:, 8 + par:9 + par], sb["rsm"][:, 4 + par:5 + par],
                             AF.Ln, bias=float(EPS))
                bk.nc.scalar.nop()
                bk.inc(s.activation(sb["rsm"][:, 12 + par:13 + par],
                                    sb["rsm"][:, 8 + par:9 + par], AF.Exp, scale=-0.5), "SV")
            sv_r = bk.cnt["SV"]

            @block.vector
            def _(v):
                bk.w(v, "SV", sv_r)
                bk.gwait(v, ("zbf5",))
                bk.inc(v.tensor_scalar(zbf[:], xtile, sb["stats"][:, 2 * i:2 * i + 1],
                                       sb["rsm"][:, 12 + par:13 + par],
                                       ALU.subtract, ALU.mult), "VV")
            vv_z = bk.cnt["VV"]
            bk.guard["sqj"] = ("VV", vv_z)

            @block.tensor
            def _(t):
                bk.w(t, "VV", vv_z)
                bk.gwait(t, ("pbf", par))
                for mc in range(NMC):
                    ins = t.transpose(pbf[par][:, mc * P:(mc + 1) * P],
                                      zbf[:, mc * P:(mc + 1) * P], c["ident"])
                bk.inc(ins, "TV")
            tv_tr = bk.cnt["TV"]
            bk.gset(("zbf5",), "TV")

            @block.vector
            def _(v):
                bk.w(v, "TV", tv_tr)
                bk.gwait(v, ("znT5",))
                for mc in range(NMC):
                    ins = v.tensor_copy(znT[:, mc * 512 + tt * P: mc * 512 + (tt + 1) * P],
                                        pbf[par][:, mc * P:(mc + 1) * P])
                bk.inc(ins, "VV")
            bk.gset(("pbf", par), "VV")
        vv_znT = bk.cnt["VV"]
        bk.gset("xg_ln", "VV")

        for fch in range(24):
            bidx = fch % 2
            pk = pf[bidx]

            @block.tensor
            def _(t):
                if hp == 0 and fch == 0:
                    bk.w(t, "WD", wd_pre)
                bk.w(t, "VV", vv_znT)
                bk.gwait(t, ("ph", bidx))
                for mc in range(NMC):
                    ins = t.matmul(pk[:], qr_slice(fch, mc), znT[:, mc * 512:(mc + 1) * 512],
                                   start=(mc == 0), stop=(mc == NMC - 1))
                bk.inc(ins, "TV")
            tv_q = bk.cnt["TV"]

            @block.scalar
            def _(s):
                bk.w(s, "TV", tv_q)
                bk.gwait(s, ("qkvT",))
                bk.inc(s.activation(qkvT[:, fch * 512:(fch + 1) * 512], pk[:],
                                    AF.Identity, bias=qkvb[:, fch:fch + 1]), "SV")
            bk.gset(("ph", bidx), "SV")
        sv_qkv = bk.cnt["SV"]

        for hh in range(2):
            h = hp * 2 + hh
            @block.sync
            def _(sync):
                bk.w(sync, "SV", sv_qkv)
                bk.gwait(sync, ("qks",))
                base = hh * 256
                sync.dma_start(
                    out=qs[0:64, 0:2048].rearrange("p (s t) -> p s t", s=8),
                    in_=qkvT[0:64, base:base + 256].rearrange(
                        "p t -> p t")[:, None, :].rearrange("p q t -> p q t")
                    if False else
                    qkvT[0:64, :].rearrange("p (ch t) -> p ch t", ch=24)[
                        :, 0:24:3, base:base + 256]).then_inc(bk.sem["LD"], 16)
                bk.cnt["LD"] += 16
                sync.dma_start(
                    out=qs[0:64, 2048:4096].rearrange("p (s t) -> p s t", s=8),
                    in_=qkvT[64:128, :].rearrange("p (ch t) -> p ch t", ch=24)[
                        :, 1:24:3, base:base + 256]).then_inc(bk.sem["LD"], 16)
                bk.cnt["LD"] += 16
                sync.dma_start(
                    out=ks[0:64, 0:2048].rearrange("p (s t) -> p s t", s=8),
                    in_=qkvT[64:128, :].rearrange("p (ch t) -> p ch t", ch=24)[
                        :, 0:24:3, base:base + 256]).then_inc(bk.sem["LD"], 16)
                bk.cnt["LD"] += 16
                bk.inc(sync.dma_start(
                    out=ks[0:64, 2048:4096].rearrange("p (s t) -> p s t", s=8),
                    in_=qkvT[0:64, :].rearrange("p (ch t) -> p ch t", ch=24)[
                        :, 2:24:3, base:base + 256]), "LD", 16)
            ld_qk = bk.cnt["LD"]

            @block.vector
            def _(v):
                bk.w(v, "SV", sv_qkv)
                bk.gwait(v, ("Vb",))
                bk.inc(v.memset(Vb, 1.0), "VV")
            vv_vb0 = bk.cnt["VV"]

            for j2 in range(16):
                for p2h in range(2):
                    ub = j2 * 2 + p2h
                    vsrc, voff = v_src(hh, j2, (p2h * 128, p2h * 128 + 128))

                    @block.tensor
                    def _(t):
                        if ub == 0:
                            bk.w(t, "VV", vv_vb0)
                        bk.gwait(t, ("pbf", ub % 2))
                        bk.inc(t.transpose(pbf[ub % 2][:, 0:64], vsrc,
                                           c["ident"][voff:voff + 64, voff:voff + 64]),
                               "TV")
                    tv_vt = bk.cnt["TV"]

                    @block.vector
                    def _(v):
                        bk.w(v, "TV", tv_vt)
                        bk.inc(v.tensor_copy(Vb[:, ub * 65:ub * 65 + 64],
                                             pbf[ub % 2][:, 0:64]), "VV")
                    bk.gset(("pbf", ub % 2), "VV")
            vv_vb = bk.cnt["VV"]

            for p1h in range(2):
                for jh in range(2):
                    n_u2 = 16 * (p1h + 1)
                    tv_s_list = [0] * n_u2

                    def emit_S(ug):
                        p2h, j2 = divmod(ug, 16)
                        par5 = ug % 2
                        sb0 = pf[par5 * 4 + 0]
                        sb1 = pf[par5 * 4 + 1]

                        @block.tensor
                        def _(t):
                            if p1h == 0 and jh == 0 and ug == 0:
                                bk.w(t, "SV", sv_qkv)
                                bk.w(t, "LD", ld_qk)
                            if ug == 0:
                                bk.w(t, "VV", vv_vb)
                                bk.gwait(t, ("pot2",))
                            bk.gwait(t, ("ps5", par5))
                            ins = None
                            for jj in range(8):
                                j1 = 8 * jh + jj
                                pk = sb0 if jj < 4 else sb1
                                ins = t.matmul(
                                    pk[:, (jj % 4) * 128:(jj % 4 + 1) * 128],
                                    k_slice(hh, j2, (p2h * 128, p2h * 128 + 128)),
                                    q_slice(hh, j1, (p1h * 128, p1h * 128 + 128)),
                                    start=True, stop=True)
                            bk.inc(ins, "TV")
                        tv_s_list[ug] = bk.cnt["TV"]

                    def emit_PAV(ug):
                        p2h, j2 = divmod(ug, 16)
                        par5 = ug % 2
                        sb0 = pf[par5 * 4 + 0]
                        sb1 = pf[par5 * 4 + 1]
                        ptbuf = PT[:, par5 * 1024:par5 * 1024 + 1024]

                        @block.scalar
                        def _(s):
                            bk.w(s, "TV", tv_s_list[ug])
                            bk.gwait(s, ("pt5", par5))
                            s.activation(ptbuf[:, 0:512], sb0[:],
                                         AF.Exp, scale=QKSCALE)
                            bk.inc(s.activation(ptbuf[:, 512:1024], sb1[:],
                                                AF.Exp, scale=QKSCALE), "SV")
                        sv_pt = bk.cnt["SV"]
                        bk.gset(("ps5", par5), "SV")

                        vv_mask = None
                        if p2h == p1h:
                            @block.vector
                            def _(v):
                                bk.w(v, "SV", sv_pt)
                                ins = None
                                for jj in range(8):
                                    j1 = 8 * jh + jj
                                    msk = tincl if j1 >= j2 else texcl
                                    ins = v.tensor_tensor(ptbuf[:, jj * 128:(jj + 1) * 128],
                                                          ptbuf[:, jj * 128:(jj + 1) * 128],
                                                          msk, ALU.mult)
                                bk.inc(ins, "VV")
                            vv_mask = bk.cnt["VV"]

                        @block.tensor
                        def _(t):
                            if vv_mask is not None:
                                bk.w(t, "VV", vv_mask)
                            else:
                                bk.w(t, "SV", sv_pt)
                            t.matmul(pf[2][0:65, 0:512],
                                     Vb[:, (j2 * 2 + p2h) * 65:(j2 * 2 + p2h) * 65 + 65],
                                     ptbuf[:, 0:512],
                                     start=(ug == 0), stop=(ug == n_u2 - 1))
                            ins = t.matmul(pf[3][0:65, 0:512],
                                           Vb[:, (j2 * 2 + p2h) * 65:(j2 * 2 + p2h) * 65 + 65],
                                           ptbuf[:, 512:1024],
                                           start=(ug == 0), stop=(ug == n_u2 - 1))
                            bk.inc(ins, "TV")
                        bk.gset(("pt5", par5), "TV")

                    for ug in range(n_u2 + 1):
                        if ug < n_u2:
                            emit_S(ug)
                        if ug >= 1:
                            emit_PAV(ug - 1)
                    tv_av = bk.cnt["TV"]

                    otb8 = sb["zbf0"][0:65, 0:1024]

                    @block.scalar
                    def _(s):
                        bk.w(s, "TV", tv_av)
                        bk.gwait(s, ("otb",))
                        ins = None
                        for jj in range(8):
                            ins = s.activation(
                                otb8[:, jj * 128:(jj + 1) * 128],
                                pf[2 + jj // 4][0:65, (jj % 4) * 128:(jj % 4 + 1) * 128],
                                AF.Identity)
                        bk.inc(ins, "SV")
                    sv_ot = bk.cnt["SV"]

                    @block.tensor
                    def _(t):
                        bk.w(t, "SV", sv_ot)
                        bk.gwait(t, ("pot",))
                        ins = None
                        for jj in range(8):
                            ins = t.transpose(pbf[0][:, jj * 128:jj * 128 + 65],
                                              otb8[:, jj * 128:(jj + 1) * 128],
                                              c["ident"][0:65, 0:65])
                        bk.inc(ins, "TV")
                    tv_ot = bk.cnt["TV"]
                    bk.gset(("otb",), "TV")
                    bk.gset(("pot2",), "TV")

                    @block.vector
                    def _(v):
                        bk.w(v, "TV", tv_ot)
                        if jh == 0:
                            bk.gwait(v, "res5")
                        for jj in range(8):
                            v.tensor_copy(recs[:, jj:jj + 1],
                                          pbf[0][:, jj * 128 + 64:jj * 128 + 65])
                        for jj in range(8):
                            v.reciprocal(recs[:, 8 + jj:9 + jj], recs[:, jj:jj + 1])
                        ins = None
                        for jj in range(8):
                            j1 = 8 * jh + jj
                            ins = v.tensor_scalar(res[:, j1 * 64:(j1 + 1) * 64],
                                                  pbf[0][:, jj * 128:jj * 128 + 64],
                                                  recs[:, 8 + jj:9 + jj], None, ALU.mult)
                        bk.inc(ins, "VV")
                    bk.gset(("pot",), "VV")
                    vv_div = bk.cnt["VV"]

                i2 = hp * 4 + hh * 2 + p1h

                @block.vector
                def _(v):
                    xsl = xg[:, (hh * 2 + p1h) * D:(hh * 2 + p1h + 1) * D]
                    v.tensor_tensor(res[:, 0:512], res[:, 0:512], xsl[:, 0:512], ALU.add)
                    bk.inc(v.tensor_tensor(res[:, 512:1024], res[:, 512:1024],
                                           xsl[:, 512:1024], ALU.add), "VV")
                vv_res = bk.cnt["VV"]

                @block.sync
                def _(sync):
                    bk.w(sync, "VV", vv_res)
                    bk.inc(sync.dma_start(out=xout[i2 * P:(i2 + 1) * P, :], in_=res),
                           "ST", 16)
                bk.gset("res5", "ST")
            bk.gset(("Vb",), "TV")
            bk.gset(("qks",), "TV")
        bk.gset("xg", "ST")
        bk.gset(("qkvT",), "TV")

CONSTS = [("ident", [P, P], BF16), ("u128", [P, P], F32), ("u32x", [32, 32], F32),
          ("iota8", [P, 8], F32), ("onesc", [P, 1], F32), ("onesr_bf", [1, P], BF16),
          ("onesr_f", [1, 128], F32), ("ones32c", [32, 1], F32), ("epsc", [128, 1], F32)]

S1INS = [("WrR", [128, 8, 8], F32), ("W1R", [8, 8, 128, 8 * 512], F32),
         ("W2R", [8, 4, 128, 32 * 256], F32), ("b1R", [128, 8, 32], F32),
         ("b2w", [128, 8, 8], F32), ("mask8_in", [8, 1], F32)]
S2INS = [("WhR", [128, 8, 4096], F32), ("WoR", [128, 16, 1024], F32),
         ("WqkR", [128, 8, 128], F32), ("rows_s2", [1, 8192], F32),
         ("osgR", [128, 4], F32), ("osbR", [128, 4], F32), ("bqkc", [128, 1], F32),
         ("lngR", [128, 8], F32), ("lnbR", [128, 8], F32), ("rel", [32, 1], F32),
         ("masksT", [32, 2, 128, 256], BF16)]
S4INS = [("encR", [128, 8, 2048], F32), ("decR", [128, 16, 1024], F32),
         ("s2cR", [128, 16, 64], F32), ("c2sR", [64, 16, 128], F32),
         ("rW1R", [64, 64], F32), ("rW2R", [64, 8], F32),
         ("We1R", [64, 8, 8, 128], F32), ("We2R", [128, 8, 8, 64], F32),
         ("encbR", [128, 16], F32), ("c2sbR", [128, 16], F32), ("s2cbc", [64, 1], F32),
         ("rb1c", [64, 1], F32), ("rb2c", [8, 1], F32), ("be1R", [128, 8, 8], F32),
         ("be2R", [64, 8], F32), ("rows_s4", [1, 8192], F32), ("rowf_s4", [1, 2048], F32)]
S5INS = [("QR", [128, 192, 128], F32), ("qkvbR", [128, 24], F32),
         ("tincl", [128, 128], BF16), ("texcl", [128, 128], BF16)]


def emit_barrier(bk, block):
    f = bk.fence()

    @block.sync
    def _(sync):
        bk.wait_fence(sync, f, ["TV", "VV", "SV", "WD", "ST", "LD"])

    @block.gpsimd
    def _(g):
        bk.wait_fence(g, f, ["TV", "VV", "SV", "ST", "LD"])

    @block.tensor
    def _(t):
        bk.wait_fence(t, f, ["VV", "SV", "WD", "ST", "LD"])

    @block.vector
    def _(v):
        bk.wait_fence(v, f, ["TV", "SV", "WD", "ST", "LD"])

    @block.scalar
    def _(s):
        bk.wait_fence(s, f, ["TV", "VV", "WD", "ST", "LD"])


def build_megakernel():
    from contextlib import ExitStack
    nc = bass.Bass("TRN2", target_bir_lowering=False)
    x0 = nc.dram_tensor("x0", [TOK, D], F32, kind="ExternalInput").ap()
    xo = nc.dram_tensor("xo", [TOK, D], F32, kind="ExternalOutput").ap()
    xbuf = nc.dram_tensor("xbuf", [TOK, D], F32).ap()
    ins = {}
    for nm, shape, dt in S1INS + S2INS + S4INS + S5INS:
        ins[nm] = nc.dram_tensor(nm, shape, dt, kind="ExternalInput").ap()
    cin = {}
    for nm, shape, dt in CONSTS:
        cin[nm] = nc.dram_tensor("cin_" + nm, shape, dt, kind="ExternalInput").ap()
    rt = nc.dram_tensor("rt", [4, 256], F32).ap()
    cc_in1 = nc.dram_tensor("cc_in1", [1, 16], F32).ap()
    cc_out1 = nc.dram_tensor("cc_out1", [8, 16], F32).ap()
    cc_in2 = nc.dram_tensor("cc_in2", [1, 16], F32).ap()
    cc_out2 = nc.dram_tensor("cc_out2", [8, 16], F32).ap()
    vsp = nc.dram_tensor("vsp", [TOK, 2048], BF16).ap()
    gsp = nc.dram_tensor("gsp", [TOK, 2048], BF16).ap()
    W1C = nc.dram_tensor("W1C", [8, 8, 128, 8 * 512], BF16).ap()
    W2C = nc.dram_tensor("W2C", [8, 4, 128, 32 * 256], BF16).ap()

    es = ExitStack()
    bk = BK(nc, es)
    sb = {}
    alloc_ffn0_sbuf(nc, es, sb)
    alloc_attn0_sbuf(nc, es, sb)
    c = {}
    for nm, shape, dt in CONSTS:
        c[nm] = es.enter_context(nc.sbuf_tensor("c_" + nm, shape, dt))
    nc.const_aps.register(mybir.dt.float32, float(EPS), c["epsc"][:, 0:1])
    pbf = [es.enter_context(nc.psum_tensor(f"pbf{i}", [P, 1024], BF16)) for i in range(2)]
    pf = [es.enter_context(nc.psum_tensor(f"pf{i}", [P, 512], F32)) for i in range(6)]

    with nc.Block() as block:
        @block.sync
        def _(sync):
            for nm in c:
                bk.inc(sync.dma_start(out=c[nm][:], in_=cin[nm]), "LD", 16)

        base_io = dict(c={k: v[:] for k, v in c.items()}, sb=sb, pbf=pbf, pf=pf)
        with nc.named_scope("s1"):
            emit_ffn0(bk, block, dict(base_io, xin=x0, xout=xbuf, WrR=ins["WrR"],
                                      W1R=ins["W1R"], W2R=ins["W2R"], b1R=ins["b1R"],
                                      b2R=ins["b2w"], mask8_in=ins["mask8_in"], rt=rt,
                                      W1C=W1C, W2C=W2C, cached=False,
                                      cc_in=cc_in1, cc_out=cc_out1, cc_name="CC1"))
            emit_barrier(bk, block)
        with nc.named_scope("s2"):
            emit_attn0(bk, block, dict(base_io, xin=xbuf, xout=xbuf, WhR=ins["WhR"],
                                       WoR=ins["WoR"], WqkR=ins["WqkR"],
                                       rows_s2=ins["rows_s2"], osgR=ins["osgR"],
                                       osbR=ins["osbR"], bqkc=ins["bqkc"],
                                       lngR=ins["lngR"], lnbR=ins["lnbR"], rel=ins["rel"],
                                       masksT=ins["masksT"], vsp=vsp, gsp=gsp))
            emit_barrier(bk, block)
        with nc.named_scope("s3"):
            emit_ffn0(bk, block, dict(base_io, xin=xbuf, xout=xbuf, WrR=ins["WrR"],
                                      W1R=ins["W1R"], W2R=ins["W2R"], b1R=ins["b1R"],
                                      b2R=ins["b2w"], mask8_in=ins["mask8_in"], rt=rt,
                                      W1C=W1C, W2C=W2C, cached=True,
                                      cc_in=cc_in2, cc_out=cc_out2, cc_name="CC2"))
            emit_barrier(bk, block)
        with nc.named_scope("s4"):
            io4 = dict(base_io, xin=xbuf, xout=xbuf)
            io4.update({nm: ins[nm] for nm, _, _ in S4INS})
            emit_ffn1(bk, block, io4)
            emit_barrier(bk, block)
        with nc.named_scope("s5"):
            io5 = dict(base_io, xin=xbuf, xout=xbuf)
            io5.update({nm: ins[nm] for nm, _, _ in S5INS})
            emit_attn1(bk, block, io5)
            emit_barrier(bk, block)
        with nc.named_scope("s6"):
            io6 = dict(base_io, xin=xbuf, xout=xo)
            io6.update({nm: ins[nm] for nm, _, _ in S4INS})
            emit_ffn1(bk, block, io6)

        @block.sync
        def _(sync):
            bk.w(sync, "ST")

        @block.gpsimd
        def _(g):
            bk.w(g, "ST")
    es.close()
    return nc


_NC_CACHE = {}


def kernel(**inputs):
    g = prep_host(inputs)
    if "nc" not in _NC_CACHE:
        _NC_CACHE["nc"] = build_megakernel()
    nc = _NC_CACHE["nc"]

    base = {"WrR": g["WrR"], "W1R": g["W1R"].reshape(8, 8, 128, 8 * 512),
            "W2R": g["W2R"].reshape(8, 4, 128, 32 * 256), "b1R": g["b1R"],
            "b2w": g["b2R"], "WhR": g["WhR"], "WoR": g["WoR"], "WqkR": g["WqkR"],
            "rows_s2": g["rows_s2"], "osgR": g["osgR"], "osbR": g["osbR"],
            "bqkc": g["bqkc"], "lngR": g["lngR"], "lnbR": g["lnbR"],
            "rel": np.ascontiguousarray(np.asarray(inputs["L0_attn_rel_emb"],
                                                   dtype=np.float32)),
            "masksT": g["masksT"], "QR": g["QR"], "qkvbR": g["qkvbR"],
            "tincl": g["tincl"], "texcl": g["texcl"]}
    for nm, _, _ in S4INS:
        base[nm] = g[nm]
    for nm, _, _ in CONSTS:
        base["cin_" + nm] = g[{"ident": "ident_bf"}.get(nm, nm)]

    x_full = np.asarray(inputs["L0_attn_x"], dtype=np.float32)
    in_maps = []
    for cid in range(8):
        m = dict(base)
        m["x0"] = np.ascontiguousarray(x_full[cid])
        msk = np.zeros((8, 1), dtype=np.float32)
        msk[:cid] = 1.0
        m["mask8_in"] = msk
        in_maps.append(m)

    from concourse.bass_utils import run_bass_kernel_spmd
    res = run_bass_kernel_spmd(nc, in_maps, core_ids=list(range(8)))
    return np.stack([res.results[cid]["xo"] for cid in range(8)], axis=0)


# version 04
# speedup vs baseline: 1.1033x; 1.1033x over previous
import numpy as np
import ml_dtypes
import concourse.bass as bass
import concourse.mybir as mybir

F32 = mybir.dt.float32
BF16 = mybir.dt.bfloat16
AF = mybir.ActivationFunctionType
ALU = mybir.AluOpType
AX = mybir.AxisListType

P = 128
TOK = 4096
D = 1024
NT = TOK // P
NMC = D // P
CAP = 5120.0
INV_D = 1.0 / D
EPS = 1e-5


def prep_host(inputs):
    g = {}
    W1 = np.asarray(inputs["L0_ffn_W1"])
    W2 = np.asarray(inputs["L0_ffn_W2"])
    g["W1R"] = np.ascontiguousarray(W1.reshape(8, 8, 128, 8, 512).transpose(0, 3, 2, 1, 4))
    g["W2R"] = np.ascontiguousarray(W2.reshape(8, 32, 128, 4, 256).transpose(0, 3, 2, 1, 4))
    g["WrR"] = np.ascontiguousarray(
        np.asarray(inputs["L0_ffn_Wr"]).reshape(8, 128, 8).transpose(1, 0, 2))
    g["b1R"] = np.ascontiguousarray(
        np.asarray(inputs["L0_ffn_b1"]).reshape(8, 32, 128).transpose(2, 0, 1))
    g["b2"] = np.asarray(inputs["L0_ffn_b2"])
    g["b2R"] = np.ascontiguousarray(
        np.asarray(inputs["L0_ffn_b2"]).reshape(8, 8, 128).transpose(2, 0, 1))
    g["ident_bf"] = np.eye(P, dtype=ml_dtypes.bfloat16)
    g["u128"] = np.triu(np.ones((P, P), dtype=np.float32))
    g["u32x"] = np.triu(np.ones((32, 32), dtype=np.float32), 1)
    g["iota8"] = np.broadcast_to(np.arange(8, dtype=np.float32), (P, 8)).copy()
    g["onesc"] = np.ones((P, 1), dtype=np.float32)
    g["onesr_bf"] = np.ones((1, P), dtype=ml_dtypes.bfloat16)
    g["onesr_f"] = np.ones((1, 128), dtype=np.float32)
    g["ones32c"] = np.ones((32, 1), dtype=np.float32)
    g["epsc"] = np.full((P, 1), EPS, dtype=np.float32)
    W_h = np.asarray(inputs["L0_attn_W_h"])
    g["WhR"] = np.ascontiguousarray(W_h.reshape(8, 128, 4096).transpose(1, 0, 2))
    W_o = np.asarray(inputs["L0_attn_W_o"])
    g["WoR"] = np.ascontiguousarray(W_o.reshape(16, 128, 1024).transpose(1, 0, 2))
    W_qk = np.asarray(inputs["L0_attn_W_qk"])
    g["WqkR"] = np.ascontiguousarray(W_qk.reshape(8, 128, 128).transpose(1, 0, 2))
    rows = np.zeros((1, 8192), dtype=np.float32)
    rows[0, 0:4096] = np.asarray(inputs["L0_attn_b_h"])
    rows[0, 4096:5120] = np.asarray(inputs["L0_attn_b_o"])
    g["rows_s2"] = rows
    g["osgR"] = np.ascontiguousarray(np.asarray(inputs["L0_attn_os_g"]).T)
    g["osbR"] = np.ascontiguousarray(np.asarray(inputs["L0_attn_os_b"]).T)
    g["bqkc"] = np.asarray(inputs["L0_attn_b_qk"]).reshape(128, 1).copy()
    g["lngR"] = np.ascontiguousarray(np.asarray(inputs["L0_attn_ln_g"]).reshape(8, 128).T)
    g["lnbR"] = np.ascontiguousarray(np.asarray(inputs["L0_attn_ln_b"]).reshape(8, 128).T)
    import math
    G = 256
    pos = np.arange(G)
    nn = pos[:, None] - pos[None, :]
    nb = 16
    ret = (nn < 0).astype(np.int32) * nb
    na = np.abs(nn)
    mx = 8
    is_small = na < mx
    val_large = mx + (np.log(np.maximum(na, 1).astype(np.float32) / mx)
                      / math.log(128 / mx) * (nb - mx)).astype(np.int32)
    val_large = np.minimum(val_large, nb - 1)
    bucket = ret + np.where(is_small, na, val_large)
    masks = np.zeros((32, 2, 128, 256), dtype=ml_dtypes.bfloat16)
    for u in range(32):
        mu = (bucket == u)
        mT = mu.T
        masks[u, 0] = mT[0:128, :]
        masks[u, 1] = mT[128:256, :]
    g["masksT"] = masks
    q = lambda n: np.asarray(inputs["L1_ffn_" + n])
    g["encR"] = np.ascontiguousarray(q("enc_W").reshape(8, 128, 2048).transpose(1, 0, 2))
    g["decR"] = np.ascontiguousarray(q("dec_W").reshape(16, 128, 1024).transpose(1, 0, 2))
    g["s2cR"] = np.ascontiguousarray(q("s2c_W").reshape(16, 128, 64).transpose(1, 0, 2))
    g["c2sR"] = q("c2s_W").reshape(64, 16, 128)
    g["rW1R"] = q("rW1")
    g["rW2R"] = q("rW2")
    g["We1R"] = np.ascontiguousarray(q("We1").reshape(8, 64, 8, 128).transpose(1, 0, 2, 3))
    g["We2R"] = np.ascontiguousarray(q("We2").reshape(8, 8, 128, 64).transpose(2, 0, 1, 3))
    g["encbR"] = np.ascontiguousarray(q("enc_b").reshape(16, 128).T)
    g["c2sbR"] = np.ascontiguousarray(q("c2s_b").reshape(16, 128).T)
    g["s2cbc"] = q("s2c_b").reshape(64, 1).copy()
    g["rb1c"] = q("rb1").reshape(64, 1).copy()
    g["rb2c"] = q("rb2").reshape(8, 1).copy()
    g["be1R"] = np.ascontiguousarray(q("be1").reshape(8, 8, 128).transpose(2, 0, 1))
    g["be2R"] = np.ascontiguousarray(q("be2").T)
    rows4 = np.zeros((1, 8192), dtype=np.float32)
    rows4[0, 0:1024] = q("dec_b")
    g["rows_s4"] = rows4
    rowf4 = np.zeros((1, 2048), dtype=np.float32)
    rowf4[0, 0:1024] = q("ln_g")
    rowf4[0, 1024:2048] = q("ln_b")
    g["rowf_s4"] = rowf4
    Wq = np.asarray(inputs["L1_attn_qkv_kernel"])
    g["QR"] = np.ascontiguousarray(
        Wq.reshape(8, 128, 24, 128).transpose(1, 2, 0, 3).reshape(128, 192, 128))
    g["qkvbR"] = np.ascontiguousarray(
        np.asarray(inputs["L1_attn_qkv_bias"]).reshape(24, 128).T)
    tri = np.zeros((128, 128), dtype=ml_dtypes.bfloat16)
    p2 = np.arange(128)[:, None]
    p1 = np.arange(128)[None, :]
    g["tincl"] = (p1 >= p2).astype(ml_dtypes.bfloat16)
    g["texcl"] = (p1 > p2).astype(ml_dtypes.bfloat16)
    return g


class BK:
    def __init__(self, nc, es):
        self.nc = nc
        self.es = es
        self.sem = {}
        self.cnt = {}
        for name in ["LD", "WD", "ST", "TV", "VV", "SV", "CC1", "CC2"]:
            self.sem[name] = es.enter_context(nc.semaphore(name))
            self.cnt[name] = 0
        self.guard = {}

    def inc(self, instr, name, n=1):
        instr.then_inc(self.sem[name], n)
        self.cnt[name] += n
        return self.cnt[name]

    def w(self, eng, name, val=None):
        if val is None:
            val = self.cnt[name]
        if val > 0:
            eng.wait_ge(self.sem[name], val)

    def fence(self):
        return dict(self.cnt)

    def wait_fence(self, eng, f, names):
        for n in names:
            if f[n] > 0:
                eng.wait_ge(self.sem[n], f[n])

    def gwait(self, eng, key):
        if key in self.guard:
            nm, v = self.guard[key]
            if v > 0:
                eng.wait_ge(self.sem[nm], v)

    def gset(self, key, name, val=None):
        self.guard[key] = (name, self.cnt[name] if val is None else val)



def alloc_ffn0_sbuf(nc, es, sb):
    def S(name, shape, dt):
        sb[name] = es.enter_context(nc.sbuf_tensor("sb_" + name, shape, dt))
    S("b1", [P, 8 * 32], F32)
    S("b2", [1, 8 * 1024], BF16)
    S("xt0", [P, D], F32)
    S("xt1", [P, D], F32)
    S("zbf0", [P, D], BF16)
    S("zbf1", [P, D], BF16)
    S("stats", [P, 2 * NT], F32)
    S("rsm", [P, 16], F32)
    S("probs", [P, 8 * NT], F32)
    S("tkt", [P, 8], F32)
    S("tkb", [P, 8 * NT], F32)
    S("tkc", [P, 8 * NT], F32)
    S("tks", [P, 8 * NT], F32)
    S("tkm", [P, 256], F32)
    S("oh1", [P, 8 * NT], F32)
    S("oh2", [P, 8 * NT], F32)
    S("wh", [P, 2 * NT], F32)
    S("wcoef", [P, 8 * NT], F32)
    S("cntrow", [1, 512], F32)
    S("cntmat", [32, 16], F32)
    S("ccrow", [1, 16], F32)
    S("bgat", [8, 16], F32)
    S("base", [1, 16], F32)
    S("offb", [32, 16], F32)
    S("offrow", [1, 512], F32)
    S("mask8", [8, 1], F32)
    S("xts", [P, 8 * D], F32)
    S("lnT", [P, NMC * 1024], BF16)
    S("hsb", [P, 32 * 1024], BF16)
    S("w1b0", [P, 8 * 512], BF16)
    S("w1b1", [P, 8 * 512], BF16)
    S("w2b0", [P, 32 * 256], BF16)
    S("w2b1", [P, 32 * 256], BF16)


def emit_ffn0(bk, block, io):
    c = io["c"]
    sb = io["sb"]
    cached = io.get("cached", False)
    W1C, W2C = io["W1C"], io["W2C"]
    wr_sb = sb["tks"][:, 128:160].bitcast(BF16)
    pbf = io["pbf"]
    pf = io["pf"]
    cc_name = io["cc_name"]
    xin, xout = io["xin"], io["xout"]

    f0 = bk.fence()

    @block.gpsimd
    def _(g):
        bk.wait_fence(g, f0, ["TV", "VV", "SV", "ST"])
        bk.inc(g.dma_start(out=wr_sb.rearrange("p (c e) -> p c e", c=NMC),
                           in_=io["WrR"]), "WD", 16)
        bk.inc(g.dma_start(out=sb["b1"][:].rearrange("p (e g) -> p e g", e=8),
                           in_=io["b1R"]), "WD", 16)
        bk.inc(g.dma_start(out=sb["tkm"][:, 192:256].rearrange("p (e h) -> p e h", e=8),
                           in_=io["b2R"]), "WD", 16)
        bk.inc(g.dma_start(out=sb["mask8"][:], in_=io["mask8_in"]), "WD", 16)
    wd_pre = bk.cnt["WD"]

    vv_done = [0] * NT
    for i in range(NT):
        par = i % 2
        xt = sb["xt0"] if par == 0 else sb["xt1"]
        zbf = sb["zbf0"] if par == 0 else sb["zbf1"]
        zT = sb["lnT"][:, 0:D] if par == 0 else sb["lnT"][:, D:2 * D]

        @block.sync
        def _(sync):
            if i == 0:
                bk.wait_fence(sync, f0, ["TV", "VV", "SV", "ST"])
            bk.gwait(sync, ("xt", par))
            bk.inc(sync.dma_start(out=xt[:], in_=xin[i * P:(i + 1) * P, :]), "LD", 16)
        ld_x = bk.cnt["LD"]

        @block.scalar
        def _(s):
            bk.w(s, "LD", ld_x)
            bk.inc(s.activation(sb["xts"][:, 0:D], xt[:], AF.Square,
                                accum_out=sb["rsm"][:, par:par + 1]), "SV")
        sv_sq = bk.cnt["SV"]

        @block.vector
        def _(v):
            bk.w(v, "LD", ld_x)
            nc = bk.nc
            v.reduce_sum(sb["rsm"][:, 2 + par:3 + par], xt[:], axis=AX.X)
            bk.w(v, "SV", sv_sq)
            v.tensor_scalar(sb["rsm"][:, 4 + par:5 + par], sb["rsm"][:, par:par + 1],
                            INV_D, None, ALU.mult)
            v.tensor_scalar(sb["rsm"][:, 6 + par:7 + par],
                            sb["rsm"][:, 2 + par:3 + par],
                            sb["rsm"][:, 2 + par:3 + par], INV_D * INV_D,
                            ALU.mult, ALU.mult)
            v.tensor_scalar(sb["stats"][:, 2 * i:2 * i + 1],
                            sb["rsm"][:, 2 + par:3 + par], INV_D, None, ALU.mult)
            bk.inc(v.tensor_tensor(sb["rsm"][:, 4 + par:5 + par],
                                   sb["rsm"][:, 4 + par:5 + par],
                                   sb["rsm"][:, 6 + par:7 + par], ALU.subtract), "VV")
        vv_var = bk.cnt["VV"]

        @block.scalar
        def _(s):
            bk.w(s, "VV", vv_var)
            s.activation(sb["rsm"][:, 8 + par:9 + par], sb["rsm"][:, 4 + par:5 + par],
                         AF.Ln, bias=float(EPS))
            bk.nc.scalar.nop()
            bk.inc(s.activation(sb["stats"][:, 2 * i + 1:2 * i + 2],
                                sb["rsm"][:, 8 + par:9 + par], AF.Exp, scale=-0.5), "SV")
        sv_rstd = bk.cnt["SV"]

        @block.vector
        def _(v):
            bk.w(v, "SV", sv_rstd)
            bk.gwait(v, ("zbf", par))
            bk.inc(v.tensor_scalar(zbf[:], xt[:], sb["stats"][:, 2 * i:2 * i + 1],
                                   sb["stats"][:, 2 * i + 1:2 * i + 2],
                                   ALU.subtract, ALU.mult), "VV")
        vv_z = bk.cnt["VV"]
        bk.gset(("xt", par), "VV")

        @block.tensor
        def _(t):
            bk.w(t, "VV", vv_z)
            bk.gwait(t, ("pbf", par))
            for mc in range(NMC):
                ins = t.transpose(pbf[par][:, mc * P:(mc + 1) * P],
                                  zbf[:, mc * P:(mc + 1) * P], c["ident"])
            bk.inc(ins, "TV")
        tv_tr = bk.cnt["TV"]

        @block.vector
        def _(v):
            bk.w(v, "TV", tv_tr)
            bk.gwait(v, ("zT", par))
            bk.inc(v.tensor_copy(zT, pbf[par][:]), "VV")
        vv_zT = bk.cnt["VV"]
        bk.gset(("pbf", par), "VV")
        bk.gset(("zbf", par), "VV")

        @block.tensor
        def _(t):
            bk.w(t, "VV", vv_zT)
            if i == 0:
                bk.w(t, "WD", wd_pre)
            bk.gwait(t, ("pfl", par))
            for mc in range(NMC):
                ins = t.matmul(pf[par][:, 0:8], zT[:, mc * P:(mc + 1) * P],
                               wr_sb[:, mc * 8:(mc + 1) * 8],
                               start=(mc == 0), stop=(mc == NMC - 1))
            bk.inc(ins, "TV")
        tv_logit = bk.cnt["TV"]
        bk.gset(("zT", par), "TV")

        @block.vector
        def _(v):
            bk.w(v, "TV", tv_logit)
            bk.inc(v.tensor_reduce(sb["rsm"][:, 10 + par:11 + par], pf[par][:, 0:8],
                                   axis=AX.X, op=ALU.max, negate=True), "VV")
        vv_nm = bk.cnt["VV"]

        @block.scalar
        def _(s):
            bk.w(s, "VV", vv_nm)
            bk.inc(s.activation(sb["probs"][:, 8 * i:8 * i + 8], pf[par][:, 0:8], AF.Exp,
                                bias=sb["rsm"][:, 10 + par:11 + par]), "SV")
        sv_exp = bk.cnt["SV"]
        bk.gset(("pfl", par), "SV")

        @block.vector
        def _(v):
            bk.w(v, "SV", sv_exp)
            bk.inc(bk.nc.vector.nop(), "VV")
        vv_done[i] = bk.cnt["VV"]

    fsw0 = bk.fence()

    @block.vector
    def _(v):
        bk.wait_fence(v, fsw0, ["SV"])
        pr = sb["probs"]
        m1 = sb["tkm"][:, 0:32]
        i1 = sb["tkm"][:, 32:64]
        m2 = sb["tkm"][:, 64:96]
        i2 = sb["tkm"][:, 96:128]
        s12 = sb["tkm"][:, 128:160]
        rh = sb["tkm"][:, 160:192]
        eq = sb["tkb"]
        e9 = sb["tkc"]
        for i in range(NT):
            v.reduce_max(m1[:, i:i + 1], pr[:, 8 * i:8 * i + 8], axis=AX.X)
        for i in range(NT):
            v.tensor_scalar(eq[:, 8 * i:8 * i + 8], pr[:, 8 * i:8 * i + 8],
                            m1[:, i:i + 1], None, ALU.is_equal)
        for i in range(NT):
            v.scalar_tensor_tensor(e9[:, 8 * i:8 * i + 8], eq[:, 8 * i:8 * i + 8],
                                   999.0, c["iota8"], ALU.mult, ALU.subtract)
        for i in range(NT):
            v.reduce_max(i1[:, i:i + 1], e9[:, 8 * i:8 * i + 8], axis=AX.X)
        for i in range(NT):
            v.tensor_scalar(i1[:, i:i + 1], i1[:, i:i + 1], -1.0, 999.0, ALU.mult, ALU.add)
        for i in range(NT):
            v.tensor_scalar(sb["oh1"][:, 8 * i:8 * i + 8], c["iota8"], i1[:, i:i + 1],
                            None, ALU.is_equal)
        for i in range(NT):
            v.scalar_tensor_tensor(eq[:, 8 * i:8 * i + 8], sb["oh1"][:, 8 * i:8 * i + 8],
                                   -1e30, pr[:, 8 * i:8 * i + 8], ALU.mult, ALU.add)
        for i in range(NT):
            v.reduce_max(m2[:, i:i + 1], eq[:, 8 * i:8 * i + 8], axis=AX.X)
        for i in range(NT):
            v.tensor_scalar(e9[:, 8 * i:8 * i + 8], eq[:, 8 * i:8 * i + 8],
                            m2[:, i:i + 1], None, ALU.is_equal)
        for i in range(NT):
            v.scalar_tensor_tensor(e9[:, 8 * i:8 * i + 8], e9[:, 8 * i:8 * i + 8],
                                   999.0, c["iota8"], ALU.mult, ALU.subtract)
        for i in range(NT):
            v.reduce_max(i2[:, i:i + 1], e9[:, 8 * i:8 * i + 8], axis=AX.X)
        for i in range(NT):
            v.tensor_scalar(i2[:, i:i + 1], i2[:, i:i + 1], -1.0, 999.0, ALU.mult, ALU.add)
        for i in range(NT):
            v.tensor_scalar(sb["oh2"][:, 8 * i:8 * i + 8], c["iota8"], i2[:, i:i + 1],
                            None, ALU.is_equal)
        for i in range(NT):
            v.tensor_tensor(s12[:, i:i + 1], m1[:, i:i + 1], m2[:, i:i + 1], ALU.add)
        for i in range(NT):
            v.reciprocal(rh[:, i:i + 1], s12[:, i:i + 1])
        for i in range(NT):
            v.tensor_scalar(rh[:, i:i + 1], rh[:, i:i + 1], 0.5, None, ALU.mult)
        for i in range(NT):
            v.tensor_scalar(sb["wh"][:, 2 * i:2 * i + 1], m1[:, i:i + 1], rh[:, i:i + 1],
                            None, ALU.mult)
        ins = None
        for i in range(NT):
            ins = v.tensor_scalar(sb["wh"][:, 2 * i + 1:2 * i + 2], m2[:, i:i + 1],
                                  rh[:, i:i + 1], None, ALU.mult)
        bk.inc(ins, "VV")

    fsw = bk.fence()

    @block.tensor
    def _(t):
        bk.wait_fence(t, fsw, ["VV", "SV"])
        t.matmul(pf[2][0:1, 0:256], c["onesc"], sb["oh1"][:], start=True, stop=True)
        bk.inc(t.matmul(pf[3][0:1, 0:256], c["onesc"], sb["oh2"][:], start=True,
                        stop=True), "TV")
    tv_cnt = bk.cnt["TV"]

    @block.vector
    def _(v):
        bk.w(v, "TV", tv_cnt)
        v.tensor_copy(sb["cntrow"][0:1, 0:256], pf[2][0:1, 0:256])
        bk.inc(v.tensor_copy(sb["cntrow"][0:1, 256:512], pf[3][0:1, 0:256]), "VV")
    vv_cnt = bk.cnt["VV"]

    rt_flat = io["rt"].rearrange("a b -> (a b)")

    @block.sync
    def _(sync):
        bk.w(sync, "VV", vv_cnt)
        bk.inc(sync.dma_start(out=rt_flat[0:512][None], in_=sb["cntrow"][:]), "LD", 16)
        bk.w(sync, "LD")
        bk.inc(sync.dma_start(out=sb["cntmat"][:, 0:8],
                              in_=rt_flat[0:256].rearrange("(i e) -> i e", e=8)), "LD", 16)
        bk.inc(sync.dma_start(out=sb["cntmat"][:, 8:16],
                              in_=rt_flat[256:512].rearrange("(i e) -> i e", e=8)), "LD", 16)
    ld_cm = bk.cnt["LD"]

    @block.tensor
    def _(t):
        bk.w(t, "LD", ld_cm)
        bk.inc(t.matmul(pf[4][0:1, 0:16], c["ones32c"], sb["cntmat"][:], start=True,
                        stop=True), "TV")
    tv_tot = bk.cnt["TV"]

    @block.vector
    def _(v):
        bk.w(v, "TV", tv_tot)
        bk.inc(v.tensor_copy(sb["ccrow"][:], pf[4][0:1, 0:16]), "VV")
    vv_tot = bk.cnt["VV"]

    @block.gpsimd
    def _(g):
        bk.w(g, "VV", vv_tot)
        bk.inc(g.dma_start(out=io["cc_in"], in_=sb["ccrow"][:]), "WD", 16)
        bk.w(g, "WD")
        g.collective_compute("AllGather", ALU.bypass,
                             replica_groups=[list(range(8))],
                             ins=[io["cc_in"].opt()],
                             outs=[io["cc_out"].opt()]).then_inc(bk.sem[cc_name], 1)
        g.wait_ge(bk.sem[cc_name], 1)
        bk.inc(g.dma_start(out=sb["bgat"][:], in_=io["cc_out"]), "WD", 16)
    wd_cc = bk.cnt["WD"]

    @block.tensor
    def _(t):
        bk.w(t, "WD", wd_cc)
        bk.inc(t.matmul(pf[4][0:1, 16:32], sb["mask8"][:], sb["bgat"][:], start=True,
                        stop=True), "TV")
    tv_base = bk.cnt["TV"]

    @block.vector
    def _(v):
        bk.w(v, "TV", tv_base)
        bk.inc(v.tensor_copy(sb["base"][:], pf[4][0:1, 16:32]), "VV")
    vv_base = bk.cnt["VV"]

    @block.tensor
    def _(t):
        bk.w(t, "VV", vv_base)
        for k in range(2):
            t.matmul(pf[4][0:32, 32 + 8 * k:40 + 8 * k], c["u32x"],
                     sb["cntmat"][:, 8 * k:8 * k + 8], start=True, stop=False)
            ins = t.matmul(pf[4][0:32, 32 + 8 * k:40 + 8 * k], c["onesr_f"][0:1, 0:32],
                           sb["base"][0:1, 8 * k:8 * k + 8], start=False, stop=True)
        bk.inc(ins, "TV")
    tv_offb = bk.cnt["TV"]

    @block.vector
    def _(v):
        bk.w(v, "TV", tv_offb)
        bk.inc(v.tensor_copy(sb["offb"][:], pf[4][0:32, 32:48]), "VV")
    vv_offb = bk.cnt["VV"]

    @block.sync
    def _(sync):
        bk.w(sync, "VV", vv_offb)
        bk.inc(sync.dma_start(out=rt_flat[512:1024].rearrange("(i k e) -> i k e", i=32, k=2),
                              in_=sb["offb"][:].rearrange("i (k e) -> i k e", k=2)), "LD", 16)
        bk.w(sync, "LD")
        bk.inc(sync.dma_start(
            out=sb["offrow"][:].rearrange("q (k i e) -> q k i e", k=2, i=32),
            in_=rt_flat[512:1024].rearrange("(i k e) -> k i e", i=32, k=2)[None]),
            "LD", 16)
    ld_off = bk.cnt["LD"]

    for i in range(NT):
        par = i % 2

        @block.tensor
        def _(t):
            if i == 0:
                bk.w(t, "LD", ld_off)
            bk.gwait(t, ("pff", par))
            for k in range(2):
                oh = sb["oh1"] if k == 0 else sb["oh2"]
                t.matmul(pf[par][:, 16 + 8 * k:24 + 8 * k], c["u128"],
                         oh[:, 8 * i:8 * i + 8], start=True, stop=False)
                ins = t.matmul(pf[par][:, 16 + 8 * k:24 + 8 * k], c["onesr_f"][0:1, :],
                               sb["offrow"][0:1, 256 * k + 8 * i:256 * k + 8 * i + 8],
                               start=False, stop=True)
            bk.inc(ins, "TV")
        tv_rank = bk.cnt["TV"]

        @block.vector
        def _(v):
            bk.w(v, "TV", tv_rank)
            v.tensor_copy(sb["tkb"][:, 8 * i:8 * i + 8], pf[par][:, 16:24])
            bk.inc(v.tensor_copy(sb["tkc"][:, 8 * i:8 * i + 8], pf[par][:, 24:32]), "VV")
        bk.gset(("pff", par), "VV")

    @block.vector
    def _(v):
        eq = sb["tks"]
        ins = None
        for k in range(2):
            rk = sb["tkb"] if k == 0 else sb["tkc"]
            oh = sb["oh1"] if k == 0 else sb["oh2"]
            for i in range(NT):
                v.tensor_single_scalar(eq[:, 8 * i:8 * i + 8], rk[:, 8 * i:8 * i + 8],
                                       CAP, ALU.is_le)
            for i in range(NT):
                v.tensor_tensor(eq[:, 8 * i:8 * i + 8], eq[:, 8 * i:8 * i + 8],
                                oh[:, 8 * i:8 * i + 8], ALU.mult)
            for i in range(NT):
                if k == 0:
                    ins = v.tensor_scalar(sb["wcoef"][:, 8 * i:8 * i + 8],
                                          eq[:, 8 * i:8 * i + 8],
                                          sb["wh"][:, 2 * i:2 * i + 1], None, ALU.mult)
                else:
                    ins = v.scalar_tensor_tensor(sb["wcoef"][:, 8 * i:8 * i + 8],
                                                 eq[:, 8 * i:8 * i + 8],
                                                 sb["wh"][:, 2 * i + 1:2 * i + 2],
                                                 sb["wcoef"][:, 8 * i:8 * i + 8],
                                                 ALU.mult, ALU.add)
        bk.inc(ins, "VV")

    bk.guard[("ph", 0)] = bk.guard.get(("pff", 0), ("VV", 0))
    bk.guard[("ph", 1)] = bk.guard.get(("pff", 1), ("VV", 0))
    bk.guard[("py", 0)] = ("VV", bk.cnt["VV"])
    bk.guard[("py", 1)] = ("VV", bk.cnt["VV"])
    bk.guard[("pbf", 0)] = bk.guard.get(("pbf", 0), ("VV", 0))
    bk.guard[("pbf", 1)] = bk.guard.get(("pbf", 1), ("VV", 0))

    frt = bk.fence()
    NSUB = 4
    for ts in range(NSUB):
        @block.sync
        def _(sync):
            if ts == 0:
                bk.wait_fence(sync, frt, ["VV", "TV", "SV"])
            bk.gwait(sync, "xts")
            bk.inc(sync.dma_start(
                out=sb["xts"][:].rearrange("p (tt d) -> p tt d", tt=8),
                in_=xin[ts * 1024:(ts + 1) * 1024, :].rearrange("(tt p) d -> p tt d", p=P)),
                "LD", 16)
        ld_xts = bk.cnt["LD"]

        for tt in range(8):
            i = ts * 8 + tt
            par = tt % 2
            zbf = sb["zbf0"] if par == 0 else sb["zbf1"]

            @block.vector
            def _(v):
                bk.w(v, "LD", ld_xts)
                bk.gwait(v, ("zbf", par))
                bk.inc(v.tensor_scalar(zbf[:], sb["xts"][:, tt * D:(tt + 1) * D],
                                       sb["stats"][:, 2 * i:2 * i + 1],
                                       sb["stats"][:, 2 * i + 1:2 * i + 2],
                                       ALU.subtract, ALU.mult), "VV")
            vv_lnz = bk.cnt["VV"]

            @block.tensor
            def _(t):
                bk.w(t, "VV", vv_lnz)
                bk.gwait(t, ("pbf", par))
                for mc in range(NMC):
                    ins = t.transpose(pbf[par][:, mc * P:(mc + 1) * P],
                                      zbf[:, mc * P:(mc + 1) * P], c["ident"])
                bk.inc(ins, "TV")
            tv_tr2 = bk.cnt["TV"]
            bk.gset(("zbf", par), "TV")

            @block.vector
            def _(v):
                bk.w(v, "TV", tv_tr2)
                for mc in range(NMC):
                    ins = v.tensor_copy(
                        sb["lnT"][:, mc * 1024 + tt * P: mc * 1024 + (tt + 1) * P],
                        pbf[par][:, mc * P:(mc + 1) * P])
                bk.inc(ins, "VV")
            bk.gset(("pbf", par), "VV")
        vv_lnT = bk.cnt["VV"]

        if ts == 0:
            @block.gpsimd
            def _(g):
                for j in range(2):
                    buf = sb["w1b0"] if j % 2 == 0 else sb["w1b1"]
                    if cached:
                        bk.gwait(g, ("w1c", 0, j))
                        bk.inc(g.dma_start(out=buf[:], in_=W1C[0, j]), "WD", 16)
                    else:
                        bk.inc(g.dma_start(out=buf[:].rearrange("p (m f) -> p m f", m=8),
                                           in_=io["W1R"][0, j]), "WD", 16)
                    bk.guard[("wd_w1", j % 2)] = bk.cnt["WD"]
                for j in range(2):
                    buf = sb["w2b0"] if j % 2 == 0 else sb["w2b1"]
                    if cached:
                        bk.gwait(g, ("w2c", 0, j))
                        bk.inc(g.dma_start(out=buf[:], in_=W2C[0, j]), "WD", 16)
                    else:
                        bk.inc(g.dma_start(out=buf[:].rearrange("p (g d) -> p g d", g=32),
                                           in_=io["W2R"][0, j]), "WD", 16)
                    bk.guard[("wd_w2", j % 2)] = bk.cnt["WD"]

        for e in range(8):
            pi = ts * 8 + e

            for fs in range(8):
                buf = sb["w1b0"] if fs % 2 == 0 else sb["w1b1"]
                for fc in range(4):
                    gch = fs * 4 + fc
                    for th in range(2):
                        bidx = (gch * 2 + th) % 2
                        pk = pf[bidx]

                        @block.tensor
                        def _(t):
                            if fs == 0 and fc == 0 and th == 0 and e == 0:
                                bk.w(t, "VV", vv_lnT)
                            bk.w(t, "WD", bk.guard[("wd_w1", fs % 2)])
                            bk.gwait(t, ("ph", bidx))
                            for mc in range(NMC):
                                ins = t.matmul(
                                    pk[:],
                                    buf[:, mc * 512 + fc * P: mc * 512 + (fc + 1) * P],
                                    sb["lnT"][:, mc * 1024 + th * 512: mc * 1024 + (th + 1) * 512],
                                    start=(mc == 0), stop=(mc == NMC - 1))
                            bk.inc(ins, "TV")
                        tv_h = bk.cnt["TV"]

                        @block.scalar
                        def _(s):
                            bk.w(s, "TV", tv_h)
                            bk.inc(s.activation(
                                sb["hsb"][:, gch * 1024 + th * 512: gch * 1024 + (th + 1) * 512],
                                pk[:], AF.Gelu,
                                bias=sb["b1"][:, e * 32 + gch:e * 32 + gch + 1]), "SV")
                        bk.gset(("ph", bidx), "SV")
                bk.guard[("w1", fs % 2)] = ("TV", bk.cnt["TV"])
                if (not cached) and ts == 0:
                    @block.sync
                    def _(sync):
                        buf = sb["w1b0"] if fs % 2 == 0 else sb["w1b1"]
                        bk.gwait(sync, ("w1", fs % 2))
                        bk.inc(sync.dma_start(out=W1C[e, fs], in_=buf[:]), "ST", 16)
                        bk.guard[("w1c", e, fs)] = ("ST", bk.cnt["ST"])
                jn = pi * 8 + fs + 2
                if jn < 32 * 8:
                    pin, nfs = divmod(jn, 8)
                    ne = pin % 8

                    @block.gpsimd
                    def _(g):
                        buf2 = sb["w1b0"] if nfs % 2 == 0 else sb["w1b1"]
                        bk.gwait(g, ("w1", nfs % 2))
                        if (not cached) and pi < 8:
                            bk.gwait(g, ("w1c", e, fs))
                        if cached or pin >= 8:
                            bk.gwait(g, ("w1c", ne, nfs))
                            bk.inc(g.dma_start(out=buf2[:], in_=W1C[ne, nfs]), "WD", 16)
                        else:
                            bk.inc(g.dma_start(out=buf2[:].rearrange("p (m f) -> p m f", m=8),
                                               in_=io["W1R"][ne, nfs]), "WD", 16)
                        bk.guard[("wd_w1", nfs % 2)] = bk.cnt["WD"]
            sv_h_all = bk.cnt["SV"]

            b2c = sb["tkm"][:, 192:256]
            pend = []

            def y_mm(ds, sub):
                th, dc = divmod(sub, 2)
                buf = sb["w2b0"] if ds % 2 == 0 else sb["w2b1"]
                pk = pf[2 + sub % 2]

                @block.tensor
                def _(t):
                    if ds == 0 and sub == 0:
                        bk.w(t, "SV", sv_h_all)
                    bk.w(t, "WD", bk.guard[("wd_w2", ds % 2)])
                    bk.gwait(t, ("py", sub % 2))
                    ins = None
                    for gg in range(32):
                        ins = t.matmul(
                            pk[:],
                            buf[:, gg * 256 + dc * 128:gg * 256 + (dc + 1) * 128],
                            sb["hsb"][:, gg * 1024 + th * 512:gg * 1024 + (th + 1) * 512],
                            start=(gg == 0), stop=(gg == 31))
                    bk.inc(ins, "TV")
                tv_y = bk.cnt["TV"]

                @block.scalar
                def _(s):
                    bk.w(s, "TV", tv_y)
                    bk.gwait(s, ("ybf", sub % 2))
                    bk.inc(s.activation(
                        sb["zbf0"][:, (sub % 2) * 512:(sub % 2) * 512 + 512], pk[:],
                        AF.Identity,
                        bias=b2c[:, e * 8 + ds * 2 + dc:e * 8 + ds * 2 + dc + 1]), "SV")
                bk.gset(("py", sub % 2), "SV")
                return (ds, sub, bk.cnt["SV"])

            def y_tr(item):
                ds, sub, sv_ybf = item
                th, dc = divmod(sub, 2)

                @block.tensor
                def _(t):
                    bk.w(t, "SV", sv_ybf)
                    bk.gwait(t, ("pbf", sub % 2))
                    ins = None
                    for tq in range(4):
                        ins = t.transpose(
                            pbf[sub % 2][:, tq * 128:(tq + 1) * 128],
                            sb["zbf0"][:, (sub % 2) * 512 + tq * 128:
                                       (sub % 2) * 512 + (tq + 1) * 128],
                            c["ident"])
                    bk.inc(ins, "TV")
                tv_ytr = bk.cnt["TV"]
                bk.gset(("ybf", sub % 2), "TV")

                @block.vector
                def _(v):
                    bk.w(v, "TV", tv_ytr)
                    ins = None
                    for tq in range(4):
                        tt = th * 4 + tq
                        i = ts * 8 + tt
                        dst = sb["xts"][:, tt * D + ds * 256 + dc * 128:
                                        tt * D + ds * 256 + (dc + 1) * 128]
                        ins = v.scalar_tensor_tensor(
                            dst, pbf[sub % 2][:, tq * 128:(tq + 1) * 128],
                            sb["wcoef"][:, 8 * i + e:8 * i + e + 1],
                            dst, ALU.mult, ALU.add)
                    bk.inc(ins, "VV")
                bk.gset(("pbf", sub % 2), "VV")

            for ds in range(4):
                for sub in range(4):
                    item = y_mm(ds, sub)
                    if pend:
                        y_tr(pend.pop(0))
                    pend.append(item)
                bk.guard[("w2", ds % 2)] = ("TV", bk.cnt["TV"])
                if (not cached) and ts == 0:
                    @block.sync
                    def _(sync):
                        buf2 = sb["w2b0"] if ds % 2 == 0 else sb["w2b1"]
                        bk.gwait(sync, ("w2", ds % 2))
                        bk.inc(sync.dma_start(out=W2C[e, ds], in_=buf2[:]), "ST", 16)
                        bk.guard[("w2c", e, ds)] = ("ST", bk.cnt["ST"])
                j2n = pi * 4 + ds + 2
                if j2n < 32 * 4:
                    pin, nds = divmod(j2n, 4)
                    ne = pin % 8

                    @block.gpsimd
                    def _(g):
                        buf2 = sb["w2b0"] if nds % 2 == 0 else sb["w2b1"]
                        bk.gwait(g, ("w2", nds % 2))
                        if (not cached) and pi < 8:
                            bk.gwait(g, ("w2c", e, ds))
                        if cached or pin >= 8:
                            bk.gwait(g, ("w2c", ne, nds))
                            bk.inc(g.dma_start(out=buf2[:], in_=W2C[ne, nds]), "WD", 16)
                        else:
                            bk.inc(g.dma_start(out=buf2[:].rearrange("p (g d) -> p g d", g=32),
                                               in_=io["W2R"][ne, nds]), "WD", 16)
                        bk.guard[("wd_w2", nds % 2)] = bk.cnt["WD"]
            while pend:
                y_tr(pend.pop(0))

        @block.sync
        def _(sync):
            bk.w(sync, "VV")
            bk.inc(sync.dma_start(
                out=xout[ts * 1024:(ts + 1) * 1024, :].rearrange("(tt p) d -> p tt d", p=P),
                in_=sb["xts"][:].rearrange("p (tt d) -> p tt d", tt=8)), "ST", 16)
        bk.gset("xts", "ST")



def alloc_attn0_sbuf(nc, es, sb):
    def S(name, shape, dt):
        sb[name] = es.enter_context(nc.sbuf_tensor("sb_" + name, shape, dt))
    S("relrow", [1, 32], F32)
    S("rowf", [1, 1024], F32)


def emit_attn0(bk, block, io):
    c = io["c"]
    sb = io["sb"]
    pbf = io["pbf"]
    pf = io["pf"]
    xin, xout = io["xin"], io["xout"]
    GINV = 1.0 / 256.0
    SCALE = float(128.0 ** 0.5)

    A = sb["hsb"]
    znT = [sb["lnT"][:, 0:D], sb["lnT"][:, D:2 * D]]
    hid = sb["lnT"][:, 2048:6144]
    outg = sb["lnT"][:, 6144:8192]
    outT = sb["lnT"][:, 0:2048]
    vg = sb["w1b0"]
    gg = sb["w1b1"]
    wo = [sb["w2b0"], sb["w2b1"]]
    qkT = sb["xts"][:, 0:4096]
    z0f = sb["xts"][:, 4096:5120]
    sqj = sb["xts"][:, 6144:7168]
    res = sb["xts"][:, 6144:7168]
    wqk = sb["xts"][:, 5120:5632].bitcast(BF16)
    lkv = sb["xts"][:, 7168:8192].bitcast(BF16)
    tkm = sb["tkm"]
    m1s = [tkm[:, 12:13], tkm[:, 13:14]]
    r1s = [tkm[:, 14:15], tkm[:, 15:16]]
    relsc = tkm[:, 128:160]
    ossc = tkm[:, 160:176]
    lng = tkm[:, 176:184]
    lnb = tkm[:, 184:192]
    bqk = tkm[:, 192:193]
    biasT = [sb["tkb"][:, 0:256], sb["tkc"][:, 0:256]]
    simf = [sb["tks"][:, 0:256], sb["probs"][:, 0:256]]
    lkbuf = sb["tks"][:, 0:128].bitcast(BF16)
    atb = [sb["zbf0"][:, 0:256], sb["zbf0"][:, 256:512]]
    qq = sb["zbf1"][:, 0:256]
    qk2 = sb["zbf1"][:, 256:512]
    lq = sb["zbf1"][:, 512:768]

    f0 = bk.fence()

    @block.gpsimd
    def _(g):
        bk.wait_fence(g, f0, ["TV", "VV", "SV", "ST"])
        bk.inc(g.dma_start(out=A[:, 0:16384].rearrange("p (u j i) -> p u j i", u=32, j=2),
                           in_=io["masksT"].rearrange("u j p i -> p u j i")), "WD", 16)
        bk.inc(g.dma_start(out=wqk.rearrange("p (m k) -> p m k", m=8),
                           in_=io["WqkR"]), "WD", 16)
        bk.inc(g.dma_start(out=sb["b2"][:], in_=io["rows_s2"]), "WD", 16)
        bk.inc(g.dma_start(out=tkm[:, 160:164], in_=io["osgR"]), "WD", 16)
        bk.inc(g.dma_start(out=tkm[:, 164:168], in_=io["osbR"]), "WD", 16)
        bk.inc(g.dma_start(out=lng, in_=io["lngR"]), "WD", 16)
        bk.inc(g.dma_start(out=lnb, in_=io["lnbR"]), "WD", 16)
        bk.inc(g.dma_start(out=bqk, in_=io["bqkc"]), "WD", 16)
        bk.inc(g.dma_start(out=sb["relrow"][:],
                           in_=io["rel"].rearrange("u q -> q u")), "WD", 16)
        bk.inc(g.dma_start(out=wo[0][:].rearrange("p (e d) -> p e d", e=8),
                           in_=io["WoR"][:, 0:8]), "WD", 16)
        bk.inc(g.dma_start(out=wo[1][:].rearrange("p (e d) -> p e d", e=8),
                           in_=io["WoR"][:, 8:16]), "WD", 16)
    wd_pre = bk.cnt["WD"]

    @block.tensor
    def _(t):
        bk.w(t, "WD", wd_pre)
        bk.inc(t.matmul(pf[0][:, 0:32], c["onesr_f"], sb["relrow"][:],
                        start=True, stop=True), "TV")
    tv_rel = bk.cnt["TV"]

    @block.vector
    def _(v):
        bk.w(v, "TV", tv_rel)
        v.tensor_scalar(relsc, pf[0][:, 0:32], SCALE, None, ALU.mult)
        v.tensor_scalar(ossc[:, 0:1], tkm[:, 160:161], GINV, None, ALU.mult)
        v.tensor_scalar(ossc[:, 1:2], tkm[:, 164:165], GINV, None, ALU.mult)
        v.tensor_copy(ossc[:, 2:3], tkm[:, 162:163])
        v.tensor_copy(ossc[:, 3:4], tkm[:, 166:167])
        v.tensor_copy(ossc[:, 4:5], tkm[:, 161:162])
        v.tensor_copy(ossc[:, 5:6], tkm[:, 165:166])
        v.tensor_copy(ossc[:, 6:7], tkm[:, 163:164])
        v.tensor_copy(ossc[:, 7:8], tkm[:, 167:168])
        v.memset(biasT[0], 0.0)
        bk.inc(v.memset(biasT[1], 0.0), "VV")
        ins = None
        for u in range(32):
            for jh in range(2):
                ins = v.scalar_tensor_tensor(
                    biasT[jh], A[:, (u * 2 + jh) * 256:(u * 2 + jh + 1) * 256],
                    relsc[:, u:u + 1], biasT[jh], ALU.mult, ALU.add)
        bk.inc(ins, "VV")
    vv_bias = bk.cnt["VV"]

    @block.gpsimd
    def _(g):
        bk.w(g, "VV", vv_bias)
        bk.inc(g.dma_start(out=A[:].rearrange("p (m f) -> p m f", m=8),
                           in_=io["WhR"]), "WD", 16)
    wd_wh = bk.cnt["WD"]

    vsp, gsp = io["vsp"], io["gsp"]
    for i in range(NT):
        par = i % 2
        xt = sb["xt0"] if par == 0 else sb["xt1"]
        zbf = sb["zbf0"] if par == 0 else sb["zbf1"]

        @block.sync
        def _(sync):
            if i == 0:
                bk.wait_fence(sync, f0, ["TV", "VV", "SV", "ST"])
            bk.gwait(sync, ("xt", par))
            bk.inc(sync.dma_start(out=xt[:], in_=xin[i * P:(i + 1) * P, :]), "LD", 16)
        ld_x = bk.cnt["LD"]

        @block.scalar
        def _(s):
            bk.w(s, "LD", ld_x)
            bk.inc(s.activation(sqj, xt[:], AF.Square,
                                accum_out=sb["rsm"][:, par:par + 1]), "SV")
        sv_sq = bk.cnt["SV"]

        @block.vector
        def _(v):
            bk.w(v, "LD", ld_x)
            v.reduce_sum(sb["rsm"][:, 2 + par:3 + par], xt[:], axis=AX.X)
            bk.w(v, "SV", sv_sq)
            v.tensor_scalar(sb["rsm"][:, 4 + par:5 + par], sb["rsm"][:, par:par + 1],
                            INV_D, None, ALU.mult)
            v.tensor_scalar(sb["rsm"][:, 6 + par:7 + par],
                            sb["rsm"][:, 2 + par:3 + par],
                            sb["rsm"][:, 2 + par:3 + par], INV_D * INV_D,
                            ALU.mult, ALU.mult)
            v.tensor_scalar(sb["stats"][:, 2 * i:2 * i + 1],
                            sb["rsm"][:, 2 + par:3 + par], INV_D, None, ALU.mult)
            bk.inc(v.tensor_tensor(sb["rsm"][:, 4 + par:5 + par],
                                   sb["rsm"][:, 4 + par:5 + par],
                                   sb["rsm"][:, 6 + par:7 + par], ALU.subtract), "VV")
        vv_var = bk.cnt["VV"]

        @block.scalar
        def _(s):
            bk.w(s, "VV", vv_var)
            s.activation(sb["rsm"][:, 8 + par:9 + par], sb["rsm"][:, 4 + par:5 + par],
                         AF.Ln, bias=float(EPS))
            bk.nc.scalar.nop()
            bk.inc(s.activation(sb["stats"][:, 2 * i + 1:2 * i + 2],
                                sb["rsm"][:, 8 + par:9 + par], AF.Exp, scale=-0.5), "SV")
        sv_r0 = bk.cnt["SV"]

        @block.vector
        def _(v):
            bk.w(v, "SV", sv_r0)
            bk.gwait(v, "z0f")
            bk.inc(v.tensor_scalar(z0f, xt[:], sb["stats"][:, 2 * i:2 * i + 1],
                                   sb["stats"][:, 2 * i + 1:2 * i + 2],
                                   ALU.subtract, ALU.mult), "VV")
        vv_z0 = bk.cnt["VV"]
        bk.gset(("xt", par), "VV")

        @block.scalar
        def _(s):
            bk.w(s, "VV", vv_z0)
            bk.inc(s.activation(sqj, z0f, AF.Square,
                                accum_out=sb["rsm"][:, par:par + 1]), "SV")
        sv_sq1 = bk.cnt["SV"]

        @block.vector
        def _(v):
            v.reduce_sum(sb["rsm"][:, 2 + par:3 + par], z0f, axis=AX.X)
            bk.w(v, "SV", sv_sq1)
            v.tensor_scalar(sb["rsm"][:, 4 + par:5 + par], sb["rsm"][:, par:par + 1],
                            INV_D, None, ALU.mult)
            v.tensor_scalar(sb["rsm"][:, 6 + par:7 + par],
                            sb["rsm"][:, 2 + par:3 + par],
                            sb["rsm"][:, 2 + par:3 + par], INV_D * INV_D,
                            ALU.mult, ALU.mult)
            v.tensor_scalar(m1s[par], sb["rsm"][:, 2 + par:3 + par], INV_D, None, ALU.mult)
            bk.inc(v.tensor_tensor(sb["rsm"][:, 4 + par:5 + par],
                                   sb["rsm"][:, 4 + par:5 + par],
                                   sb["rsm"][:, 6 + par:7 + par], ALU.subtract), "VV")
        vv_var1 = bk.cnt["VV"]

        @block.scalar
        def _(s):
            bk.w(s, "VV", vv_var1)
            s.activation(sb["rsm"][:, 8 + par:9 + par], sb["rsm"][:, 4 + par:5 + par],
                         AF.Ln, bias=float(EPS))
            bk.nc.scalar.nop()
            bk.inc(s.activation(r1s[par], sb["rsm"][:, 8 + par:9 + par],
                                AF.Exp, scale=-0.5), "SV")
        sv_r1 = bk.cnt["SV"]

        @block.vector
        def _(v):
            bk.w(v, "SV", sv_r1)
            bk.gwait(v, ("zbf", par))
            bk.inc(v.tensor_scalar(zbf[:], z0f, m1s[par], r1s[par],
                                   ALU.subtract, ALU.mult), "VV")
        vv_zn = bk.cnt["VV"]
        bk.gset("z0f", "VV")

        @block.tensor
        def _(t):
            bk.w(t, "VV", vv_zn)
            bk.gwait(t, ("pbf", par))
            for mc in range(NMC):
                ins = t.transpose(pbf[par][:, mc * P:(mc + 1) * P],
                                  zbf[:, mc * P:(mc + 1) * P], c["ident"])
            bk.inc(ins, "TV")
        tv_tr = bk.cnt["TV"]
        bk.gset(("zbf", par), "TV")

        @block.vector
        def _(v):
            bk.w(v, "TV", tv_tr)
            bk.gwait(v, ("znT", par))
            for mc in range(NMC):
                ins = v.tensor_scalar(znT[par][:, mc * P:(mc + 1) * P],
                                      pbf[par][:, mc * P:(mc + 1) * P],
                                      lng[:, mc:mc + 1], lnb[:, mc:mc + 1],
                                      ALU.mult, ALU.add)
            bk.inc(ins, "VV")
        vv_znT = bk.cnt["VV"]
        bk.gset(("pbf", par), "VV")

        for fc in range(8):
            bidx = fc % 2
            pk = pf[bidx]

            @block.tensor
            def _(t):
                if i == 0 and fc == 0:
                    bk.w(t, "WD", wd_wh)
                bk.w(t, "VV", vv_znT)
                bk.gwait(t, ("ph", bidx))
                for mc in range(NMC):
                    t.matmul(pk[:], znT[par][:, mc * P:(mc + 1) * P],
                             A[:, mc * 4096 + fc * 512: mc * 4096 + (fc + 1) * 512],
                             start=(mc == 0), stop=False)
                ins = t.matmul(pk[:], c["onesr_bf"],
                               sb["b2"][0:1, fc * 512:(fc + 1) * 512],
                               start=False, stop=True)
                bk.inc(ins, "TV")
            tv_h = bk.cnt["TV"]

            @block.scalar
            def _(s):
                bk.w(s, "TV", tv_h)
                bk.gwait(s, "hid")
                bk.gwait(s, "hid_st")
                bk.inc(s.activation(hid[:, fc * 512:(fc + 1) * 512], pk[:],
                                    AF.Silu), "SV")
            bk.gset(("ph", bidx), "SV")
        sv_hid = bk.cnt["SV"]

        @block.sync
        def _(sync):
            bk.w(sync, "SV", sv_hid)
            sync.dma_start(out=vsp[i * P:(i + 1) * P, :], in_=hid[:, 0:2048]).then_inc(
                bk.sem["ST"], 16)
            bk.cnt["ST"] += 16
            bk.inc(sync.dma_start(out=gsp[i * P:(i + 1) * P, :], in_=hid[:, 2048:4096]),
                   "ST", 16)
        st_sp = bk.cnt["ST"]

        @block.tensor
        def _(t):
            bk.gwait(t, ("ph", par))
            for mc in range(NMC):
                ins = t.matmul(pf[par][:, 0:128], wqk[:, mc * P:(mc + 1) * P],
                               znT[par][:, mc * P:(mc + 1) * P],
                               start=(mc == 0), stop=(mc == NMC - 1))
            bk.inc(ins, "TV")
        tv_qk = bk.cnt["TV"]
        bk.gset(("znT", par), "TV")

        @block.scalar
        def _(s):
            bk.w(s, "TV", tv_qk)
            bk.inc(s.activation(qkT[:, i * P:(i + 1) * P], pf[par][:, 0:128],
                                AF.Silu, bias=bqk), "SV")
        sv_qkT = bk.cnt["SV"]
        bk.gset(("ph", par), "SV")

        @block.vector
        def _(v):
            bk.w(v, "SV", sv_qkT)
            bk.inc(v.tensor_scalar(lkbuf[:, 0:128], qkT[:, i * P:(i + 1) * P],
                                   ossc[:, 6:7], ossc[:, 7:8], ALU.mult, ALU.add), "VV")
        vv_lk = bk.cnt["VV"]

        @block.tensor
        def _(t):
            bk.w(t, "VV", vv_lk)
            bk.inc(t.transpose(pbf[par][:, 896:1024], lkbuf[:, 0:128],
                               c["ident"]), "TV")
        tv_lkt = bk.cnt["TV"]

        @block.vector
        def _(v):
            bk.w(v, "TV", tv_lkt)
            bk.inc(v.tensor_copy(lkbuf[:, 128:256], pbf[par][:, 896:1024]), "VV")
        vv_lkt = bk.cnt["VV"]
        bk.gset(("pbf", par), "VV")

        @block.tensor
        def _(t):
            bk.w(t, "VV", vv_lkt)
            ins = None
            for es_ in range(4):
                ins = t.matmul(pf[2 + es_][:], lkbuf[:, 128:256],
                               hid[:, es_ * 512:(es_ + 1) * 512],
                               start=(i == 0), stop=(i == NT - 1))
            bk.inc(ins, "TV")
        tv_pkv = bk.cnt["TV"]
        bk.guard["hid"] = ("TV", tv_pkv)
        bk.guard["hid_st"] = ("ST", st_sp)

    fp1 = bk.fence()

    @block.vector
    def _(v):
        bk.wait_fence(v, fp1, ["TV"])
        ins = None
        for es_ in range(4):
            ins = v.tensor_scalar(lkv[:, es_ * 512:(es_ + 1) * 512],
                                  pf[2 + es_][:], 1.0 / 4096.0, None, ALU.mult)
        bk.inc(ins, "VV")
    vv_lkv = bk.cnt["VV"]
    bk.guard[("psim", 0)] = ("VV", vv_lkv)
    bk.guard[("pout", 0)] = ("VV", vv_lkv)
    bk.guard[("pout", 1)] = ("VV", vv_lkv)
    bk.guard[("pwo", 0)] = ("SV", bk.cnt["SV"])
    bk.guard[("pwo", 1)] = ("SV", bk.cnt["SV"])

    for gidx in range(16):
        @block.sync
        def _(sync):
            if gidx == 0:
                bk.w(sync, "VV", vv_lkv)
            bk.gwait(sync, "vgld")
            sync.dma_start(out=vg[:].rearrange("p (tt e) -> p tt e", tt=2),
                           in_=vsp[gidx * 256:(gidx + 1) * 256, :].rearrange(
                               "(tt p) e -> p tt e", p=P)).then_inc(bk.sem["LD"], 16)
            bk.cnt["LD"] += 16
            bk.inc(sync.dma_start(out=gg[:].rearrange("p (tt e) -> p tt e", tt=2),
                                  in_=gsp[gidx * 256:(gidx + 1) * 256, :].rearrange(
                                      "(tt p) e -> p tt e", p=P)), "LD", 16)
        ld_vg = bk.cnt["LD"]

        @block.vector
        def _(v):
            bk.gwait(v, "qslice")
            v.tensor_scalar(qq, qkT[:, gidx * 256:(gidx + 1) * 256],
                            ossc[:, 0:1], ossc[:, 1:2], ALU.mult, ALU.add)
            v.tensor_scalar(qk2, qkT[:, gidx * 256:(gidx + 1) * 256],
                            ossc[:, 2:3], ossc[:, 3:4], ALU.mult, ALU.add)
            bk.inc(v.tensor_scalar(lq, qkT[:, gidx * 256:(gidx + 1) * 256],
                                   ossc[:, 4:5], ossc[:, 5:6], ALU.mult, ALU.add), "VV")
        vv_q = bk.cnt["VV"]

        @block.tensor
        def _(t):
            bk.w(t, "VV", vv_q)
            bk.gwait(t, ("psim", 0))
            t.matmul(pf[4][:, 0:256], qk2[:, 0:128], qq, start=True, stop=True)
            bk.inc(t.matmul(pf[5][:, 0:256], qk2[:, 128:256], qq, start=True,
                            stop=True), "TV")
        tv_sim = bk.cnt["TV"]
        bk.gset("qslice", "TV")

        @block.vector
        def _(v):
            bk.w(v, "TV", tv_sim)
            v.tensor_tensor(simf[0], pf[4][:, 0:256], biasT[0], ALU.add)
            v.tensor_tensor(simf[1], pf[5][:, 0:256], biasT[1], ALU.add)
            v.tensor_scalar(simf[0], simf[0], 0.0, None, ALU.max)
            bk.inc(v.tensor_scalar(simf[1], simf[1], 0.0, None, ALU.max), "VV")
        vv_sim = bk.cnt["VV"]
        bk.gset(("psim", 0), "VV")

        @block.scalar
        def _(s):
            bk.w(s, "VV", vv_sim)
            s.activation(atb[0], simf[0], AF.Square)
            bk.inc(s.activation(atb[1], simf[1], AF.Square), "SV")
        sv_at = bk.cnt["SV"]

        for tt in range(2):
            i2 = gidx * 2 + tt
            par2 = i2 % 2
            xt2 = sb["xt0"] if par2 == 0 else sb["xt1"]

            @block.sync
            def _(sync):
                bk.gwait(sync, ("xt", par2))
                bk.inc(sync.dma_start(out=xt2[:], in_=xin[i2 * P:(i2 + 1) * P, :]),
                       "LD", 16)
            ld_x2 = bk.cnt["LD"]

            for es_ in range(4):
                pk = pf[2 + tt]

                @block.tensor
                def _(t):
                    if tt == 0 and es_ == 0:
                        bk.w(t, "SV", sv_at)
                        bk.w(t, "LD", ld_vg)
                    bk.gwait(t, ("pout", tt))
                    t.matmul(pk[:], atb[0][:, tt * 128:(tt + 1) * 128],
                             vg[:, es_ * 512:(es_ + 1) * 512],
                             start=True, stop=False)
                    t.matmul(pk[:], atb[1][:, tt * 128:(tt + 1) * 128],
                             vg[:, 2048 + es_ * 512:2048 + (es_ + 1) * 512],
                             start=False, stop=False)
                    ins = t.matmul(pk[:], lq[:, tt * 128:(tt + 1) * 128],
                                   lkv[:, es_ * 512:(es_ + 1) * 512],
                                   start=False, stop=True)
                    bk.inc(ins, "TV")
                tv_out = bk.cnt["TV"]

                @block.vector
                def _(v):
                    bk.w(v, "TV", tv_out)
                    bk.inc(v.tensor_tensor(outg[:, es_ * 512:(es_ + 1) * 512], pk[:],
                                           gg[:, tt * 2048 + es_ * 512:
                                              tt * 2048 + (es_ + 1) * 512],
                                           ALU.mult), "VV")
                bk.gset(("pout", tt), "VV")
            vv_og = bk.cnt["VV"]
            if tt == 1:
                bk.gset("vgld", "VV")

            for half in range(2):
                @block.tensor
                def _(t):
                    if half == 0:
                        bk.w(t, "VV", vv_og)
                    bk.gwait(t, ("pbf", half))
                    for k8 in range(8):
                        ec = half * 8 + k8
                        ins = t.transpose(pbf[half][:, k8 * 128:(k8 + 1) * 128],
                                          outg[:, ec * 128:(ec + 1) * 128], c["ident"])
                    bk.inc(ins, "TV")
                tv_tp = bk.cnt["TV"]

                @block.vector
                def _(v):
                    bk.w(v, "TV", tv_tp)
                    bk.inc(v.tensor_copy(outT[:, half * 1024:(half + 1) * 1024],
                                         pbf[half][:]), "VV")
                bk.gset(("pbf", half), "VV")
            vv_oT = bk.cnt["VV"]
            bk.gset("outg_free", "VV")

            for dh in range(2):
                pk = pf[dh]

                @block.tensor
                def _(t):
                    if dh == 0:
                        bk.w(t, "VV", vv_oT)
                    bk.gwait(t, ("pwo", dh))
                    for ec in range(16):
                        t.matmul(pk[:], outT[:, ec * 128:(ec + 1) * 128],
                                 wo[ec // 8][:, (ec % 8) * 1024 + dh * 512:
                                              (ec % 8) * 1024 + (dh + 1) * 512],
                                 start=(ec == 0), stop=False)
                    ins = t.matmul(pk[:], c["onesr_bf"],
                                   sb["b2"][0:1, 4096 + dh * 512:4096 + (dh + 1) * 512],
                                   start=False, stop=True)
                    bk.inc(ins, "TV")
            tv_wo = bk.cnt["TV"]

            @block.vector
            def _(v):
                bk.w(v, "TV", tv_wo)
                bk.w(v, "LD", ld_x2)
                bk.gwait(v, "z0f")
                bk.gwait(v, "resfree")
                v.tensor_scalar(z0f, xt2[:], sb["stats"][:, 2 * i2:2 * i2 + 1],
                                sb["stats"][:, 2 * i2 + 1:2 * i2 + 2],
                                ALU.subtract, ALU.mult)
                v.tensor_tensor(res[:, 0:512], pf[0][:], z0f[:, 0:512], ALU.add)
                v.tensor_tensor(res[:, 512:1024], pf[1][:], z0f[:, 512:1024], ALU.add)
                v.tensor_tensor(res[:, 0:512], res[:, 0:512], xt2[:, 0:512], ALU.add)
                bk.inc(v.tensor_tensor(res[:, 512:1024], res[:, 512:1024],
                                       xt2[:, 512:1024], ALU.add), "VV")
            vv_res = bk.cnt["VV"]
            bk.gset(("xt", par2), "VV")
            bk.gset(("pwo", 0), "VV")
            bk.gset(("pwo", 1), "VV")
            bk.gset("z0f", "VV")

            @block.sync
            def _(sync):
                bk.w(sync, "VV", vv_res)
                bk.inc(sync.dma_start(out=xout[i2 * P:(i2 + 1) * P, :], in_=res),
                       "ST", 16)
            bk.gset("resfree", "ST")



def emit_ffn1(bk, block, io):
    c = io["c"]
    sb = io["sb"]
    pbf = io["pbf"]
    pf = io["pf"]
    xin, xout = io["xin"], io["xout"]

    A = sb["hsb"]
    znT = sb["lnT"][:, 0:4096]
    smT = [sb["lnT"][:, 0:4096], sb["w2b0"][:, 0:4096]]
    contT = sb["lnT"][:, 4096:8192]
    spT = [sb["w1b0"], sb["w1b1"]]
    We1R = sb["w2b1"]
    We2R = sb["w1b0"]
    heT = sb["w1b1"]
    sw = sb["w2b0"][:, 4096:8192]
    xg = sb["xts"][:, 0:4096]
    dec_t = sb["xts"][:, 4096:5120]
    sqj = sb["xts"][:, 5120:6144]
    ghbc = sb["xts"][:, 6144:7168]
    bhbc = sb["xts"][:, 7168:8192]
    encb = sb["tkm"][:, 192:208]
    c2sb = sb["tkm"][:, 208:224]
    s2cb = sb["tkm"][:, 224:225]
    rb1 = sb["tkm"][:, 225:226]
    rb2 = sb["tkm"][:, 226:227]
    be1 = sb["tkm"][:, 128:192]
    be2 = sb["tkm"][:, 227:235]
    prall = sb["probs"]
    f0 = bk.fence()

    @block.gpsimd
    def _(g):
        bk.wait_fence(g, f0, ["TV", "VV", "SV", "ST"])
        bk.inc(g.dma_start(out=A[:, 0:16384].rearrange("p (m f) -> p m f", m=8),
                           in_=io["encR"]), "WD", 16)
        bk.inc(g.dma_start(out=A[:, 16384:32768].rearrange("p (f d) -> p f d", f=16),
                           in_=io["decR"]), "WD", 16)
        bk.inc(g.dma_start(out=sw[:, 0:1024].rearrange("p (f k) -> p f k", f=16),
                           in_=io["s2cR"]), "WD", 16)
        bk.inc(g.dma_start(out=sw[0:64, 1024:3072].rearrange("p (f k) -> p f k", f=16),
                           in_=io["c2sR"]), "WD", 16)
        bk.inc(g.dma_start(out=sw[0:64, 3072:3136], in_=io["rW1R"]), "WD", 16)
        bk.inc(g.dma_start(out=sw[0:64, 3136:3144], in_=io["rW2R"]), "WD", 16)
        bk.inc(g.dma_start(out=encb, in_=io["encbR"]), "WD", 16)
        bk.inc(g.dma_start(out=c2sb, in_=io["c2sbR"]), "WD", 16)
        bk.inc(g.dma_start(out=s2cb[0:64, :], in_=io["s2cbc"]), "WD", 16)
        bk.inc(g.dma_start(out=rb1[0:64, :], in_=io["rb1c"]), "WD", 16)
        bk.inc(g.dma_start(out=rb2[0:8, :], in_=io["rb2c"]), "WD", 16)
        bk.inc(g.dma_start(out=be1, in_=io["be1R"].rearrange("p e h -> p (e h)")), "WD", 16)
        bk.inc(g.dma_start(out=be2[0:64, :], in_=io["be2R"]), "WD", 16)
        bk.inc(g.dma_start(out=sb["b2"][:], in_=io["rows_s4"]), "WD", 16)
        bk.inc(g.dma_start(out=sb["rowf"][:], in_=io["rowf_s4"][:, 0:1024]), "WD", 16)
    wd_pre = bk.cnt["WD"]

    @block.tensor
    def _(t):
        bk.w(t, "WD", wd_pre)
        t.matmul(pf[4][:], c["onesr_f"], sb["rowf"][0:1, 0:512], start=True, stop=True)
        bk.inc(t.matmul(pf[5][:], c["onesr_f"], sb["rowf"][0:1, 512:1024],
                        start=True, stop=True), "TV")
    tv_g1 = bk.cnt["TV"]

    @block.vector
    def _(v):
        bk.w(v, "TV", tv_g1)
        v.tensor_scalar(ghbc[:, 0:512], pf[4][:], 0.5, None, ALU.mult)
        bk.inc(v.tensor_scalar(ghbc[:, 512:1024], pf[5][:], 0.5, None, ALU.mult), "VV")
    vv_g1 = bk.cnt["VV"]

    @block.gpsimd
    def _(g):
        bk.w(g, "VV", vv_g1)
        bk.inc(g.dma_start(out=sb["rowf"][:], in_=io["rowf_s4"][:, 1024:2048]), "WD", 16)
    wd_r2 = bk.cnt["WD"]

    @block.tensor
    def _(t):
        bk.w(t, "WD", wd_r2)
        t.matmul(pf[4][:], c["onesr_f"], sb["rowf"][0:1, 0:512], start=True, stop=True)
        bk.inc(t.matmul(pf[5][:], c["onesr_f"], sb["rowf"][0:1, 512:1024],
                        start=True, stop=True), "TV")
    tv_g2 = bk.cnt["TV"]

    @block.vector
    def _(v):
        bk.w(v, "TV", tv_g2)
        v.tensor_scalar(bhbc[:, 0:512], pf[4][:], 0.5, None, ALU.mult)
        bk.inc(v.tensor_scalar(bhbc[:, 512:1024], pf[5][:], 0.5, None, ALU.mult), "VV")
    vv_g2 = bk.cnt["VV"]
    bk.guard[("p45",)] = ("VV", vv_g2)

    for gi in range(8):
        @block.sync
        def _(sync):
            if gi == 0:
                bk.wait_fence(sync, f0, ["TV", "VV", "SV", "ST"])
            bk.gwait(sync, "xg")
            bk.inc(sync.dma_start(
                out=xg.rearrange("p (tt d) -> p tt d", tt=4),
                in_=xin[gi * 512:(gi + 1) * 512, :].rearrange("(tt p) d -> p tt d", p=P)),
                "LD", 16)
        ld_xg = bk.cnt["LD"]

        for tt in range(4):
            i = gi * 4 + tt
            par = tt % 2
            zbf = sb["zbf0"] if par == 0 else sb["zbf1"]
            xtile = xg[:, tt * D:(tt + 1) * D]

            @block.scalar
            def _(s):
                bk.w(s, "LD", ld_xg)
                bk.gwait(s, "sqj")
                bk.inc(s.activation(sqj, xtile, AF.Square,
                                    accum_out=sb["rsm"][:, par:par + 1]), "SV")
            sv_sq = bk.cnt["SV"]

            @block.vector
            def _(v):
                bk.w(v, "LD", ld_xg)
                v.reduce_sum(sb["rsm"][:, 2 + par:3 + par], xtile, axis=AX.X)
                bk.w(v, "SV", sv_sq)
                v.tensor_scalar(sb["rsm"][:, 4 + par:5 + par], sb["rsm"][:, par:par + 1],
                                INV_D, None, ALU.mult)
                v.tensor_scalar(sb["rsm"][:, 6 + par:7 + par],
                                sb["rsm"][:, 2 + par:3 + par],
                                sb["rsm"][:, 2 + par:3 + par], INV_D * INV_D,
                                ALU.mult, ALU.mult)
                v.tensor_scalar(sb["stats"][:, 2 * i:2 * i + 1],
                                sb["rsm"][:, 2 + par:3 + par], INV_D, None, ALU.mult)
                bk.inc(v.tensor_tensor(sb["rsm"][:, 4 + par:5 + par],
                                       sb["rsm"][:, 4 + par:5 + par],
                                       sb["rsm"][:, 6 + par:7 + par], ALU.subtract), "VV")
            vv_var = bk.cnt["VV"]

            @block.scalar
            def _(s):
                bk.w(s, "VV", vv_var)
                s.activation(sb["rsm"][:, 8 + par:9 + par], sb["rsm"][:, 4 + par:5 + par],
                             AF.Ln, bias=float(EPS))
                bk.nc.scalar.nop()
                bk.inc(s.activation(sb["stats"][:, 2 * i + 1:2 * i + 2],
                                    sb["rsm"][:, 8 + par:9 + par], AF.Exp, scale=-0.5), "SV")
            sv_r = bk.cnt["SV"]

            @block.vector
            def _(v):
                bk.w(v, "SV", sv_r)
                bk.gwait(v, ("zbf", par))
                bk.inc(v.tensor_scalar(zbf[:], xtile, sb["stats"][:, 2 * i:2 * i + 1],
                                       sb["stats"][:, 2 * i + 1:2 * i + 2],
                                       ALU.subtract, ALU.mult), "VV")
            vv_z = bk.cnt["VV"]
            bk.guard["sqj"] = ("VV", vv_z)

            @block.tensor
            def _(t):
                bk.w(t, "VV", vv_z)
                bk.gwait(t, ("pbf", par))
                for mc in range(NMC):
                    ins = t.transpose(pbf[par][:, mc * P:(mc + 1) * P],
                                      zbf[:, mc * P:(mc + 1) * P], c["ident"])
                bk.inc(ins, "TV")
            tv_tr = bk.cnt["TV"]
            bk.gset(("zbf", par), "TV")

            @block.vector
            def _(v):
                bk.w(v, "TV", tv_tr)
                bk.gwait(v, ("znT", 0))
                for mc in range(NMC):
                    ins = v.tensor_copy(znT[:, mc * 512 + tt * P: mc * 512 + (tt + 1) * P],
                                        pbf[par][:, mc * P:(mc + 1) * P])
                bk.inc(ins, "VV")
            bk.gset(("pbf", par), "VV")
        vv_znT = bk.cnt["VV"]
        bk.guard["xg_znT"] = ("VV", vv_znT)

        for fch in range(16):
            bidx = fch % 2
            pk = pf[bidx]

            @block.tensor
            def _(t):
                if gi == 0 and fch == 0:
                    bk.w(t, "WD", wd_pre)
                bk.w(t, "VV", vv_znT)
                bk.gwait(t, ("ph", bidx))
                for mc in range(NMC):
                    ins = t.matmul(pk[:], A[:, mc * 2048 + fch * P: mc * 2048 + (fch + 1) * P],
                                   znT[:, mc * 512:(mc + 1) * 512],
                                   start=(mc == 0), stop=(mc == NMC - 1))
                bk.inc(ins, "TV")
            tv_e = bk.cnt["TV"]

            @block.scalar
            def _(s):
                bk.w(s, "TV", tv_e)
                bk.gwait(s, ("spT", fch % 2))
                bk.inc(s.activation(spT[fch // 8][:, (fch % 8) * 512:(fch % 8 + 1) * 512],
                                    pk[:], AF.Sigmoid,
                                    bias=encb[:, fch:fch + 1]), "SV")
            bk.gset(("ph", bidx), "SV")
        sv_sp = bk.cnt["SV"]

        @block.tensor
        def _(t):
            bk.w(t, "SV", sv_sp)
            bk.gwait(t, ("pc",))
            ins = None
            for fch in range(16):
                ins = t.matmul(pf[2][0:64, 0:512], sw[:, fch * 64:(fch + 1) * 64],
                               spT[fch // 8][:, (fch % 8) * 512:(fch % 8 + 1) * 512],
                               start=(fch == 0), stop=(fch == 15))
            bk.inc(ins, "TV")
        tv_c = bk.cnt["TV"]
        bk.guard[("spT", 0)] = ("TV", tv_c)
        bk.guard[("spT", 1)] = ("TV", tv_c)

        @block.scalar
        def _(s):
            bk.w(s, "TV", tv_c)
            bk.inc(s.activation(contT[0:64, gi * 512:(gi + 1) * 512], pf[2][0:64, 0:512],
                                AF.Identity, bias=s2cb[0:64, :]), "SV")
        sv_ct = bk.cnt["SV"]
        bk.gset(("pc",), "SV")

        @block.tensor
        def _(t):
            bk.w(t, "SV", sv_ct)
            bk.gwait(t, ("ph1",))
            bk.inc(t.matmul(pf[3][0:64, 0:512], sw[0:64, 3072:3136],
                            contT[0:64, gi * 512:(gi + 1) * 512],
                            start=True, stop=True), "TV")
        tv_h1 = bk.cnt["TV"]

        @block.scalar
        def _(s):
            bk.w(s, "TV", tv_h1)
            bk.inc(s.activation(sb["zbf0"][0:64, 0:512], pf[3][0:64, 0:512],
                                AF.Tanh, bias=rb1[0:64, :]), "SV")
        sv_h1 = bk.cnt["SV"]
        bk.gset(("ph1",), "SV")

        @block.tensor
        def _(t):
            bk.w(t, "SV", sv_h1)
            bk.gwait(t, ("plg",))
            bk.inc(t.matmul(pf[3][0:8, 0:512], sw[0:64, 3136:3144],
                            sb["zbf0"][0:64, 0:512], start=True, stop=True), "TV")
        tv_lg = bk.cnt["TV"]

        @block.scalar
        def _(s):
            bk.w(s, "TV", tv_lg)
            bk.inc(s.activation(sb["zbf1"][0:8, 0:512], pf[3][0:8, 0:512],
                                AF.Identity, bias=rb2[0:8, :]), "SV")
        sv_lg = bk.cnt["SV"]
        bk.gset(("plg",), "SV")

        @block.tensor
        def _(t):
            bk.w(t, "SV", sv_lg)
            bk.gwait(t, ("pbf", 0))
            for tt in range(4):
                ins = t.transpose(pbf[0][:, tt * 8:(tt + 1) * 8],
                                  sb["zbf1"][0:8, tt * 128:(tt + 1) * 128],
                                  c["ident"][0:8, 0:8])
            bk.inc(ins, "TV")
        tv_lt = bk.cnt["TV"]

        @block.vector
        def _(v):
            bk.w(v, "TV", tv_lt)
            ins = None
            for tt in range(4):
                i = gi * 4 + tt
                ins = v.tensor_copy(prall[:, 8 * i:8 * i + 8], pbf[0][:, tt * 8:(tt + 1) * 8])
            bk.inc(ins, "VV")
        bk.gset(("pbf", 0), "VV")
        bk.gset("xg", "VV")
    fA = bk.fence()

    @block.vector
    def _(v):
        bk.wait_fence(v, fA, ["VV", "SV"])
        for i in range(NT):
            v.tensor_reduce(sb["tkm"][:, i:i + 1], prall[:, 8 * i:8 * i + 8],
                            axis=AX.X, op=ALU.max, negate=True)
        bk.inc(bk.nc.vector.nop(), "VV")
    vv_nm = bk.cnt["VV"]

    @block.scalar
    def _(s):
        bk.w(s, "VV", vv_nm)
        ins = None
        for i in range(NT):
            ins = s.activation(prall[:, 8 * i:8 * i + 8], prall[:, 8 * i:8 * i + 8],
                               AF.Exp, bias=sb["tkm"][:, i:i + 1])
        bk.inc(ins, "SV")
    sv_ex = bk.cnt["SV"]

    @block.vector
    def _(v):
        bk.w(v, "SV", sv_ex)
        pr = prall
        m1 = sb["tkm"][:, 0:32]
        i1 = sb["tkm"][:, 32:64]
        m2 = sb["tkm"][:, 64:96]
        i2 = sb["tkm"][:, 96:128]
        rh = sb["tkm"][:, 235:236]
        eq = sb["tkb"]
        e9 = sb["tkc"]
        for i in range(NT):
            v.reduce_max(m1[:, i:i + 1], pr[:, 8 * i:8 * i + 8], axis=AX.X)
        for i in range(NT):
            v.tensor_scalar(eq[:, 8 * i:8 * i + 8], pr[:, 8 * i:8 * i + 8],
                            m1[:, i:i + 1], None, ALU.is_equal)
        for i in range(NT):
            v.scalar_tensor_tensor(e9[:, 8 * i:8 * i + 8], eq[:, 8 * i:8 * i + 8],
                                   999.0, c["iota8"], ALU.mult, ALU.subtract)
        for i in range(NT):
            v.reduce_max(i1[:, i:i + 1], e9[:, 8 * i:8 * i + 8], axis=AX.X)
        for i in range(NT):
            v.tensor_scalar(i1[:, i:i + 1], i1[:, i:i + 1], -1.0, 999.0, ALU.mult, ALU.add)
        for i in range(NT):
            v.tensor_scalar(sb["oh1"][:, 8 * i:8 * i + 8], c["iota8"], i1[:, i:i + 1],
                            None, ALU.is_equal)
        for i in range(NT):
            v.scalar_tensor_tensor(eq[:, 8 * i:8 * i + 8], sb["oh1"][:, 8 * i:8 * i + 8],
                                   -1e30, pr[:, 8 * i:8 * i + 8], ALU.mult, ALU.add)
        for i in range(NT):
            v.reduce_max(m2[:, i:i + 1], eq[:, 8 * i:8 * i + 8], axis=AX.X)
        for i in range(NT):
            v.tensor_scalar(e9[:, 8 * i:8 * i + 8], eq[:, 8 * i:8 * i + 8],
                            m2[:, i:i + 1], None, ALU.is_equal)
        for i in range(NT):
            v.scalar_tensor_tensor(e9[:, 8 * i:8 * i + 8], e9[:, 8 * i:8 * i + 8],
                                   999.0, c["iota8"], ALU.mult, ALU.subtract)
        for i in range(NT):
            v.reduce_max(i2[:, i:i + 1], e9[:, 8 * i:8 * i + 8], axis=AX.X)
        for i in range(NT):
            v.tensor_scalar(i2[:, i:i + 1], i2[:, i:i + 1], -1.0, 999.0, ALU.mult, ALU.add)
        for i in range(NT):
            v.tensor_scalar(sb["oh2"][:, 8 * i:8 * i + 8], c["iota8"], i2[:, i:i + 1],
                            None, ALU.is_equal)
        for i in range(NT):
            v.tensor_tensor(sb["tks"][:, i:i + 1], m1[:, i:i + 1], m2[:, i:i + 1], ALU.add)
        for i in range(NT):
            v.reciprocal(sb["tks"][:, 32 + i:33 + i], sb["tks"][:, i:i + 1])
        for i in range(NT):
            v.tensor_scalar(sb["wh"][:, 2 * i:2 * i + 1], m1[:, i:i + 1],
                            sb["tks"][:, 32 + i:33 + i], None, ALU.mult)
        for i in range(NT):
            v.tensor_scalar(sb["wh"][:, 2 * i + 1:2 * i + 2], m2[:, i:i + 1],
                            sb["tks"][:, 32 + i:33 + i], None, ALU.mult)
        for i in range(NT):
            v.tensor_scalar(sb["wcoef"][:, 8 * i:8 * i + 8], sb["oh1"][:, 8 * i:8 * i + 8],
                            sb["wh"][:, 2 * i:2 * i + 1], None, ALU.mult)
        ins = None
        for i in range(NT):
            ins = v.scalar_tensor_tensor(sb["wcoef"][:, 8 * i:8 * i + 8],
                                         sb["oh2"][:, 8 * i:8 * i + 8],
                                         sb["wh"][:, 2 * i + 1:2 * i + 2],
                                         sb["wcoef"][:, 8 * i:8 * i + 8],
                                         ALU.mult, ALU.add)
        bk.inc(ins, "VV")
    vv_gates = bk.cnt["VV"]

    @block.gpsimd
    def _(g):
        bk.w(g, "VV", vv_gates)
        bk.gwait(g, ("spT", 0))
        bk.gwait(g, ("spT", 1))
        bk.inc(g.dma_start(out=We1R[0:64, :].rearrange("p (e h f) -> p e h f", e=8, h=8),
                           in_=io["We1R"]), "WD", 16)
        bk.inc(g.dma_start(out=We2R[:].rearrange("p (e h k) -> p e h k", e=8, h=8),
                           in_=io["We2R"]), "WD", 16)
    wd_we = bk.cnt["WD"]

    for gi in range(8):
        @block.sync
        def _(sync):
            bk.gwait(sync, "xg")
            bk.inc(sync.dma_start(
                out=xg.rearrange("p (tt d) -> p tt d", tt=4),
                in_=xin[gi * 512:(gi + 1) * 512, :].rearrange("(tt p) d -> p tt d", p=P)),
                "LD", 16)
        ld_xg = bk.cnt["LD"]

        @block.vector
        def _(v):
            bk.gwait(v, "ofacc")
            bk.inc(v.memset(sb["tkb"][:, 0:256], 0.0), "VV")
        vv_of0 = bk.cnt["VV"]

        for e in range(8):
            for ch in range(8):
                bidx = ch % 2
                pk = pf[bidx]

                @block.tensor
                def _(t):
                    if gi == 0 and e == 0 and ch == 0:
                        bk.w(t, "WD", wd_we)
                        bk.w(t, "VV", vv_of0)
                    bk.gwait(t, ("ph", bidx))
                    bk.inc(t.matmul(pk[:], We1R[0:64, (e * 8 + ch) * 128:(e * 8 + ch + 1) * 128],
                                    contT[0:64, gi * 512:(gi + 1) * 512],
                                    start=True, stop=True), "TV")
                tv_he = bk.cnt["TV"]

                @block.scalar
                def _(s):
                    bk.w(s, "TV", tv_he)
                    bk.gwait(s, ("heT", ch % 2))
                    bk.inc(s.activation(heT[:, ch * 512:(ch + 1) * 512], pk[:],
                                        AF.Relu, bias=be1[:, e * 8 + ch:e * 8 + ch + 1]),
                           "SV")
                bk.gset(("ph", bidx), "SV")
            sv_he = bk.cnt["SV"]

            @block.tensor
            def _(t):
                bk.w(t, "SV", sv_he)
                bk.gwait(t, ("pc",))
                ins = None
                for ch in range(8):
                    ins = t.matmul(pf[2][0:64, 0:512],
                                   We2R[:, (e * 8 + ch) * 64:(e * 8 + ch + 1) * 64],
                                   heT[:, ch * 512:(ch + 1) * 512],
                                   start=(ch == 0), stop=(ch == 7))
                bk.inc(ins, "TV")
            tv_eo = bk.cnt["TV"]
            bk.guard[("heT", 0)] = ("TV", tv_eo)
            bk.guard[("heT", 1)] = ("TV", tv_eo)

            @block.scalar
            def _(s):
                bk.w(s, "TV", tv_eo)
                bk.gwait(s, ("eobf",))
                bk.inc(s.activation(sb["zbf0"][0:64, 0:512], pf[2][0:64, 0:512],
                                    AF.Identity, bias=be2[0:64, e:e + 1]), "SV")
            sv_eob = bk.cnt["SV"]
            bk.gset(("pc",), "SV")

            @block.tensor
            def _(t):
                bk.w(t, "SV", sv_eob)
                bk.gwait(t, ("pbf", 0))
                for tt in range(4):
                    ins = t.transpose(pbf[0][:, tt * 64:(tt + 1) * 64],
                                      sb["zbf0"][0:64, tt * 128:(tt + 1) * 128],
                                      c["ident"][0:64, 0:64])
                bk.inc(ins, "TV")
            tv_eot = bk.cnt["TV"]
            bk.gset(("eobf",), "TV")

            @block.vector
            def _(v):
                bk.w(v, "TV", tv_eot)
                ins = None
                for tt in range(4):
                    i = gi * 4 + tt
                    dst = sb["tkb"][:, tt * 64:(tt + 1) * 64]
                    ins = v.scalar_tensor_tensor(dst, pbf[0][:, tt * 64:(tt + 1) * 64],
                                                 sb["wcoef"][:, 8 * i + e:8 * i + e + 1],
                                                 dst, ALU.mult, ALU.add)
                bk.inc(ins, "VV")
            bk.gset(("pbf", 0), "VV")
        vv_of = bk.cnt["VV"]

        @block.vector
        def _(v):
            v.tensor_copy(sb["tkt"][:, 0:1], c["iota8"][:, 0:1])
            v.tensor_copy(sb["tkt"][:, 1:2], c["iota8"][:, 0:1])
            bk.inc(v.tensor_copy(sb["zbf1"][:, 0:256], sb["tkb"][:, 0:256]), "VV")
        vv_ofb = bk.cnt["VV"]
        bk.gset("ofacc", "VV")

        @block.tensor
        def _(t):
            bk.w(t, "VV", vv_ofb)
            bk.gwait(t, ("pbf", 1))
            for tt in range(4):
                ins = t.transpose(pbf[1][0:64, tt * 128:(tt + 1) * 128],
                                  sb["zbf1"][:, tt * 64:(tt + 1) * 64], c["ident"])
            bk.inc(ins, "TV")
        tv_oft = bk.cnt["TV"]

        @block.vector
        def _(v):
            bk.w(v, "TV", tv_oft)
            ins = None
            for tt in range(4):
                ins = v.tensor_copy(sb["zbf0"][0:64, tt * 128:(tt + 1) * 128],
                                    pbf[1][0:64, tt * 128:(tt + 1) * 128])
            bk.inc(ins, "VV")
        vv_ofT = bk.cnt["VV"]
        bk.gset(("pbf", 1), "VV")

        for fch in range(16):
            bidx = fch % 2
            pk = pf[bidx]

            @block.tensor
            def _(t):
                if fch == 0:
                    bk.w(t, "VV", vv_ofT)
                bk.gwait(t, ("ph", bidx))
                bk.inc(t.matmul(pk[:], sw[0:64, 1024 + fch * 128:1024 + (fch + 1) * 128],
                                sb["zbf0"][0:64, 0:512], start=True, stop=True), "TV")
            tv_sm = bk.cnt["TV"]

            @block.scalar
            def _(s):
                bk.w(s, "TV", tv_sm)
                bk.gwait(s, ("smT", fch % 2))
                bk.inc(s.activation(smT[fch // 8][:, (fch % 8) * 512:(fch % 8 + 1) * 512],
                                    pk[:], AF.Sigmoid, bias=c2sb[:, fch:fch + 1]), "SV")
            bk.gset(("ph", bidx), "SV")
        sv_smT = bk.cnt["SV"]

        for tt in range(4):
            i = gi * 4 + tt
            for dh in range(2):
                pk = pf[2 + dh]

                @block.tensor
                def _(t):
                    if tt == 0 and dh == 0:
                        bk.w(t, "SV", sv_smT)
                    bk.gwait(t, ("pd", dh))
                    for fch in range(16):
                        t.matmul(pk[:], smT[fch // 8][:, (fch % 8) * 512 + tt * 128:
                                                      (fch % 8) * 512 + (tt + 1) * 128],
                                 A[:, 16384 + fch * 1024 + dh * 512:
                                   16384 + fch * 1024 + (dh + 1) * 512],
                                 start=(fch == 0), stop=False)
                    ins = t.matmul(pk[:], c["onesr_bf"],
                                   sb["b2"][0:1, dh * 512:(dh + 1) * 512],
                                   start=False, stop=True)
                    bk.inc(ins, "TV")
                tv_d = bk.cnt["TV"]

                @block.scalar
                def _(s):
                    bk.w(s, "TV", tv_d)
                    bk.gwait(s, ("dec_t",))
                    bk.inc(s.activation(dec_t[:, dh * 512:(dh + 1) * 512], pk[:],
                                        AF.Sigmoid), "SV")
                bk.gset(("pd", dh), "SV")
            sv_dec = bk.cnt["SV"]

            par = tt % 2

            @block.scalar
            def _(s):
                bk.w(s, "SV", sv_dec)
                bk.gwait(s, "sqj")
                bk.inc(s.activation(sqj, dec_t, AF.Square,
                                    accum_out=sb["rsm"][:, par:par + 1]), "SV")
            sv_sq = bk.cnt["SV"]

            @block.vector
            def _(v):
                bk.w(v, "SV", sv_sq)
                v.reduce_sum(sb["rsm"][:, 2 + par:3 + par], dec_t, axis=AX.X)
                v.tensor_scalar(sb["rsm"][:, 4 + par:5 + par], sb["rsm"][:, par:par + 1],
                                INV_D, None, ALU.mult)
                v.tensor_scalar(sb["rsm"][:, 6 + par:7 + par],
                                sb["rsm"][:, 2 + par:3 + par],
                                sb["rsm"][:, 2 + par:3 + par], INV_D * INV_D,
                                ALU.mult, ALU.mult)
                v.tensor_scalar(sb["rsm"][:, 10 + par:11 + par],
                                sb["rsm"][:, 2 + par:3 + par], INV_D, None, ALU.mult)
                bk.inc(v.tensor_tensor(sb["rsm"][:, 4 + par:5 + par],
                                       sb["rsm"][:, 4 + par:5 + par],
                                       sb["rsm"][:, 6 + par:7 + par], ALU.subtract), "VV")
            vv_var = bk.cnt["VV"]

            @block.scalar
            def _(s):
                bk.w(s, "VV", vv_var)
                s.activation(sb["rsm"][:, 8 + par:9 + par], sb["rsm"][:, 4 + par:5 + par],
                             AF.Ln, bias=float(EPS))
                bk.nc.scalar.nop()
                bk.inc(s.activation(sb["rsm"][:, 12 + par:13 + par],
                                    sb["rsm"][:, 8 + par:9 + par], AF.Exp, scale=-0.5), "SV")
            sv_r = bk.cnt["SV"]

            @block.vector
            def _(v):
                bk.w(v, "SV", sv_r)
                bk.w(v, "LD", ld_xg)
                v.tensor_scalar(dec_t[:, 0:512], dec_t[:, 0:512],
                                sb["rsm"][:, 10 + par:11 + par],
                                sb["rsm"][:, 12 + par:13 + par], ALU.subtract, ALU.mult)
                v.tensor_scalar(dec_t[:, 512:1024], dec_t[:, 512:1024],
                                sb["rsm"][:, 10 + par:11 + par],
                                sb["rsm"][:, 12 + par:13 + par], ALU.subtract, ALU.mult)
                v.tensor_tensor(dec_t[:, 0:512], dec_t[:, 0:512], ghbc[:, 0:512], ALU.mult)
                v.tensor_tensor(dec_t[:, 512:1024], dec_t[:, 512:1024], ghbc[:, 512:1024],
                                ALU.mult)
                v.tensor_tensor(dec_t[:, 0:512], dec_t[:, 0:512], bhbc[:, 0:512], ALU.add)
                v.tensor_tensor(dec_t[:, 512:1024], dec_t[:, 512:1024], bhbc[:, 512:1024],
                                ALU.add)
                xsl = xg[:, tt * D:(tt + 1) * D]
                v.tensor_tensor(xsl[:, 0:512], xsl[:, 0:512], dec_t[:, 0:512], ALU.add)
                bk.inc(v.tensor_tensor(xsl[:, 512:1024], xsl[:, 512:1024],
                                       dec_t[:, 512:1024], ALU.add), "VV")
            vv_res = bk.cnt["VV"]
            bk.gset(("dec_t",), "VV")
            bk.guard["sqj"] = ("VV", vv_res)

        @block.sync
        def _(sync):
            bk.w(sync, "VV")
            bk.inc(sync.dma_start(
                out=xout[gi * 512:(gi + 1) * 512, :].rearrange("(tt p) d -> p tt d", p=P),
                in_=xg.rearrange("p (tt d) -> p tt d", tt=4)), "ST", 16)
        bk.gset("xg", "ST")



def emit_attn1(bk, block, io):
    c = io["c"]
    sb = io["sb"]
    pbf = io["pbf"]
    pf = io["pf"]
    xin, xout = io["xin"], io["xout"]
    QKSCALE = float(1.0 / np.sqrt(2.0 * 1024.0))
    SHIFT = 0.0

    A = sb["hsb"]
    qkvT = A[:, 0:12288]
    tincl = A[:, 12288:12416]
    texcl = A[:, 12416:12544]
    QR1 = A[:, 12544:32768]
    QR2 = sb["w2b0"][:, 0:4352]
    znT = sb["lnT"][:, 0:4096]
    qs = sb["lnT"][:, 4096:8192]
    ks = sb["w2b1"][:, 0:4096]
    Vb = sb["w1b0"][:, 0:2080]
    PT = sb["w1b1"]
    otb = sb["zbf0"][0:65, 0:128]
    xg = sb["xts"][:, 0:4096]
    res = sb["xts"][:, 4096:5120]
    sqj = sb["xts"][:, 5120:6144]
    qkvb = sb["tkm"][:, 192:216]
    recs = sb["tkm"][:, 0:64]

    def qr_slice(fch, mc):
        g = fch * 8 + mc
        if g < 158:
            return QR1[:, g * 128:(g + 1) * 128]
        return QR2[:, (g - 158) * 128:(g - 157) * 128]

    def slot(j):
        return j // 2 + 8 * (j % 2)

    def q_slice(hh, j, prange):
        s = slot(j)
        return qs[0:64, s * 256 + prange[0]: s * 256 + prange[1]]

    def k_slice(hh, j, prange):
        s = slot(j)
        return ks[0:64, s * 256 + prange[0]: s * 256 + prange[1]]

    def v_src(hh, j, prange):
        ch, off = (3 * j + 2) // 2, 64 * (j % 2)
        base = hh * 256
        return (qkvT[off:off + 64, ch * 512 + base + prange[0]: ch * 512 + base + prange[1]],
                off)

    f0 = bk.fence()

    @block.gpsimd
    def _(g):
        bk.wait_fence(g, f0, ["TV", "VV", "SV", "ST"])
        bk.inc(g.dma_start(out=QR1[:].rearrange("p (g f) -> p g f", f=128),
                           in_=io["QR"][:, 0:158]), "WD", 16)
        bk.inc(g.dma_start(out=QR2[:].rearrange("p (g f) -> p g f", f=128),
                           in_=io["QR"][:, 158:192]), "WD", 16)
        bk.inc(g.dma_start(out=tincl, in_=io["tincl"]), "WD", 16)
        bk.inc(g.dma_start(out=texcl, in_=io["texcl"]), "WD", 16)
        bk.inc(g.dma_start(out=qkvb, in_=io["qkvbR"]), "WD", 16)
    wd_pre = bk.cnt["WD"]

    for hp in range(8):
        @block.sync
        def _(sync):
            if hp == 0:
                bk.wait_fence(sync, f0, ["TV", "VV", "SV", "ST"])
            bk.gwait(sync, "xg")
            bk.inc(sync.dma_start(
                out=xg.rearrange("p (tt d) -> p tt d", tt=4),
                in_=xin[hp * 512:(hp + 1) * 512, :].rearrange("(tt p) d -> p tt d", p=P)),
                "LD", 16)
        ld_xg = bk.cnt["LD"]

        for tt in range(4):
            i = hp * 4 + tt
            par = tt % 2
            zbf = sb["zbf1"]
            xtile = xg[:, tt * D:(tt + 1) * D]

            @block.scalar
            def _(s):
                bk.w(s, "LD", ld_xg)
                bk.gwait(s, "sqj")
                bk.inc(s.activation(sqj, xtile, AF.Square,
                                    accum_out=sb["rsm"][:, par:par + 1]), "SV")
            sv_sq = bk.cnt["SV"]

            @block.vector
            def _(v):
                bk.w(v, "LD", ld_xg)
                v.reduce_sum(sb["rsm"][:, 2 + par:3 + par], xtile, axis=AX.X)
                bk.w(v, "SV", sv_sq)
                v.tensor_scalar(sb["rsm"][:, 4 + par:5 + par], sb["rsm"][:, par:par + 1],
                                INV_D, None, ALU.mult)
                v.tensor_scalar(sb["rsm"][:, 6 + par:7 + par],
                                sb["rsm"][:, 2 + par:3 + par],
                                sb["rsm"][:, 2 + par:3 + par], INV_D * INV_D,
                                ALU.mult, ALU.mult)
                v.tensor_scalar(sb["stats"][:, 2 * i:2 * i + 1],
                                sb["rsm"][:, 2 + par:3 + par], INV_D, None, ALU.mult)
                bk.inc(v.tensor_tensor(sb["rsm"][:, 4 + par:5 + par],
                                       sb["rsm"][:, 4 + par:5 + par],
                                       sb["rsm"][:, 6 + par:7 + par], ALU.subtract), "VV")
            vv_var = bk.cnt["VV"]

            @block.scalar
            def _(s):
                bk.w(s, "VV", vv_var)
                s.activation(sb["rsm"][:, 8 + par:9 + par], sb["rsm"][:, 4 + par:5 + par],
                             AF.Ln, bias=float(EPS))
                bk.nc.scalar.nop()
                bk.inc(s.activation(sb["rsm"][:, 12 + par:13 + par],
                                    sb["rsm"][:, 8 + par:9 + par], AF.Exp, scale=-0.5), "SV")
            sv_r = bk.cnt["SV"]

            @block.vector
            def _(v):
                bk.w(v, "SV", sv_r)
                bk.gwait(v, ("zbf5",))
                bk.inc(v.tensor_scalar(zbf[:], xtile, sb["stats"][:, 2 * i:2 * i + 1],
                                       sb["rsm"][:, 12 + par:13 + par],
                                       ALU.subtract, ALU.mult), "VV")
            vv_z = bk.cnt["VV"]
            bk.guard["sqj"] = ("VV", vv_z)

            @block.tensor
            def _(t):
                bk.w(t, "VV", vv_z)
                bk.gwait(t, ("pbf", par))
                for mc in range(NMC):
                    ins = t.transpose(pbf[par][:, mc * P:(mc + 1) * P],
                                      zbf[:, mc * P:(mc + 1) * P], c["ident"])
                bk.inc(ins, "TV")
            tv_tr = bk.cnt["TV"]
            bk.gset(("zbf5",), "TV")

            @block.vector
            def _(v):
                bk.w(v, "TV", tv_tr)
                bk.gwait(v, ("znT5",))
                for mc in range(NMC):
                    ins = v.tensor_copy(znT[:, mc * 512 + tt * P: mc * 512 + (tt + 1) * P],
                                        pbf[par][:, mc * P:(mc + 1) * P])
                bk.inc(ins, "VV")
            bk.gset(("pbf", par), "VV")
        vv_znT = bk.cnt["VV"]
        bk.gset("xg_ln", "VV")

        for fch in range(24):
            bidx = fch % 2
            pk = pf[bidx]

            @block.tensor
            def _(t):
                if hp == 0 and fch == 0:
                    bk.w(t, "WD", wd_pre)
                bk.w(t, "VV", vv_znT)
                bk.gwait(t, ("ph", bidx))
                for mc in range(NMC):
                    ins = t.matmul(pk[:], qr_slice(fch, mc), znT[:, mc * 512:(mc + 1) * 512],
                                   start=(mc == 0), stop=(mc == NMC - 1))
                bk.inc(ins, "TV")
            tv_q = bk.cnt["TV"]

            @block.scalar
            def _(s):
                bk.w(s, "TV", tv_q)
                bk.gwait(s, ("qkvT",))
                bk.inc(s.activation(qkvT[:, fch * 512:(fch + 1) * 512], pk[:],
                                    AF.Identity, bias=qkvb[:, fch:fch + 1]), "SV")
            bk.gset(("ph", bidx), "SV")
        sv_qkv = bk.cnt["SV"]

        for hh in range(2):
            h = hp * 2 + hh
            @block.sync
            def _(sync):
                bk.w(sync, "SV", sv_qkv)
                bk.gwait(sync, ("qks",))
                base = hh * 256
                sync.dma_start(
                    out=qs[0:64, 0:2048].rearrange("p (s t) -> p s t", s=8),
                    in_=qkvT[0:64, base:base + 256].rearrange(
                        "p t -> p t")[:, None, :].rearrange("p q t -> p q t")
                    if False else
                    qkvT[0:64, :].rearrange("p (ch t) -> p ch t", ch=24)[
                        :, 0:24:3, base:base + 256]).then_inc(bk.sem["LD"], 16)
                bk.cnt["LD"] += 16
                sync.dma_start(
                    out=qs[0:64, 2048:4096].rearrange("p (s t) -> p s t", s=8),
                    in_=qkvT[64:128, :].rearrange("p (ch t) -> p ch t", ch=24)[
                        :, 1:24:3, base:base + 256]).then_inc(bk.sem["LD"], 16)
                bk.cnt["LD"] += 16
                sync.dma_start(
                    out=ks[0:64, 0:2048].rearrange("p (s t) -> p s t", s=8),
                    in_=qkvT[64:128, :].rearrange("p (ch t) -> p ch t", ch=24)[
                        :, 0:24:3, base:base + 256]).then_inc(bk.sem["LD"], 16)
                bk.cnt["LD"] += 16
                bk.inc(sync.dma_start(
                    out=ks[0:64, 2048:4096].rearrange("p (s t) -> p s t", s=8),
                    in_=qkvT[0:64, :].rearrange("p (ch t) -> p ch t", ch=24)[
                        :, 2:24:3, base:base + 256]), "LD", 16)
            ld_qk = bk.cnt["LD"]

            @block.vector
            def _(v):
                bk.w(v, "SV", sv_qkv)
                bk.gwait(v, ("Vb",))
                bk.inc(v.memset(Vb, 1.0), "VV")
            vv_vb0 = bk.cnt["VV"]

            for j2 in range(16):
                for p2h in range(2):
                    ub = j2 * 2 + p2h
                    vsrc, voff = v_src(hh, j2, (p2h * 128, p2h * 128 + 128))

                    @block.tensor
                    def _(t):
                        if ub == 0:
                            bk.w(t, "VV", vv_vb0)
                        bk.gwait(t, ("pbf", ub % 2))
                        bk.inc(t.transpose(pbf[ub % 2][:, 0:64], vsrc,
                                           c["ident"][voff:voff + 64, voff:voff + 64]),
                               "TV")
                    tv_vt = bk.cnt["TV"]

                    @block.vector
                    def _(v):
                        bk.w(v, "TV", tv_vt)
                        bk.inc(v.tensor_copy(Vb[:, ub * 65:ub * 65 + 64],
                                             pbf[ub % 2][:, 0:64]), "VV")
                    bk.gset(("pbf", ub % 2), "VV")
            vv_vb = bk.cnt["VV"]

            for p1h in range(2):
                for jh in range(2):
                    n_u2 = 16 * (p1h + 1)
                    tv_s_list = [0] * n_u2

                    def emit_S(ug):
                        p2h, j2 = divmod(ug, 16)
                        par5 = ug % 2
                        sb0 = pf[par5 * 4 + 0]
                        sb1 = pf[par5 * 4 + 1]

                        @block.tensor
                        def _(t):
                            if p1h == 0 and jh == 0 and ug == 0:
                                bk.w(t, "SV", sv_qkv)
                                bk.w(t, "LD", ld_qk)
                            if ug == 0:
                                bk.w(t, "VV", vv_vb)
                                bk.gwait(t, ("pot2",))
                            bk.gwait(t, ("ps5", par5))
                            ins = None
                            for jj in range(8):
                                j1 = 8 * jh + jj
                                pk = sb0 if jj < 4 else sb1
                                ins = t.matmul(
                                    pk[:, (jj % 4) * 128:(jj % 4 + 1) * 128],
                                    k_slice(hh, j2, (p2h * 128, p2h * 128 + 128)),
                                    q_slice(hh, j1, (p1h * 128, p1h * 128 + 128)),
                                    start=True, stop=True)
                            bk.inc(ins, "TV")
                        tv_s_list[ug] = bk.cnt["TV"]

                    def emit_PAV(ug):
                        p2h, j2 = divmod(ug, 16)
                        par5 = ug % 2
                        sb0 = pf[par5 * 4 + 0]
                        sb1 = pf[par5 * 4 + 1]
                        ptbuf = PT[:, par5 * 1024:par5 * 1024 + 1024]

                        @block.scalar
                        def _(s):
                            bk.w(s, "TV", tv_s_list[ug])
                            bk.gwait(s, ("pt5", par5))
                            s.activation(ptbuf[:, 0:512], sb0[:],
                                         AF.Exp, scale=QKSCALE)
                            bk.inc(s.activation(ptbuf[:, 512:1024], sb1[:],
                                                AF.Exp, scale=QKSCALE), "SV")
                        sv_pt = bk.cnt["SV"]
                        bk.gset(("ps5", par5), "SV")

                        vv_mask = None
                        if p2h == p1h:
                            @block.vector
                            def _(v):
                                bk.w(v, "SV", sv_pt)
                                ins = None
                                for jj in range(8):
                                    j1 = 8 * jh + jj
                                    msk = tincl if j1 >= j2 else texcl
                                    ins = v.tensor_tensor(ptbuf[:, jj * 128:(jj + 1) * 128],
                                                          ptbuf[:, jj * 128:(jj + 1) * 128],
                                                          msk, ALU.mult)
                                bk.inc(ins, "VV")
                            vv_mask = bk.cnt["VV"]

                        @block.tensor
                        def _(t):
                            if vv_mask is not None:
                                bk.w(t, "VV", vv_mask)
                            else:
                                bk.w(t, "SV", sv_pt)
                            t.matmul(pf[2][0:65, 0:512],
                                     Vb[:, (j2 * 2 + p2h) * 65:(j2 * 2 + p2h) * 65 + 65],
                                     ptbuf[:, 0:512],
                                     start=(ug == 0), stop=(ug == n_u2 - 1))
                            ins = t.matmul(pf[3][0:65, 0:512],
                                           Vb[:, (j2 * 2 + p2h) * 65:(j2 * 2 + p2h) * 65 + 65],
                                           ptbuf[:, 512:1024],
                                           start=(ug == 0), stop=(ug == n_u2 - 1))
                            bk.inc(ins, "TV")
                        bk.gset(("pt5", par5), "TV")

                    for ug in range(n_u2 + 1):
                        if ug < n_u2:
                            emit_S(ug)
                        if ug >= 1:
                            emit_PAV(ug - 1)
                    tv_av = bk.cnt["TV"]

                    otb8 = sb["zbf0"][0:65, 0:1024]

                    @block.scalar
                    def _(s):
                        bk.w(s, "TV", tv_av)
                        bk.gwait(s, ("otb",))
                        ins = None
                        for jj in range(8):
                            ins = s.activation(
                                otb8[:, jj * 128:(jj + 1) * 128],
                                pf[2 + jj // 4][0:65, (jj % 4) * 128:(jj % 4 + 1) * 128],
                                AF.Identity)
                        bk.inc(ins, "SV")
                    sv_ot = bk.cnt["SV"]

                    @block.tensor
                    def _(t):
                        bk.w(t, "SV", sv_ot)
                        bk.gwait(t, ("pot",))
                        ins = None
                        for jj in range(8):
                            ins = t.transpose(pbf[0][:, jj * 128:jj * 128 + 65],
                                              otb8[:, jj * 128:(jj + 1) * 128],
                                              c["ident"][0:65, 0:65])
                        bk.inc(ins, "TV")
                    tv_ot = bk.cnt["TV"]
                    bk.gset(("otb",), "TV")
                    bk.gset(("pot2",), "TV")

                    @block.vector
                    def _(v):
                        bk.w(v, "TV", tv_ot)
                        if jh == 0:
                            bk.gwait(v, "res5")
                        for jj in range(8):
                            v.tensor_copy(recs[:, jj:jj + 1],
                                          pbf[0][:, jj * 128 + 64:jj * 128 + 65])
                        for jj in range(8):
                            v.reciprocal(recs[:, 8 + jj:9 + jj], recs[:, jj:jj + 1])
                        ins = None
                        for jj in range(8):
                            j1 = 8 * jh + jj
                            ins = v.tensor_scalar(res[:, j1 * 64:(j1 + 1) * 64],
                                                  pbf[0][:, jj * 128:jj * 128 + 64],
                                                  recs[:, 8 + jj:9 + jj], None, ALU.mult)
                        bk.inc(ins, "VV")
                    bk.gset(("pot",), "VV")
                    vv_div = bk.cnt["VV"]

                i2 = hp * 4 + hh * 2 + p1h

                @block.vector
                def _(v):
                    xsl = xg[:, (hh * 2 + p1h) * D:(hh * 2 + p1h + 1) * D]
                    v.tensor_tensor(res[:, 0:512], res[:, 0:512], xsl[:, 0:512], ALU.add)
                    bk.inc(v.tensor_tensor(res[:, 512:1024], res[:, 512:1024],
                                           xsl[:, 512:1024], ALU.add), "VV")
                vv_res = bk.cnt["VV"]

                @block.sync
                def _(sync):
                    bk.w(sync, "VV", vv_res)
                    bk.inc(sync.dma_start(out=xout[i2 * P:(i2 + 1) * P, :], in_=res),
                           "ST", 16)
                bk.gset("res5", "ST")
            bk.gset(("Vb",), "TV")
            bk.gset(("qks",), "TV")
        bk.gset("xg", "ST")
        bk.gset(("qkvT",), "TV")

CONSTS = [("ident", [P, P], BF16), ("u128", [P, P], F32), ("u32x", [32, 32], F32),
          ("iota8", [P, 8], F32), ("onesc", [P, 1], F32), ("onesr_bf", [1, P], BF16),
          ("onesr_f", [1, 128], F32), ("ones32c", [32, 1], F32), ("epsc", [128, 1], F32)]

S1INS = [("WrR", [128, 8, 8], F32), ("W1R", [8, 8, 128, 8 * 512], F32),
         ("W2R", [8, 4, 128, 32 * 256], F32), ("b1R", [128, 8, 32], F32),
         ("b2w", [128, 8, 8], F32), ("mask8_in", [8, 1], F32)]
S2INS = [("WhR", [128, 8, 4096], F32), ("WoR", [128, 16, 1024], F32),
         ("WqkR", [128, 8, 128], F32), ("rows_s2", [1, 8192], F32),
         ("osgR", [128, 4], F32), ("osbR", [128, 4], F32), ("bqkc", [128, 1], F32),
         ("lngR", [128, 8], F32), ("lnbR", [128, 8], F32), ("rel", [32, 1], F32),
         ("masksT", [32, 2, 128, 256], BF16)]
S4INS = [("encR", [128, 8, 2048], F32), ("decR", [128, 16, 1024], F32),
         ("s2cR", [128, 16, 64], F32), ("c2sR", [64, 16, 128], F32),
         ("rW1R", [64, 64], F32), ("rW2R", [64, 8], F32),
         ("We1R", [64, 8, 8, 128], F32), ("We2R", [128, 8, 8, 64], F32),
         ("encbR", [128, 16], F32), ("c2sbR", [128, 16], F32), ("s2cbc", [64, 1], F32),
         ("rb1c", [64, 1], F32), ("rb2c", [8, 1], F32), ("be1R", [128, 8, 8], F32),
         ("be2R", [64, 8], F32), ("rows_s4", [1, 8192], F32), ("rowf_s4", [1, 2048], F32)]
S5INS = [("QR", [128, 192, 128], F32), ("qkvbR", [128, 24], F32),
         ("tincl", [128, 128], BF16), ("texcl", [128, 128], BF16)]


def emit_barrier(bk, block):
    f = bk.fence()

    @block.sync
    def _(sync):
        bk.wait_fence(sync, f, ["TV", "VV", "SV", "WD", "ST", "LD"])

    @block.gpsimd
    def _(g):
        bk.wait_fence(g, f, ["TV", "VV", "SV", "ST", "LD"])

    @block.tensor
    def _(t):
        bk.wait_fence(t, f, ["VV", "SV", "WD", "ST", "LD"])

    @block.vector
    def _(v):
        bk.wait_fence(v, f, ["TV", "SV", "WD", "ST", "LD"])

    @block.scalar
    def _(s):
        bk.wait_fence(s, f, ["TV", "VV", "WD", "ST", "LD"])


def build_megakernel():
    from contextlib import ExitStack
    nc = bass.Bass("TRN2", target_bir_lowering=False)
    x0 = nc.dram_tensor("x0", [TOK, D], F32, kind="ExternalInput").ap()
    xo = nc.dram_tensor("xo", [TOK, D], F32, kind="ExternalOutput").ap()
    xbuf = nc.dram_tensor("xbuf", [TOK, D], F32).ap()
    ins = {}
    for nm, shape, dt in S1INS + S2INS + S4INS + S5INS:
        ins[nm] = nc.dram_tensor(nm, shape, dt, kind="ExternalInput").ap()
    cin = {}
    for nm, shape, dt in CONSTS:
        cin[nm] = nc.dram_tensor("cin_" + nm, shape, dt, kind="ExternalInput").ap()
    rt = nc.dram_tensor("rt", [4, 256], F32).ap()
    cc_in1 = nc.dram_tensor("cc_in1", [1, 16], F32).ap()
    cc_out1 = nc.dram_tensor("cc_out1", [8, 16], F32).ap()
    cc_in2 = nc.dram_tensor("cc_in2", [1, 16], F32).ap()
    cc_out2 = nc.dram_tensor("cc_out2", [8, 16], F32).ap()
    vsp = nc.dram_tensor("vsp", [TOK, 2048], BF16).ap()
    gsp = nc.dram_tensor("gsp", [TOK, 2048], BF16).ap()
    W1C = nc.dram_tensor("W1C", [8, 8, 128, 8 * 512], BF16).ap()
    W2C = nc.dram_tensor("W2C", [8, 4, 128, 32 * 256], BF16).ap()

    es = ExitStack()
    bk = BK(nc, es)
    sb = {}
    alloc_ffn0_sbuf(nc, es, sb)
    alloc_attn0_sbuf(nc, es, sb)
    c = {}
    for nm, shape, dt in CONSTS:
        c[nm] = es.enter_context(nc.sbuf_tensor("c_" + nm, shape, dt))
    nc.const_aps.register(mybir.dt.float32, float(EPS), c["epsc"][:, 0:1])
    pbf = [es.enter_context(nc.psum_tensor(f"pbf{i}", [P, 1024], BF16)) for i in range(2)]
    pf = [es.enter_context(nc.psum_tensor(f"pf{i}", [P, 512], F32)) for i in range(6)]

    with nc.Block() as block:
        @block.sync
        def _(sync):
            for nm in c:
                bk.inc(sync.dma_start(out=c[nm][:], in_=cin[nm]), "LD", 16)

        base_io = dict(c={k: v[:] for k, v in c.items()}, sb=sb, pbf=pbf, pf=pf)
        with nc.named_scope("s1"):
            emit_ffn0(bk, block, dict(base_io, xin=x0, xout=xbuf, WrR=ins["WrR"],
                                      W1R=ins["W1R"], W2R=ins["W2R"], b1R=ins["b1R"],
                                      b2R=ins["b2w"], mask8_in=ins["mask8_in"], rt=rt,
                                      W1C=W1C, W2C=W2C, cached=False,
                                      cc_in=cc_in1, cc_out=cc_out1, cc_name="CC1"))
            emit_barrier(bk, block)
        with nc.named_scope("s2"):
            emit_attn0(bk, block, dict(base_io, xin=xbuf, xout=xbuf, WhR=ins["WhR"],
                                       WoR=ins["WoR"], WqkR=ins["WqkR"],
                                       rows_s2=ins["rows_s2"], osgR=ins["osgR"],
                                       osbR=ins["osbR"], bqkc=ins["bqkc"],
                                       lngR=ins["lngR"], lnbR=ins["lnbR"], rel=ins["rel"],
                                       masksT=ins["masksT"], vsp=vsp, gsp=gsp))
            emit_barrier(bk, block)
        with nc.named_scope("s3"):
            emit_ffn0(bk, block, dict(base_io, xin=xbuf, xout=xbuf, WrR=ins["WrR"],
                                      W1R=ins["W1R"], W2R=ins["W2R"], b1R=ins["b1R"],
                                      b2R=ins["b2w"], mask8_in=ins["mask8_in"], rt=rt,
                                      W1C=W1C, W2C=W2C, cached=True,
                                      cc_in=cc_in2, cc_out=cc_out2, cc_name="CC2"))
            emit_barrier(bk, block)
        with nc.named_scope("s4"):
            io4 = dict(base_io, xin=xbuf, xout=xbuf)
            io4.update({nm: ins[nm] for nm, _, _ in S4INS})
            emit_ffn1(bk, block, io4)
            emit_barrier(bk, block)
        with nc.named_scope("s5"):
            io5 = dict(base_io, xin=xbuf, xout=xbuf)
            io5.update({nm: ins[nm] for nm, _, _ in S5INS})
            emit_attn1(bk, block, io5)
            emit_barrier(bk, block)
        with nc.named_scope("s6"):
            io6 = dict(base_io, xin=xbuf, xout=xo)
            io6.update({nm: ins[nm] for nm, _, _ in S4INS})
            emit_ffn1(bk, block, io6)

        @block.sync
        def _(sync):
            bk.w(sync, "ST")

        @block.gpsimd
        def _(g):
            bk.w(g, "ST")
    es.close()
    return nc


_NC_CACHE = {}


def kernel(**inputs):
    g = prep_host(inputs)
    if "nc" not in _NC_CACHE:
        _NC_CACHE["nc"] = build_megakernel()
    nc = _NC_CACHE["nc"]

    base = {"WrR": g["WrR"], "W1R": g["W1R"].reshape(8, 8, 128, 8 * 512),
            "W2R": g["W2R"].reshape(8, 4, 128, 32 * 256), "b1R": g["b1R"],
            "b2w": g["b2R"], "WhR": g["WhR"], "WoR": g["WoR"], "WqkR": g["WqkR"],
            "rows_s2": g["rows_s2"], "osgR": g["osgR"], "osbR": g["osbR"],
            "bqkc": g["bqkc"], "lngR": g["lngR"], "lnbR": g["lnbR"],
            "rel": np.ascontiguousarray(np.asarray(inputs["L0_attn_rel_emb"],
                                                   dtype=np.float32)),
            "masksT": g["masksT"], "QR": g["QR"], "qkvbR": g["qkvbR"],
            "tincl": g["tincl"], "texcl": g["texcl"]}
    for nm, _, _ in S4INS:
        base[nm] = g[nm]
    for nm, _, _ in CONSTS:
        base["cin_" + nm] = g[{"ident": "ident_bf"}.get(nm, nm)]

    x_full = np.asarray(inputs["L0_attn_x"], dtype=np.float32)
    in_maps = []
    for cid in range(8):
        m = dict(base)
        m["x0"] = np.ascontiguousarray(x_full[cid])
        msk = np.zeros((8, 1), dtype=np.float32)
        msk[:cid] = 1.0
        m["mask8_in"] = msk
        in_maps.append(m)

    from concourse.bass_utils import run_bass_kernel_spmd
    res = run_bass_kernel_spmd(nc, in_maps, core_ids=list(range(8)))
    return np.stack([res.results[cid]["xo"] for cid in range(8)], axis=0)
